# version 28
# baseline (speedup 1.0000x reference)
import contextlib
from collections import defaultdict

import numpy as np
import ml_dtypes
import concourse.bass as bass
import concourse.mybir as mybir
from concourse.bass_utils import run_bass_kernel_spmd

F32 = mybir.dt.float32
BF16 = mybir.dt.bfloat16
I32 = mybir.dt.int32
AF = mybir.ActivationFunctionType
ALU = mybir.AluOpType
AX = mybir.AxisListType

NCORE = 8
S = 2048
D = 1024
NT = 16
QK_IN_LOOP = True
NC = 8
EPS = 1e-5
RS2 = 2.0 ** (-0.5)
EPSA = 1e-30


class Prog:

    def __init__(self, sems):
        self.ops = defaultdict(list)
        self.cnt = defaultdict(int)
        self.sems = sems

    def emit(self, eng, fn):
        self.ops[eng].append(fn)

    def op(self, eng, fn, sem=None):
        if sem is None:
            self.emit(eng, fn)
            return None
        self.cnt[sem] += 1
        s = self.sems[sem]

        def run(e, fn=fn, s=s):
            fn(e).then_inc(s, 1)
        self.emit(eng, run)
        return self.cnt[sem]

    def dma(self, eng, fn, sem):
        self.cnt[sem] += 16
        s = self.sems[sem]

        def run(e, fn=fn, s=s):
            fn(e).then_inc(s, 16)
        self.emit(eng, run)
        return self.cnt[sem]

    def wait(self, eng, sem, value):
        if value is None:
            return
        s = self.sems[sem]
        self.emit(eng, lambda e, s=s, v=value: e.wait_ge(s, v))

    def drain(self, eng):
        self.emit(eng, lambda e: e.drain())


def build_program():
    nc = bass.Bass("TRN2", target_bir_lowering=False)

    d_x = nc.dram_tensor("x", [S, D], F32, kind="ExternalInput").ap()
    d_wqk = nc.dram_tensor("wqk", [D, D], F32, kind="ExternalInput").ap()
    d_wv = nc.dram_tensor("wv", [D, 512], F32, kind="ExternalInput").ap()
    d_wout = nc.dram_tensor("wout", [512, D], F32, kind="ExternalInput").ap()
    d_spk = nc.dram_tensor("spk", [128, 120], F32, kind="ExternalInput").ap()
    d_bv_row = nc.dram_tensor("bv_row", [1, 512], F32, kind="ExternalInput").ap()
    d_bout_row = nc.dram_tensor("bout_row", [1, D], F32, kind="ExternalInput").ap()
    d_gamma_row = nc.dram_tensor("gamma_row", [1, D], F32, kind="ExternalInput").ap()
    d_beta_row = nc.dram_tensor("beta_row", [1, D], F32, kind="ExternalInput").ap()
    d_wd2 = nc.dram_tensor("wd2", [1024, 512], F32, kind="ExternalInput").ap()
    d_wu2 = nc.dram_tensor("wu2", [512, 1024], F32, kind="ExternalInput").ap()
    d_bd2 = nc.dram_tensor("bd2", [512, 1], F32, kind="ExternalInput").ap()
    d_bu2 = nc.dram_tensor("bu2", [8, 1024], F32, kind="ExternalInput").ap()
    d_identb = nc.dram_tensor("identb", [128, 128], BF16, kind="ExternalInput").ap()
    d_identf8 = nc.dram_tensor("identf8", [8, 8], F32, kind="ExternalInput").ap()
    d_crows = nc.dram_tensor("crows", [1, 768], F32, kind="ExternalInput").ap()
    d_onesb_r = nc.dram_tensor("onesb_r", [1, 128], BF16, kind="ExternalInput").ap()
    d_onesb_c = nc.dram_tensor("onesb_c", [128, 1], BF16, kind="ExternalInput").ap()
    d_iotai = nc.dram_tensor("iotai", [128, 1], I32, kind="ExternalInput").ap()
    d_y = nc.dram_tensor("y", [S, D], F32, kind="ExternalOutput").ap()

    st = contextlib.ExitStack()
    with st:
        E = st.enter_context
        x_sb = E(nc.sbuf_tensor("sb_x_sb", [128, NT * D], F32))
        big_a = E(nc.sbuf_tensor("sb_big_a", [128, NT * D], BF16))
        big_b = E(nc.sbuf_tensor("sb_big_b", [128, NT * D], BF16))
        vprime = E(nc.sbuf_tensor("sb_vprime", [128, 16 * 520], BF16))
        p_buf = E(nc.sbuf_tensor("sb_p_buf", [128, 3072], BF16))
        hT = p_buf[0:65, :]
        wqk_bf = E(nc.sbuf_tensor("sb_wqk_bf", [128, 8 * D], BF16))
        wv_bf = E(nc.sbuf_tensor("sb_wv_bf", [128, 8 * 512], BF16))
        wout_bf = E(nc.sbuf_tensor("sb_wout_bf", [128, 4 * D], BF16))
        stage = E(nc.sbuf_tensor("sb_stage", [128, 2048], F32))
        gamC = E(nc.sbuf_tensor("sb_gamC", [128, D], BF16))
        betC = E(nc.sbuf_tensor("sb_betC", [128, D], BF16))
        wd_bf = E(nc.sbuf_tensor("sb_wd_bf", [128, 512], BF16))
        wu_ext = E(nc.sbuf_tensor("sb_wu_ext", [72, 1024], BF16))
        bv_b = E(nc.sbuf_tensor("sb_bv_b", [1, 512], BF16))
        bout_b = E(nc.sbuf_tensor("sb_bout_b", [1, D], BF16))
        spk = E(nc.sbuf_tensor("sb_spk", [128, 120], F32))
        bqk2 = spk[:, 0:8]
        ffng2 = spk[:, 8:16]
        ffnb2 = spk[:, 16:24]
        bnm2 = spk[:, 24:32]
        bnv2 = spk[:, 32:40]
        bnb2 = spk[:, 40:48]
        bnsc2 = spk[:, 48:56]
        wr2 = spk[:, 56:120]
        bqk2s = E(nc.sbuf_tensor("sb_bqk2s", [128, 8], F32))
        tmp8 = E(nc.sbuf_tensor("sb_tmp8", [128, 8], F32))
        tmp8b = E(nc.sbuf_tensor("sb_tmp8b", [128, 8], F32))
        bnbuf = E(nc.sbuf_tensor("sb_bnbuf", [128, 12 * NT], F32))
        stat_mv = E(nc.sbuf_tensor("sb_stat_mv", [128, 2 * NT], F32))
        sc_a = E(nc.sbuf_tensor("sb_sc_a", [128, NT], F32))
        sc_b = E(nc.sbuf_tensor("sb_sc_b", [128, NT], F32))
        sc_c = E(nc.sbuf_tensor("sb_sc_c", [128, NT], F32))
        mu2_all = E(nc.sbuf_tensor("sb_mu2_all", [128, NT], F32))
        r2_all = E(nc.sbuf_tensor("sb_r2_all", [128, NT], F32))
        alpha_a = E(nc.sbuf_tensor("sb_alpha_a", [128, NT], F32))
        m2_a = E(nc.sbuf_tensor("sb_m2_a", [128, NT], F32))
        nsum = E(nc.sbuf_tensor("sb_nsum", [128, 8], F32))
        r8 = E(nc.sbuf_tensor("sb_r8", [8, 1], F32))
        rrow = E(nc.sbuf_tensor("sb_rrow", [1, 8], F32))
        rex = E(nc.sbuf_tensor("sb_rex", [1, 8], F32))
        rtmp = E(nc.sbuf_tensor("sb_rtmp", [1, 8], F32))
        rtmp2 = E(nc.sbuf_tensor("sb_rtmp2", [1, 8], F32))
        rmax = E(nc.sbuf_tensor("sb_rmax", [1, 1], F32))
        rnmax = E(nc.sbuf_tensor("sb_rnmax", [1, 1], F32))
        rsum = E(nc.sbuf_tensor("sb_rsum", [1, 1], F32))
        rrec = E(nc.sbuf_tensor("sb_rrec", [1, 1], F32))
        gate1 = E(nc.sbuf_tensor("sb_gate1", [1, 1], F32))
        idxf = E(nc.sbuf_tensor("sb_idxf", [1, 1], F32))
        idxi1 = E(nc.sbuf_tensor("sb_idxi1", [1, 1], I32))
        idxc = E(nc.sbuf_tensor("sb_idxc", [128, 1], F32))
        idxi = E(nc.sbuf_tensor("sb_idxi", [128, 1], I32))
        widx = E(nc.sbuf_tensor("sb_widx", [128, 1], I32))
        uidxc = E(nc.sbuf_tensor("sb_uidxc", [64, 1], F32))
        uidxi = E(nc.sbuf_tensor("sb_uidxi", [64, 1], I32))
        uidx = E(nc.sbuf_tensor("sb_uidx", [64, 1], I32))
        sgcol = E(nc.sbuf_tensor("sb_sgcol", [128, 1], F32))
        bd_f = E(nc.sbuf_tensor("sb_bd_f", [64, 1], F32))
        hbias = E(nc.sbuf_tensor("sb_hbias", [64, 1], F32))
        mqk8 = E(nc.sbuf_tensor("sb_mqk8", [1, 8], F32))
        c4 = E(nc.sbuf_tensor("sb_c4", [1, 4], F32))
        negc_bc = E(nc.sbuf_tensor("sb_negc_bc", [128, 4], F32))
        yve = E(nc.sbuf_tensor("sb_yve", [128, 2], F32))
        x2mv = E(nc.sbuf_tensor("sb_x2mv", [128, 2 * NT], F32))
        epscol = E(nc.sbuf_tensor("sb_epscol", [128, 1], F32))
        gsS = E(nc.sbuf_tensor("sb_gsS", [128, 8], F32))
        cC = E(nc.sbuf_tensor("sb_cC", [128, 8], F32))
        npart = E(nc.sbuf_tensor("sb_npart", [128, 32], F32))
        ylnv = E(nc.sbuf_tensor("sb_ylnv", [128, 2], F32))
        yrstd = E(nc.sbuf_tensor("sb_yrstd", [128, 2], F32))
        ynb = E(nc.sbuf_tensor("sb_ynb", [128, 2], F32))
        identb = E(nc.sbuf_tensor("sb_identb", [128, 128], BF16))
        identf8 = E(nc.sbuf_tensor("sb_identf8", [8, 8], F32))
        crows = E(nc.sbuf_tensor("sb_crows", [1, 768], F32))
        c1row = crows[0:1, 0:128]
        cn1row = crows[0:1, 128:256]
        c128row = crows[0:1, 256:384]
        c64row = crows[0:1, 384:448]
        c05row = crows[0:1, 448:576]
        crs2row = crows[0:1, 576:704]
        onesb_r = E(nc.sbuf_tensor("sb_onesb_r", [1, 128], BF16))
        onesb_c = E(nc.sbuf_tensor("sb_onesb_c", [128, 1], BF16))
        iotai = E(nc.sbuf_tensor("sb_iotai", [128, 1], I32))
        iota8f = crows[0:1, 704:712]
        br_row = crows[0:1, 720:728]
        ones11 = crows[0:1, 712:713]

        sems = {}
        for name in ["sdx", "sdxb", "sdx2", "sdx2b", "sdw", "sdl", "sdlb",
                     "sdg", "sout", "sp", "sv", "sa", "sq"]:
            sems[name] = E(nc.semaphore(name))

        P = Prog(sems)
        TEN, VEC, SCA, POOL, SYN = "tensor", "vector", "scalar", "gpsimd", "sync"

        def xt(t):
            return x_sb[:, D * t:D * (t + 1)]

        def nz_t(t):
            return big_a[:, D * t:D * (t + 1)]

        def zT_c(c):
            return big_b[:, S * c:S * (c + 1)]

        def qk_of(fc):
            return big_a[:, 2048 * fc:2048 * (fc + 1)]

        def ost_of(pr):
            return big_b[:, 2048 * pr:2048 * (pr + 1)]

        wd_f32 = stage[:, 0:512]
        wu_f32 = stage[0:64, 512:1536]
        bu_row = stage[64:65, 512:1536]

        def u_of(k):
            return stage[:, 1024 * k:1024 * (k + 1)]

        recA = stage[0:1, 0:512]
        recB = stage[0:1, 512:1024]
        bcAB = stage[0:64, 1024:1536]

        mv_mean = stat_mv.rearrange("p (t k) -> p t k", k=2)[:, :, 0:1]
        mv_var = stat_mv.rearrange("p (t k) -> p t k", k=2)[:, :, 1:2]
        mv_mean2 = mv_mean.rearrange("p t k -> p (t k)")
        mv_var2 = mv_var.rearrange("p t k -> p (t k)")

        xr = d_x.rearrange("(t p) d -> t p d", p=128)
        for sbuf, dram in [
            (spk[:], d_spk), (identb[:], d_identb), (identf8[:], d_identf8),
            (crows[:], d_crows), (onesb_r[:], d_onesb_r), (onesb_c[:], d_onesb_c),
            (iotai[:], d_iotai),
        ]:
            P.dma(POOL, lambda e, sb=sbuf, dr=dram: e.dma_start(sb, dr), "sdw")
        dw_small = P.cnt["sdw"]
        x_tile_dma = [None] * NT
        for grp, (eng, sems2) in enumerate((
                (SYN, ("sdx", "sdxb")), (POOL, ("sdx2", "sdx2b")))):
            for half in range(2):
                sname = sems2[half]
                tiles = [t for t in range(NT)
                         if t % 2 == grp and (t >= 8) == bool(half)]
                full = None
                for t in tiles:
                    full = P.dma(eng, lambda e, t=t: e.dma_start(
                        xt(t), xr[t]), sname)
                for t in tiles:
                    x_tile_dma[t] = (sname, full)
        P.wait(TEN, "sdw", dw_small)
        P.wait(VEC, "sdw", dw_small)
        P.wait(SCA, "sdw", dw_small)



        vp_ones_sq = P.op(POOL, lambda e: e.memset(
            vprime.rearrange("p (t h c) -> p t h c", t=16, h=8)[:, :, :, 64:65],
            1.0), "sq")
        eps_sq = P.op(POOL, lambda e: e.memset(epscol[:], EPS), "sq")

        P.wait(POOL, "sdw", dw_small)
        P.op(POOL, lambda e: e.tensor_scalar(bqk2s[:], bqk2, 0.125, None,
                                             ALU.mult))
        P.drain(POOL)
        sv_ve = P.op(POOL, lambda e: e.tensor_scalar(tmp8[:], bnv2, EPS, None,
                                                     ALU.add), "sq")
        P.wait(SCA, "sq", sv_ve)
        P.op(SCA, lambda e: e.activation(tmp8b[:], tmp8[:], AF.Ln))
        P.drain(SCA)
        sa_bnr = P.op(SCA, lambda e: e.activation(tmp8[:], tmp8b[:], AF.Exp,
                                                  scale=-0.5), "sa")
        P.wait(POOL, "sa", sa_bnr)
        sv_const = P.op(POOL, lambda e: e.tensor_tensor(bnsc2, bnsc2,
                                                        tmp8[:], ALU.mult),
                        "sq")
        P.drain(POOL)
        P.op(POOL, lambda e: e.tensor_tensor(gsS[:], ffng2, bnsc2, ALU.mult))
        P.op(POOL, lambda e: e.tensor_tensor(cC[:], ffnb2, bnm2, ALU.subtract))
        P.drain(POOL)
        P.op(POOL, lambda e: e.tensor_scalar(gsS[:], gsS[:], 1.0 / S, None,
                                             ALU.mult))
        P.op(POOL, lambda e: e.tensor_tensor(cC[:], cC[:], bnsc2, ALU.mult))
        P.drain(POOL)
        gs_sq = P.op(POOL, lambda e: e.tensor_tensor(cC[:], cC[:], bnb2,
                                                     ALU.add), "sq")

        bu_stage = stage[0:8, 0:1024]
        dw_bu = P.dma(POOL, lambda e: e.dma_start(bu_stage, d_bu2), "sdw")
        P.wait(POOL, "sdw", dw_bu)
        buc_sq = P.op(POOL, lambda e: e.tensor_copy(wu_ext[64:72, :], bu_stage),
                      "sq")


        def ln_stats_apply(kind, x_ready, want_nsum=False, psum_guard=None):
            for t in range(NT):
                if x_ready is not None and x_ready[t] is not None:
                    P.wait(VEC, x_ready[t][0], x_ready[t][1])
                P.op(VEC, lambda e, t=t: e.bn_stats(
                    bnbuf[:, 12 * t:12 * t + 6], xt(t)[:, 0:512]))
                P.op(VEC, lambda e, t=t: e.bn_stats(
                    bnbuf[:, 12 * t + 6:12 * t + 12], xt(t)[:, 512:1024]))
            P.drain(VEC)
            for t in range(NT):
                P.op(VEC, lambda e, t=t: e.bn_aggr(
                    stat_mv[:, 2 * t:2 * t + 2], bnbuf[:, 12 * t:12 * t + 12]))
            P.drain(VEC)
            sv1 = P.op(VEC, lambda e: e.tensor_scalar(
                sc_a[:], mv_var2, EPS, None, ALU.add), "sv")
            P.wait(SCA, "sv", sv1)
            P.op(SCA, lambda e: e.activation(sc_c[:], sc_a[:], AF.Ln))
            P.drain(SCA)
            sa1 = P.op(SCA, lambda e: e.activation(sc_b[:], sc_c[:], AF.Exp,
                                                   scale=-0.5), "sa")
            P.wait(VEC, "sa", sa1)
            if kind == "ffn":
                P.op(VEC, lambda e: e.tensor_tensor(sc_c[:], mv_var2, sc_b[:],
                                                    ALU.mult))
                P.drain(VEC)
                P.op(VEC, lambda e: e.tensor_tensor(sc_c[:], sc_c[:], sc_b[:],
                                                    ALU.mult))
                P.drain(VEC)
                sv2 = P.op(VEC, lambda e: e.tensor_scalar(
                    sc_c[:], sc_c[:], EPS, None, ALU.add), "sv")
                P.wait(SCA, "sv", sv2)
                P.op(SCA, lambda e: e.activation(sc_a[:], sc_c[:], AF.Ln))
                P.drain(SCA)
                sa2 = P.op(SCA, lambda e: e.activation(
                    sc_c[:], sc_a[:], AF.Exp, scale=-0.5), "sa")
                P.wait(VEC, "sa", sa2)
                P.op(VEC, lambda e: e.tensor_tensor(sc_b[:], sc_b[:], sc_c[:],
                                                    ALU.mult))
                P.drain(VEC)
            else:
                P.op(VEC, lambda e: e.tensor_copy(mu2_all[:], mv_mean2))
                P.op(VEC, lambda e: e.tensor_copy(r2_all[:], sc_b[:]))
                P.drain(VEC)
            nb_sv = P.op(VEC, lambda e: e.scalar_tensor_tensor(
                sc_c[:], mv_mean2, -1.0, sc_b[:], ALU.mult, ALU.mult), "sv")
            P.drain(VEC)
            zview = big_b.rearrange("p (c tt) -> p c tt", c=8)
            last = None
            nsum_sq = None
            ev_t = [None] * NT
            with contextlib.ExitStack() as ps:
                p_t2 = [ps.enter_context(nc.psum_tensor(
                    f"p_t2{k}_{P.cnt['sp']}", [128, 1024], BF16))
                    for k in range(2)]
                ev_hist = [None, None]
                if psum_guard is not None:
                    P.wait(TEN, psum_guard[0], psum_guard[1])
                P.wait(SCA, "sv", nb_sv)
                for t in range(NT):
                    ap_ms = P.op(SCA, lambda e, t=t: e.activation(
                        nz_t(t), xt(t), AF.Identity,
                        bias=sc_c[:, t:t + 1], scale=sc_b[:, t:t + 1]), "sa")
                    P.wait(TEN, "sa", ap_ms)
                    P.wait(TEN, "sv", ev_hist[t % 2])
                    ms = None
                    for c in range(NC):
                        ms = P.op(TEN, lambda e, t=t, c=c, pt=p_t2: e.transpose(
                            pt[t % 2][:, 128 * c:128 * (c + 1)],
                            big_a[:, D * t + 128 * c:D * t + 128 * (c + 1)],
                            identb[:]), "sp" if c == NC - 1 else None)
                    P.wait(VEC, "sp", ms)
                    last = P.op(VEC, lambda e, t=t, pt=p_t2: e.tensor_copy(
                        zview[:, :, 128 * t:128 * (t + 1)],
                        pt[t % 2][:].rearrange("p (c tt) -> p c tt", c=8)),
                        "sv")
                    ev_hist[t % 2] = last
                    ev_t[t] = last
            if want_nsum:
                P.wait(SCA, "sv", last)
                for c in range(NC):
                    nsum_sq = P.op(SCA, lambda e, c=c: e.activation(
                        big_a[:, 2048 * c:2048 * (c + 1)], zT_c(c), AF.Identity,
                        scale=1.0, accum_out=nsum[:, c:c + 1]), "sa")
            return last, nsum_sq

        ffn_state = {}

        def emit_ffn_back(nsum_sq, stage_guard=None, rt_guards=(),
                          with_n2=False):
            vp_ones_sv = P.op(POOL, lambda e: e.memset(p_buf[64:72, :], 1.0),
                              "sq")
            sel_sv = None

            P.wait(VEC, "sa", nsum_sq)
            P.wait(VEC, "sq", gs_sq)
            P.op(VEC, lambda e: e.tensor_tensor(tmp8[:], nsum[:], gsS[:],
                                                ALU.mult))
            P.drain(VEC)
            zbn_sv = P.op(VEC, lambda e: e.tensor_tensor(
                tmp8[:], tmp8[:], cC[:], ALU.add), "sv")

            with contextlib.ExitStack() as ps:
                p_r = ps.enter_context(
                    nc.psum_tensor(f"p_r_{P.cnt['sp']}", [8, 1], F32))
                p_rt = ps.enter_context(
                    nc.psum_tensor(f"p_rt_{P.cnt['sp']}", [1, 8], F32))
                p_b1 = ps.enter_context(
                    nc.psum_tensor(f"p_b1_{P.cnt['sp']}", [128, 1], F32))
                p_b2 = ps.enter_context(
                    nc.psum_tensor(f"p_b2_{P.cnt['sp']}", [64, 1], F32))
                p_b3 = ps.enter_context(
                    nc.psum_tensor(f"p_b3_{P.cnt['sp']}", [128, 1], F32))
                p_hb = ps.enter_context(
                    nc.psum_tensor(f"p_hb_{P.cnt['sp']}", [64, 1], F32))

                for g_sem, g_val in rt_guards:
                    P.wait(TEN, g_sem, g_val)
                P.wait(TEN, "sv", zbn_sv)
                for c in range(NC):
                    P.op(TEN, lambda e, c=c: e.matmul(
                        p_r[:], wr2[:, 8 * c:8 * (c + 1)], tmp8[:, c:c + 1],
                        start=(c == 0), stop=False))
                r_ms = P.op(TEN, lambda e: e.matmul(
                    p_r[:], br_row, ones11, start=False, stop=True), "sp")
                P.wait(VEC, "sp", r_ms)
                r8_sv = P.op(VEC, lambda e: e.tensor_copy(r8[:], p_r[:]), "sv")
                P.wait(TEN, "sv", r8_sv)
                rt_ms = P.op(TEN, lambda e: e.transpose(
                    p_rt[:], r8[:], identf8[:]), "sp")
                P.wait(VEC, "sp", rt_ms)
                P.op(VEC, lambda e: e.tensor_copy(rrow[:], p_rt[:]))
                P.drain(VEC)
                P.op(VEC, lambda e: e.reduce_max(rmax[:], rrow[:], axis=AX.X))
                P.drain(VEC)
                nm_sv = P.op(VEC, lambda e: e.tensor_scalar(
                    rnmax[:], rmax[:], -1.0, None, ALU.mult), "sv")
                P.wait(SCA, "sv", nm_sv)
                ex_sa = P.op(SCA, lambda e: e.activation(
                    rex[:], rrow[:], AF.Exp, bias=rnmax[:], scale=1.0), "sa")
                P.wait(VEC, "sa", ex_sa)
                P.op(VEC, lambda e: e.reduce_sum(rsum[:], rex[:], axis=AX.X))
                P.op(VEC, lambda e: e.tensor_scalar(
                    rtmp[:], rex[:], 1.0, None, ALU.subtract))
                P.drain(VEC)
                P.op(VEC, lambda e: e.reciprocal(rrec[:], rsum[:]))
                P.op(VEC, lambda e: e.tensor_tensor(rtmp2[:], rtmp[:], rtmp[:],
                                                    ALU.mult))
                P.drain(VEC)
                P.op(VEC, lambda e: e.tensor_scalar(
                    rtmp2[:], rtmp2[:], EPSA, None, ALU.add))
                P.drain(VEC)
                P.op(VEC, lambda e: e.reciprocal(rtmp[:], rtmp2[:]))
                P.drain(VEC)
                mask_sv = P.op(VEC, lambda e: e.tensor_scalar(
                    rtmp[:], rtmp[:], EPSA, None, ALU.mult), "sv")
                P.drain(VEC)
                P.op(VEC, lambda e: e.tensor_tensor(rtmp2[:], rtmp[:], iota8f,
                                                    ALU.mult))
                P.drain(VEC)
                idx_sv = P.op(VEC, lambda e: e.reduce_sum(
                    idxf[:], rtmp2[:], axis=AX.X), "sv")
                P.wait(TEN, "sv", mask_sv)
                mt_ms = P.op(TEN, lambda e: e.transpose(
                    p_r[:], rtmp[:], identf8[0:1, 0:1]), "sp")
                P.wait(VEC, "sp", mt_ms)
                P.op(VEC, lambda e: e.tensor_copy(tmp8b[0:8, 0:1], p_r[:]))
                P.drain(VEC)
                P.wait(VEC, "sq", vp_ones_sv)
                sel_sv = P.op(VEC, lambda e: e.tensor_scalar(
                    p_buf[64:72, :], p_buf[64:72, :], tmp8b[0:8, 0:1], None,
                    ALU.mult), "sv")
                P.wait(TEN, "sv", idx_sv)
                b1_ms = P.op(TEN, lambda e: e.matmul(
                    p_b1[:], c128row, idxf[:], start=True, stop=True), "sp")
                b2_ms = P.op(TEN, lambda e: e.matmul(
                    p_b2[:], c64row, idxf[:], start=True, stop=True), "sp")
                b3_ms = P.op(TEN, lambda e: e.matmul(
                    p_b3[:], c05row, rrec[:], start=True, stop=True), "sp")
                P.wait(VEC, "sp", b3_ms)
                P.op(VEC, lambda e: e.tensor_copy(idxc[:], p_b1[:]))
                P.op(VEC, lambda e: e.tensor_copy(uidxc[:], p_b2[:]))
                P.op(VEC, lambda e: e.tensor_copy(sgcol[:], p_b3[:]))
                P.op(VEC, lambda e: e.tensor_copy(idxi1[:], idxf[:]))
                P.drain(VEC)
                P.op(VEC, lambda e: e.tensor_copy(idxi[:], idxc[:]))
                P.op(VEC, lambda e: e.tensor_copy(uidxi[:], uidxc[:]))
                P.drain(VEC)
                P.op(VEC, lambda e: e.tensor_tensor(widx[:], idxi[:], iotai[:],
                                                    ALU.add))
                wix_sv = P.op(VEC, lambda e: e.tensor_tensor(
                    uidx[:], uidxi[:], iotai[0:64, :], ALU.add), "sv")

                P.wait(POOL, "sv", wix_sv)
                if stage_guard is not None:
                    P.wait(POOL, stage_guard[0], stage_guard[1])
                P.dma(POOL, lambda e: e.indirect_dma_start(
                    wd_f32, None, d_wd2,
                    bass.IndirectOffsetOnAxis(ap=widx[:], axis=0)), "sdg")
                P.dma(POOL, lambda e: e.indirect_dma_start(
                    wu_f32, None, d_wu2,
                    bass.IndirectOffsetOnAxis(ap=uidx[:], axis=0)), "sdg")
                g_ms = P.dma(POOL, lambda e: e.indirect_dma_start(
                    bd_f[:], None, d_bd2,
                    bass.IndirectOffsetOnAxis(ap=uidx[:], axis=0)), "sdg")

                P.wait(VEC, "sdg", g_ms)
                for c in range(NC):
                    P.op(VEC, lambda e, c=c: e.tensor_scalar(
                        wd_bf[:, 64 * c:64 * (c + 1)],
                        wd_f32[:, 64 * c:64 * (c + 1)],
                        ffng2[:, c:c + 1], None, ALU.mult))
                wprep_sv = P.op(VEC, lambda e: e.tensor_copy(
                    wu_ext[0:64, :], wu_f32), "sv")
                P.wait(TEN, "sdg", g_ms)
                for c in range(NC):
                    P.op(TEN, lambda e, c=c: e.matmul(
                        p_hb[:], wd_f32[:, 64 * c:64 * (c + 1)],
                        ffnb2[:, c:c + 1], start=(c == 0), stop=(c == NC - 1)),
                        "sp" if c == NC - 1 else None)
                hb_ms = P.cnt["sp"]
                P.wait(VEC, "sp", hb_ms)
                hb_sv = P.op(VEC, lambda e: e.tensor_tensor(
                    hbias[:], p_hb[:], bd_f[:], ALU.add), "sv")
                ffn_state["stage_last_sv"] = wprep_sv
                ffn_state["stage_last_sp"] = hb_ms

            with contextlib.ExitStack() as psh:
                p_h = [psh.enter_context(nc.psum_tensor(
                    f"p_h{i}_{P.cnt['sp']}", [64, 512], F32)) for i in range(2)]
                P.wait(TEN, "sv", wprep_sv)
                P.wait(TEN, "sv", hb_sv)
                relu_ms = [None, None]
                for tc in range(4):
                    if relu_ms[tc % 2] is not None:
                        P.wait(TEN, "sa", relu_ms[tc % 2])
                    ms = None
                    for c in range(NC):
                        ms = P.op(TEN, lambda e, c=c, tc=tc: e.matmul(
                            p_h[tc % 2][:],
                            wd_bf[:, 64 * c:64 * (c + 1)],
                            zT_c(c)[:, 512 * tc:512 * (tc + 1)],
                            start=(c == 0), stop=(c == NC - 1)),
                            "sp" if c == NC - 1 else None)
                    P.wait(SCA, "sp", ms)
                    P.wait(SCA, "sv", hb_sv)
                    relu_ms[tc % 2] = P.op(SCA, lambda e, tc=tc: e.activation(
                        p_buf[0:64, 512 * tc:512 * (tc + 1)], p_h[tc % 2][:],
                        AF.Relu, bias=hbias[:], scale=1.0), "sa")
            with contextlib.ExitStack() as ps:
                p_y = [ps.enter_context(nc.psum_tensor(
                    f"p_yf{i}_{P.cnt['sp']}", [128, 1024], F32)) for i in range(2)]
                if with_n2:
                    p_t2n = [ps.enter_context(nc.psum_tensor(
                        f"p_t2n{k}_{P.cnt['sp']}", [128, 1024], BF16))
                        for k in range(2)]
                    p_qk2 = [ps.enter_context(nc.psum_tensor(
                        f"p_qk2{k}_{P.cnt['sp']}", [128, 512], F32))
                        for k in range(2)]
                    zview_n = big_b.rearrange("p (c tt) -> p c tt", c=8)
                    vpb = vprime
                    P.wait(SCA, "sq", eps_sq)
                    P.wait(SCA, "sq", vp_ones_sq)
                    P.wait(VEC, "sq", sv_const)
                    wplan = []
                    for dram_w, n_ch, ccols, dstw in (
                            (d_wqk, 8, 1024, wqk_bf), (d_wv, 8, 512, wv_bf),
                            (d_wout, 4, 1024, wout_bf)):
                        dr = dram_w.rearrange("(c p) f -> c p f", p=128)
                        for cch in range(n_ch):
                            wplan.append((dr, cch, ccols, dstw))
                    wdma_ms = [None] * len(wplan)
                    wcast_ms = [None] * len(wplan)
                    n_dma = [0]
                    n_cast = [0]

                    def emit_wdma():
                        c = n_dma[0]
                        if c >= len(wplan):
                            return
                        dr, cch, ccols, dstw = wplan[c]
                        sl_ = stage[:, 1024 * (c % 2):1024 * (c % 2) + ccols]
                        if c < 2:
                            P.wait(SYN, "sv", wprep_sv)
                            P.wait(SYN, "sv", hb_sv)
                            P.wait(SYN, "sp", ffn_state["stage_last_sp"])
                        else:
                            P.wait(SYN, "sq", wcast_ms[c - 2])
                        wdma_ms[c] = P.dma(SYN, lambda e, sl_=sl_, dr=dr,
                                           cch=cch: e.dma_start(sl_, dr[cch]),
                                           "sdl" if c % 2 == 0 else "sdlb")
                        n_dma[0] += 1

                    def emit_wcast():
                        c = n_cast[0]
                        if c >= len(wplan) or wdma_ms[c] is None:
                            return
                        dr, cch, ccols, dstw = wplan[c]
                        sl_ = stage[:, 1024 * (c % 2):1024 * (c % 2) + ccols]
                        P.wait(POOL, "sdl" if c % 2 == 0 else "sdlb",
                               wdma_ms[c])
                        wcast_ms[c] = P.op(POOL, lambda e, sl_=sl_, dstw=dstw,
                                           cch=cch, ccols=ccols: e.tensor_copy(
                            dstw[:, ccols * cch:ccols * (cch + 1)], sl_), "sq")
                        n_cast[0] += 1

                    emit_wdma()
                    emit_wdma()
                    qk_batches = [(tc, fc) for tc in range(4)
                                  for fc in range(8)]
                    qb_next = [0]
                    qk_ev2 = [None, None]
                    qk_started = [False]

                    def emit_qk_batch():
                        b = qb_next[0]
                        if b >= len(qk_batches):
                            return False
                        tc, fc = qk_batches[b]
                        if n2ev_t[4 * tc + 3] is None:
                            return False
                        if n_cast[0] < 8:
                            return False
                        if not qk_started[0]:
                            P.wait(TEN, "sq", wcast_ms[7])
                            qk_started[0] = True
                        sl_ = b % 2
                        P.wait(TEN, "sv", qk_ev2[sl_])
                        if fc == 0:
                            P.wait(TEN, "sv", n2ev_t[4 * tc + 3])
                        ms = None
                        for c in range(NC):
                            ms = P.op(TEN, lambda e, c=c, fc=fc, tc=tc,
                                      sl_=sl_: e.matmul(
                                p_qk2[sl_][:],
                                wqk_bf[:, D * c + 128 * fc:
                                       D * c + 128 * (fc + 1)],
                                zT_c(c)[:, 512 * tc:512 * (tc + 1)],
                                start=(c == 0), stop=(c == NC - 1),
                                skip_group_check=True),
                                "sp" if c == NC - 1 else None)
                        P.wait(VEC, "sp", ms)
                        if fc % 2 == 0:
                            qk_ev2[sl_] = P.op(VEC, lambda e, fc=fc, tc=tc,
                                               sl_=sl_: e.tensor_scalar(
                                qk_of(fc)[:, 512 * tc:512 * (tc + 1)],
                                p_qk2[sl_][:], 0.125, bqk2s[:, fc:fc + 1],
                                ALU.mult, ALU.add), "sv")
                        else:
                            qk_ev2[sl_] = P.op(VEC, lambda e, fc=fc, tc=tc,
                                               sl_=sl_: e.tensor_scalar(
                                qk_of(fc)[:, 512 * tc:512 * (tc + 1)],
                                p_qk2[sl_][:], 1.0, bqk2[:, fc:fc + 1],
                                ALU.mult, ALU.add), "sv")
                        qb_next[0] += 1
                        return True
                P.wait(TEN, "sa", relu_ms[0])
                P.wait(TEN, "sa", relu_ms[1])
                P.wait(TEN, "sq", buc_sq)
                P.wait(TEN, "sv", sel_sv)
                up_ev = [None, None]
                x_done = []
                n2ap = [None] * NT
                n2r1 = [None] * NT
                n2ag = [None] * NT
                n2nb = [None] * NT
                n2tr = [None] * NT
                n2ev_t = [None] * NT
                n2_last = None
                n_it = NT + 6 if with_n2 else NT
                for i in range(n_it):
                    if i < NT:
                        t = i
                        if up_ev[t % 2] is not None:
                            P.wait(TEN, "sv", up_ev[t % 2])
                        ms = None
                        for h in range(2):
                            ms = P.op(TEN, lambda e, t=t, h=h: e.matmul(
                                p_y[t % 2][:, 512 * h:512 * (h + 1)],
                                p_buf[0:72, 128 * t:128 * (t + 1)],
                                wu_ext[0:72, 512 * h:512 * (h + 1)],
                                start=True, stop=True, skip_group_check=True),
                                "sp" if h == 1 else None)
                    if with_n2 and 0 <= i - 5 < NT:
                        td = i - 5
                        P.wait(TEN, "sa", n2ap[td])
                        P.wait(TEN, "sv", n2ev_t[td - 2] if td >= 2 else None)
                        for c in range(NC):
                            n2tr[td] = P.op(TEN, lambda e, td=td, c=c:
                                            e.transpose(
                                p_t2n[td % 2][:, 128 * c:128 * (c + 1)],
                                vpb[:, 1024 * (td % 2) + 128 * c:
                                    1024 * (td % 2) + 128 * (c + 1)],
                                identb[:]), "sp" if c == NC - 1 else None)
                    if i < NT:
                        P.wait(VEC, "sp", ms)
                        up_ev[t % 2] = P.op(
                            VEC, lambda e, t=t: e.scalar_tensor_tensor(
                                xt(t), p_y[t % 2][:], sgcol[:], xt(t),
                                ALU.mult, ALU.add), "sv")
                        x_done.append(("sv", up_ev[t % 2]))
                    if with_n2:
                        P.drain(VEC)
                        if 0 <= i - 1 < NT:
                            td = i - 1
                            if td >= 2:
                                P.wait(VEC, "sv", n2ag[td - 2])
                            P.op(VEC, lambda e, td=td: e.bn_stats(
                                bnbuf[:, 12 * (td % 2):12 * (td % 2) + 6],
                                xt(td)[:, 0:512]))
                            P.op(VEC, lambda e, td=td: e.bn_stats(
                                bnbuf[:, 12 * (td % 2) + 6:12 * (td % 2) + 12],
                                xt(td)[:, 512:1024]))
                        if 0 <= i - 2 < NT:
                            td = i - 2
                            n2ag[td] = P.op(VEC, lambda e, td=td: e.bn_aggr(
                                stat_mv[:, 2 * td:2 * td + 2],
                                bnbuf[:, 12 * (td % 2):12 * (td % 2) + 12]),
                                "sv")
                        if 0 <= i - 4 < NT:
                            td = i - 4
                            P.wait(VEC, "sa", n2r1[td])
                            n2nb[td] = P.op(VEC, lambda e, td=td:
                                            e.scalar_tensor_tensor(
                                sc_c[:, td:td + 1], stat_mv[:, 2 * td:
                                                            2 * td + 1],
                                -1.0, r2_all[:, td:td + 1],
                                ALU.mult, ALU.mult), "sv")
                        if 0 <= i - 5 < NT:
                            td = i - 5
                            P.wait(VEC, "sp", n2tr[td])
                            n2_last = P.op(VEC, lambda e, td=td: e.tensor_copy(
                                zview_n[:, :, 128 * td:128 * (td + 1)],
                                p_t2n[td % 2][:].rearrange(
                                    "p (c tt) -> p c tt", c=8)), "sv")
                            n2ev_t[td] = n2_last
                        if 0 <= i - 3 < NT:
                            td = i - 3
                            P.wait(SCA, "sv", n2ag[td])
                            P.op(SCA, lambda e, td=td: e.activation(
                                sc_a[:, td:td + 1],
                                stat_mv[:, 2 * td + 1:2 * td + 2], AF.Ln,
                                bias=epscol[:], scale=1.0))
                        P.drain(SCA)
                        if 0 <= i - 3 < NT:
                            td = i - 3
                            n2r1[td] = P.op(SCA, lambda e, td=td: e.activation(
                                r2_all[:, td:td + 1], sc_a[:, td:td + 1],
                                AF.Exp, scale=-0.5), "sa")
                        if 0 <= i - 4 < NT:
                            td = i - 4
                            P.wait(SCA, "sv", n2nb[td])
                            if td >= 2:
                                P.wait(SCA, "sp", n2tr[td - 2])
                            n2ap[td] = P.op(SCA, lambda e, td=td: e.activation(
                                vpb[:, 1024 * (td % 2):1024 * (td % 2 + 1)],
                                xt(td), AF.Identity,
                                bias=sc_c[:, td:td + 1],
                                scale=r2_all[:, td:td + 1]), "sa")
                        emit_wcast()
                        emit_wdma()
                        emit_wcast()
                        emit_wdma()
                        if QK_IN_LOOP:
                            for _ in range(3):
                                if not emit_qk_batch():
                                    break
                if with_n2:
                    while n_cast[0] < len(wplan):
                        emit_wdma()
                        emit_wcast()
                    while emit_qk_batch():
                        pass
                    ffn_state["sq_wqk"] = wcast_ms[7]
                    ffn_state["sq_wv"] = wcast_ms[15]
                    ffn_state["sq_wout"] = wcast_ms[19]
                    ffn_state["qkproj_sv"] = max(
                        v for v in qk_ev2 if v is not None)
                    P.wait(POOL, "sv", n2_last)
                    ffn_state["vp2_sq"] = P.op(POOL, lambda e: e.memset(
                        vprime.rearrange("p (t h c) -> p t h c", t=16, h=8)
                        [:, :, :, 64:65], 1.0), "sq")
            return x_done, n2_last

        def emit_ffn(x_ready, stage_guard=None, psum_guard=None,
                     with_n2=False):
            zt_sv, nsum_sq = ln_stats_apply("ffn", x_ready, want_nsum=True,
                                            psum_guard=psum_guard)
            return emit_ffn_back(nsum_sq, stage_guard=stage_guard,
                                 with_n2=with_n2)

        x_done, n2_ev = emit_ffn(x_tile_dma, stage_guard=("sq", buc_sq),
                                 with_n2=True)

        sq_wqk = ffn_state["sq_wqk"]
        sq_wv = ffn_state["sq_wv"]
        sq_wout = ffn_state["sq_wout"]
        qkproj_sv = ffn_state["qkproj_sv"]
        vp2_sq = ffn_state["vp2_sq"]
        P.wait(POOL, "sq", sq_wout)
        ffn1_done_sv = x_done[NT - 1][1]
        P.wait(TEN, "sv", qkproj_sv)
        g_row = stage[0:1, 0:1024]
        b_row = stage[0:1, 1024:2048]
        P.dma(POOL, lambda e: e.dma_start(g_row, d_gamma_row), "sdw")
        dw_gb = P.dma(POOL, lambda e: e.dma_start(b_row, d_beta_row), "sdw")
        with nc.psum_tensor("p_bcg", [128, 512], F32) as p_bcg:
            P.wait(TEN, "sdw", dw_gb)
            for src_row, dst in ((g_row, gamC), (b_row, betC)):
                for half in range(2):
                    ms = P.op(TEN, lambda e, h=half, sr=src_row, pb=p_bcg:
                              e.matmul(
                        pb[:], crs2row, sr[:, 512 * h:512 * (h + 1)],
                        start=True, stop=True), "sp")
                    P.wait(VEC, "sp", ms)
                    ms_v = P.op(VEC, lambda e, h=half, d=dst, pb=p_bcg:
                                e.tensor_copy(
                        d[:, 512 * h:512 * (h + 1)], pb[:]), "sv")
                    P.wait(TEN, "sv", ms_v)
        sv_gb = P.cnt["sv"]
        P.wait(POOL, "sv", sv_gb)
        P.dma(POOL, lambda e: e.dma_start(g_row[:, 0:512], d_bv_row), "sdw")
        dw_bvo = P.dma(POOL, lambda e: e.dma_start(b_row, d_bout_row), "sdw")
        P.wait(VEC, "sdw", dw_bvo)
        P.op(VEC, lambda e: e.tensor_copy(bv_b[:], g_row[:, 0:512]))
        sv_rows = P.op(VEC, lambda e: e.tensor_copy(bout_b[:], b_row), "sv")

        with contextlib.ExitStack() as ps:
            p_v = [ps.enter_context(nc.psum_tensor(
                f"p_v{i}", [128, 512], F32)) for i in range(2)]
            p_q2 = ps.enter_context(nc.psum_tensor("p_q2", [1, 2048], F32))
            p_nc_ = ps.enter_context(nc.psum_tensor("p_nc", [128, 4], F32))
            stb = stage.bitcast(BF16)
            bsl = [stb[:, 0:2048], stb[:, 2048:4096]]
            cs_list = [None] * 8
            mx_list = [None] * 8
            P.wait(TEN, "sq", sq_wv)
            P.wait(TEN, "sv", qkproj_sv)
            P.wait(TEN, "sv", sv_rows)
            P.wait(VEC, "sq", vp2_sq)
            P.wait(VEC, "sq", sq_wout)
            P.wait(VEC, "sv", sv_rows)
            v_ev = [None, None]

            def emit_mx(fc):
                P.wait(VEC, "sp", cs_list[fc])
                mx_list[fc] = P.op(VEC, lambda e, fc=fc: e.reduce_max(
                    mqk8[:, fc:fc + 1], p_q2[:], axis=AX.X), "sv")

            def emit_vtile(t):
                sl = t % 2
                if v_ev[sl] is not None:
                    P.wait(TEN, "sv", v_ev[sl])
                for c in range(NC):
                    P.op(TEN, lambda e, c=c, t=t, sl=sl: e.matmul(
                        p_v[sl][:],
                        big_b[:, S * c + 128 * t:S * c + 128 * (t + 1)],
                        wv_bf[:, 512 * c:512 * (c + 1)],
                        start=(c == 0), stop=False))
                ms = P.op(TEN, lambda e, sl=sl: e.matmul(
                    p_v[sl][:], onesb_r[:], bv_b[:], start=False, stop=True),
                    "sp")
                P.wait(VEC, "sp", ms)
                v_ev[sl] = P.op(VEC, lambda e, t=t, sl=sl: e.tensor_copy(
                    vprime[:, 520 * t:520 * (t + 1)].rearrange(
                        "p (h c) -> p h c", h=8)[:, :, 0:64],
                    p_v[sl][:].rearrange("p (h c) -> p h c", h=8)), "sv")

            for fc in range(8):
                if fc >= 1:
                    emit_mx(fc - 1)
                if fc == 0:
                    P.wait(VEC, "sv", qkproj_sv)
                if fc >= 2:
                    P.wait(VEC, "sp", cs_list[fc - 2])
                s2_sv = P.op(VEC, lambda e, fc=fc: e.tensor_tensor(
                    bsl[fc % 2], qk_of(fc), qk_of(fc), ALU.mult), "sv")
                emit_vtile(2 * fc)
                emit_vtile(2 * fc + 1)
                P.wait(TEN, "sv", s2_sv)
                if fc >= 1:
                    P.wait(TEN, "sv", mx_list[fc - 1])
                ms = None
                for sp_ in range(4):
                    ms = P.op(TEN, lambda e, sp_=sp_, fc=fc: e.matmul(
                        p_q2[:, 512 * sp_:512 * (sp_ + 1)], onesb_c[:],
                        bsl[fc % 2][:, 512 * sp_:512 * (sp_ + 1)],
                        start=True, stop=True, skip_group_check=True),
                        "sp" if sp_ == 3 else None)
                cs_list[fc] = ms
            emit_mx(7)
            vproj_sv = max(v for v in v_ev if v is not None)
            P.drain(VEC)
            mq_sv = P.op(VEC, lambda e: e.tensor_copy(rtmp[:], mqk8[:]), "sv")
            P.wait(SCA, "sv", mq_sv)
            ln8_sa = P.op(SCA, lambda e: e.activation(rtmp2[:], rtmp[:], AF.Ln),
                          "sa")
            P.wait(VEC, "sa", ln8_sa)
            pr_view = rtmp2.rearrange("p (k two) -> p k two", two=2)
            sum_sv = P.op(VEC, lambda e: e.tensor_tensor(
                c4[:], pr_view[:, :, 0:1].rearrange("p k two -> p (k two)"),
                pr_view[:, :, 1:2].rearrange("p k two -> p (k two)"),
                ALU.add), "sv")
            P.wait(SCA, "sv", sum_sv)
            c4_sa = P.op(SCA, lambda e: e.activation(
                rtmp[0:1, 0:4], c4[:], AF.Exp, scale=0.5), "sa")
            P.wait(TEN, "sa", c4_sa)
            ms = P.op(TEN, lambda e: e.matmul(
                p_nc_[:], cn1row, rtmp[0:1, 0:4], start=True,
                stop=True), "sp")
            P.wait(VEC, "sp", ms)
            negc_sv = P.op(VEC, lambda e: e.tensor_copy(
                negc_bc[:], p_nc_[:]), "sv")

        with contextlib.ExitStack() as ps:
            reg = [ps.enter_context(nc.psum_tensor(
                f"p_reg{i}", [128, 1024], F32)) for i in range(2)]
            p_oA = ps.enter_context(nc.psum_tensor("p_oA", [65, 512], F32))
            p_oB = ps.enter_context(nc.psum_tensor("p_oB", [65, 512], F32))
            p_bc0 = ps.enter_context(nc.psum_tensor("p_bcst0", [64, 512], F32))
            p_bc1 = ps.enter_context(nc.psum_tensor("p_bcst1", [64, 512], F32))

            P.wait(TEN, "sv", max(vproj_sv, negc_sv))
            P.wait(TEN, "sq", vp2_sq)
            P.wait(SCA, "sv", negc_sv)
            P.wait(SCA, "sq", sq_wout)
            P.wait(VEC, "sq", sq_wout)
            reg_free = [None, None]
            pslot_free = [None, None, None]
            rowtmp = stage[0:1, 1536:2048]
            rowtmpB = wqk_bf.bitcast(F32)[0:1, 0:512]
            bcA_sb = stage[0:64, 1024:1536]
            bcB_sb = stage[64:128, 1024:1536]

            def vsl(j, h):
                return vprime[:, 520 * j + 65 * h:520 * j + 65 * h + 65]

            state = {"pend": None, "o_free": None, "bc_free": None}

            def onorm_act(pend):
                _, _, pv15 = pend
                P.wait(SCA, "sp", pv15)
                P.op(SCA, lambda e: e.activation(rowtmp, p_oA[64:65, :], AF.Ln))
                P.op(SCA, lambda e: e.activation(rowtmpB, p_oB[64:65, :],
                                                 AF.Ln))
                P.drain(SCA)
                P.op(SCA, lambda e: e.activation(recA, rowtmp, AF.Exp,
                                                 scale=-1.0))
                return P.op(SCA, lambda e: e.activation(
                    recB[:], rowtmpB, AF.Exp, scale=-1.0), "sa")

            def onorm_pe(pend, rec_ms):
                P.wait(TEN, "sa", rec_ms)
                P.wait(TEN, "sv", state["bc_free"])
                P.op(TEN, lambda e: e.matmul(
                    p_bc0[:], crows[0:1, 0:64], recA, start=True, stop=True))
                return P.op(TEN, lambda e: e.matmul(
                    p_bc1[:], crows[0:1, 0:64], recB[:], start=True,
                    stop=True), "sp")

            def onorm_vec(pend, bc_ms):
                pr, i, _ = pend
                P.wait(VEC, "sp", bc_ms)
                P.op(VEC, lambda e: e.tensor_copy(bcA_sb, p_bc0[:]))
                P.op(VEC, lambda e: e.tensor_copy(bcB_sb, p_bc1[:]))
                P.drain(VEC)
                P.op(VEC, lambda e, pr=pr, i=i: e.tensor_tensor(
                    ost_of(pr)[0:64, 512 * i:512 * (i + 1)], p_oA[0:64, :],
                    bcA_sb, ALU.mult))
                ev = P.op(VEC, lambda e, pr=pr, i=i: e.tensor_tensor(
                    ost_of(pr)[64:128, 512 * i:512 * (i + 1)],
                    p_oB[0:64, :], bcB_sb, ALU.mult), "sv")
                state["o_free"] = ev
                state["bc_free"] = ev

            blocks = [(i, pr) for i in range(4) for pr in range(4)]
            TOT = 16 * 16
            exp_hist = {}
            qk_hist = {}
            rec_pend = None
            for g in range(TOT + 3):
                if g >= TOT:
                    P.wait(TEN, "sa", exp_hist.get(g - 2))
                if g < TOT:
                    b, j = blocks[g // 16], g % 16
                    i, pr = b
                    sl = g % 2
                    P.wait(TEN, "sa", exp_hist.get(g - 2))
                    P.op(TEN, lambda e, pr=pr, i=i, j=j, sl=sl: e.matmul(
                        reg[sl][:, 0:512],
                        qk_of(2 * pr + 1)[0:64, 128 * j:128 * (j + 1)],
                        qk_of(2 * pr)[0:64, 512 * i:512 * (i + 1)],
                        start=True, stop=True, tile_position=(0, 0),
                        skip_group_check=True))
                    qk_hist[g] = P.op(TEN, lambda e, pr=pr, i=i, j=j, sl=sl:
                                      e.matmul(
                        reg[sl][:, 512:1024],
                        qk_of(2 * pr + 1)[64:128, 128 * j:128 * (j + 1)],
                        qk_of(2 * pr)[64:128, 512 * i:512 * (i + 1)],
                        start=True, stop=True, tile_position=(64, 0),
                        skip_group_check=True), "sp")
                if rec_pend is not None and g % 16 == 3:
                    pend, rec_ms = rec_pend
                    bc_ms = onorm_pe(pend, rec_ms)
                    onorm_vec(pend, bc_ms)
                    rec_pend = None
                pg = g - 3
                if pg >= 0:
                    pb, pj = blocks[pg // 16], pg % 16
                    pi, ppr = pb
                    hA, hB = 2 * ppr, 2 * ppr + 1
                    psl = pg % 3
                    if pj == 0:
                        P.wait(TEN, "sv", state["o_free"])
                    P.op(TEN, lambda e, pj=pj, hA=hA, psl=psl: e.matmul(
                        p_oA[:], vsl(pj, hA),
                        p_buf[:, 1024 * psl:1024 * psl + 512],
                        start=(pj == 0), stop=(pj == 15),
                        skip_group_check=True))
                    pv_ms = P.op(TEN, lambda e, pj=pj, hB=hB, psl=psl: e.matmul(
                        p_oB[:], vsl(pj, hB),
                        p_buf[:, 1024 * psl + 512:1024 * (psl + 1)],
                        start=(pj == 0), stop=(pj == 15),
                        skip_group_check=True), "sp")
                    pslot_free[psl] = pv_ms
                    if pj == 15:
                        pend = (ppr, pi, pv_ms)
                        rec_pend = (pend, onorm_act(pend))
                if g < TOT:
                    if pslot_free[g % 3] is not None and \
                            pslot_free[g % 3] > qk_hist[g]:
                        P.wait(SCA, "sp", pslot_free[g % 3])
                    else:
                        P.wait(SCA, "sp", qk_hist[g])
                        P.wait(SCA, "sp", pslot_free[g % 3])
                    exp_hist[g] = P.op(SCA, lambda e, pr=pr, sl=sl, es=(g % 3):
                                       e.activation(
                        p_buf[:, 1024 * es:1024 * (es + 1)], reg[sl][:],
                        AF.Exp, bias=negc_bc[:, pr:pr + 1], scale=1.0), "sa")
            pend, rec_ms = rec_pend
            bc_ms = onorm_pe(pend, rec_ms)
            onorm_vec(pend, bc_ms)
            ostack_sv = P.cnt["sv"]
            p4_sp_last = P.cnt["sp"]

        P.op(VEC, lambda e: e.tensor_scalar(
            alpha_a[:], r2_all[:], RS2, 1.0, ALU.mult, ALU.add))
        al_sv = P.op(VEC, lambda e: e.scalar_tensor_tensor(
            m2_a[:], mv_mean2, -RS2, r2_all[:], ALU.mult, ALU.mult), "sv")
        P.drain(VEC)

        xbt = vprime.bitcast(F32)

        def xa_of(k):
            return xbt[:, 1024 * k:1024 * (k + 1)]

        zview = big_b.rearrange("p (c tt) -> p c tt", c=8)
        with contextlib.ExitStack() as ps:
            p_y = [ps.enter_context(nc.psum_tensor(
                f"p_yo{i}", [128, 1024], F32)) for i in range(3)]
            p_t2f = [ps.enter_context(nc.psum_tensor(
                f"p_t2f{k}", [128, 1024], BF16)) for k in range(2)]
            P.wait(TEN, "sv", ostack_sv)
            P.wait(TEN, "sq", sq_wout)
            P.wait(SCA, "sv", ostack_sv)
            P.wait(SCA, "sp", p4_sp_last)
            P.wait(SCA, "sq", eps_sq)
            w1a_hist = [None] * NT
            fin_hist = [None] * NT
            yr_hist = [None] * NT
            ap_hist = [None] * NT
            ag_hist = [None] * NT
            x2ag_hist = [None] * NT
            lnb_hist = [None] * NT
            sum2_hist = [None] * NT
            sexp_hist = [None] * NT
            nb_hist = [None] * NT
            tr_hist = [None] * NT
            xb_pend = [None] * NT
            w1b_pend = [None] * NT
            ev_hist = [None, None]
            last_ev = None
            nacc_q = []
            nsum_sq2 = None

            wq_f = wqk_bf.bitcast(F32)

            def u4(k):
                return wq_f[:, 1024 * k:1024 * (k + 1)]

            def ymv(k):
                return stat_mv[:, 4 + 2 * k:6 + 2 * k]

            def x2m(td):
                return x2mv[:, 2 * td:2 * td + 1]

            def x2v(td):
                return x2mv[:, 2 * td + 1:2 * td + 2]

            def scol(buf, td):
                return buf[:, td:td + 1]

            for i in range(NT + 10):
                if i < NT:
                    t = i
                    sl = t % 3
                    if t >= 3:
                        P.wait(TEN, "sa", w1a_hist[t - 3])
                        P.wait(TEN, "sv", ag_hist[t - 3])
                    for h in range(2):
                        for pr in range(4):
                            P.op(TEN, lambda e, t=t, h=h, pr=pr, sl=sl:
                                 e.matmul(
                                p_y[sl][:, 512 * h:512 * (h + 1)],
                                ost_of(pr)[:, 128 * t:128 * (t + 1)],
                                wout_bf[:, D * pr + 512 * h:
                                        D * pr + 512 * (h + 1)],
                                start=(pr == 0), stop=False,
                                skip_group_check=True))
                        y_ms = P.op(TEN, lambda e, h=h, sl=sl: e.matmul(
                            p_y[sl][:, 512 * h:512 * (h + 1)], onesb_r[:],
                            bout_b[:, 512 * h:512 * (h + 1)], start=False,
                            stop=True, skip_group_check=True), "sp")
                if 0 <= i - 10 < NT:
                    td = i - 10
                    P.wait(TEN, "sa", ap_hist[td])
                    P.wait(TEN, "sv", ev_hist[td % 2])
                    for c in range(NC):
                        tr_hist[td] = P.op(TEN, lambda e, td=td, c=c:
                                           e.transpose(
                            p_t2f[td % 2][:, 128 * c:128 * (c + 1)],
                            big_a[:, D * td + 128 * c:D * td + 128 * (c + 1)],
                            identb[:]), "sp" if c == NC - 1 else None)
                if i < NT:
                    P.wait(VEC, "sp", y_ms)
                    if i >= 1:
                        P.wait(VEC, "sv", ag_hist[i - 1])
                    P.op(VEC, lambda e, sl=sl: e.bn_stats(
                        bnbuf[:, 0:6], p_y[sl][:, 0:512]))
                    P.op(VEC, lambda e, sl=sl: e.bn_stats(
                        bnbuf[:, 6:12], p_y[sl][:, 512:1024]))
                P.drain(VEC)
                if i < NT:
                    ag_hist[i] = P.op(VEC, lambda e, i=i: e.bn_aggr(
                        ymv(i % 2), bnbuf[:, 0:12]), "sv")
                if 0 <= i - 2 < NT:
                    td = i - 2
                    P.wait(VEC, "sq", xb_pend[td])
                    P.op(VEC, lambda e, td=td: e.scalar_tensor_tensor(
                        xa_of(td % 4), gamC[:], ynb[:, td % 2:td % 2 + 1],
                        xa_of(td % 4), ALU.mult, ALU.add))
                if 0 <= i - 3 < NT:
                    td = i - 3
                    P.wait(VEC, "sq", w1b_pend[td])
                    fin_hist[td] = P.op(VEC, lambda e, td=td: e.tensor_tensor(
                        xt(td), xa_of(td % 4), u4(td % 4), ALU.add), "sv")
                if 0 <= i - 4 < NT:
                    td = i - 4
                    if td >= 2:
                        P.wait(VEC, "sv", x2ag_hist[td - 2])
                    P.op(VEC, lambda e, td=td: e.bn_stats(
                        bnbuf[:, 12 + 12 * (td % 2):18 + 12 * (td % 2)],
                        xt(td)[:, 0:512]))
                    P.op(VEC, lambda e, td=td: e.bn_stats(
                        bnbuf[:, 18 + 12 * (td % 2):24 + 12 * (td % 2)],
                        xt(td)[:, 512:1024]))
                if 0 <= i - 5 < NT:
                    td = i - 5
                    x2ag_hist[td] = P.op(VEC, lambda e, td=td: e.bn_aggr(
                        x2mv[:, 2 * td:2 * td + 2],
                        bnbuf[:, 12 + 12 * (td % 2):24 + 12 * (td % 2)]), "sv")
                if 0 <= i - 8 < NT:
                    td = i - 8
                    P.wait(VEC, "sa", lnb_hist[td])
                    sum2_hist[td] = P.op(VEC, lambda e, td=td: e.tensor_tensor(
                        scol(sc_a, td), scol(sc_a, td), scol(sc_c, td),
                        ALU.add), "sv")
                if 0 <= i - 9 < NT:
                    td = i - 9
                    P.wait(VEC, "sa", sexp_hist[td])
                    nb_hist[td] = P.op(VEC, lambda e, td=td:
                                       e.scalar_tensor_tensor(
                        scol(sc_c, td), x2m(td), -1.0, scol(sc_b, td),
                        ALU.mult, ALU.mult), "sv")
                if 0 <= i - 1 < NT:
                    td = i - 1
                    P.wait(VEC, "sa", yr_hist[td])
                    P.op(VEC, lambda e, td=td: e.scalar_tensor_tensor(
                        ynb[:, td % 2:td % 2 + 1], ymv(td % 2)[:, 0:1], -1.0,
                        yrstd[:, td % 2:td % 2 + 1], ALU.mult, ALU.mult))
                if 0 <= i - 10 < NT:
                    td = i - 10
                    P.wait(VEC, "sp", tr_hist[td])
                    last_ev = P.op(VEC, lambda e, td=td: e.tensor_copy(
                        zview[:, :, 128 * td:128 * (td + 1)],
                        p_t2f[td % 2][:].rearrange("p (c tt) -> p c tt", c=8)),
                        "sv")
                    ev_hist[td % 2] = last_ev
                if i < NT:
                    t = i
                    P.wait(SCA, "sv", ag_hist[t])
                    P.op(SCA, lambda e, t=t: e.activation(
                        ylnv[:, t % 2:t % 2 + 1], ymv(t % 2)[:, 1:2], AF.Ln,
                        bias=epscol[:], scale=1.0))
                if 0 <= i - 6 < NT:
                    td = i - 6
                    P.wait(SCA, "sv", x2ag_hist[td])
                    P.op(SCA, lambda e, td=td: e.activation(
                        scol(sc_a, td), x2v(td), AF.Ln,
                        bias=epscol[:], scale=1.0))
                P.drain(SCA)
                if i < NT:
                    t = i
                    yr_hist[t] = P.op(SCA, lambda e, t=t: e.activation(
                        yrstd[:, t % 2:t % 2 + 1], ylnv[:, t % 2:t % 2 + 1],
                        AF.Exp, scale=-0.5), "sa")
                if 0 <= i - 6 < NT:
                    td = i - 6
                    P.op(SCA, lambda e, td=td: e.activation(
                        scol(sc_b, td), scol(sc_a, td), AF.Exp, scale=-1.0))
                if 0 <= i - 7 < NT:
                    td = i - 7
                    lnb_hist[td] = P.op(SCA, lambda e, td=td: e.activation(
                        scol(sc_c, td), x2v(td), AF.Ln,
                        bias=epscol[:], scale=scol(sc_b, td)), "sa")
                P.drain(SCA)
                if i < NT:
                    t = i
                    if t == 0:
                        P.wait(SCA, "sv", al_sv)
                    if t >= 4:
                        P.wait(SCA, "sv", fin_hist[t - 4])
                    xa_sa = P.op(SCA, lambda e, t=t: e.activation(
                        xa_of(t % 4), xt(t), AF.Identity,
                        bias=m2_a[:, t:t + 1], scale=alpha_a[:, t:t + 1]),
                        "sa")
                    w1a_hist[t] = P.op(SCA, lambda e, t=t: e.activation(
                        u4(t % 4), p_y[t % 3][:], AF.Identity,
                        scale=yrstd[:, t % 2:t % 2 + 1]), "sa")
                if 0 <= i - 8 < NT:
                    td = i - 8
                    P.wait(SCA, "sv", sum2_hist[td])
                    sexp_hist[td] = P.op(SCA, lambda e, td=td: e.activation(
                        scol(sc_b, td), scol(sc_a, td), AF.Exp, scale=-0.5),
                        "sa")
                if 0 <= i - 9 < NT:
                    td = i - 9
                    P.wait(SCA, "sv", nb_hist[td])
                    ap_hist[td] = P.op(SCA, lambda e, td=td: e.activation(
                        nz_t(td), xt(td), AF.Identity,
                        bias=scol(sc_c, td), scale=scol(sc_b, td)), "sa")
                if i < NT:
                    t = i
                    P.wait(POOL, "sa", xa_sa)
                    xb_pend[t] = P.op(POOL, lambda e, t=t: e.tensor_tensor(
                        xa_of(t % 4), xa_of(t % 4), betC[:], ALU.add), "sq")
                    P.wait(POOL, "sa", w1a_hist[t])
                    w1b_pend[t] = P.op(POOL, lambda e, t=t: e.tensor_tensor(
                        u4(t % 4), u4(t % 4), gamC[:], ALU.mult), "sq")
            P.wait(SCA, "sv", last_ev)
            for c in range(NC):
                nsum_sq2 = P.op(SCA, lambda e, c=c: e.activation(
                    big_a[:, 2048 * c:2048 * (c + 1)], zT_c(c), AF.Identity,
                    scale=1.0, accum_out=nsum[:, c:c + 1]), "sa")

        w1a_last = w1a_hist[NT - 1]
        x_done3, _ = emit_ffn_back(nsum_sq2, stage_guard=("sa", w1a_last),
                                   rt_guards=[("sa", w1a_last)])

        yr_ = d_y.rearrange("(t p) d -> t p d", p=128)
        for t in range(NT):
            P.wait(SYN, x_done3[t][0], x_done3[t][1])
            P.dma(SYN, lambda e, t=t: e.dma_start(yr_[t], xt(t)), "sout")
        P.wait(SYN, "sout", P.cnt["sout"])

        block = E(nc.Block())

        @block.sync
        def _(eng):
            for fn in P.ops[SYN]:
                fn(eng)

        @block.tensor
        def _(eng):
            for fn in P.ops[TEN]:
                fn(eng)

        @block.vector
        def _(eng):
            for fn in P.ops[VEC]:
                fn(eng)

        @block.scalar
        def _(eng):
            for fn in P.ops[SCA]:
                fn(eng)

        @block.gpsimd
        def _(eng):
            for fn in P.ops[POOL]:
                fn(eng)

    return nc


def _host_inputs(inp):
    W_qk = np.asarray(inp["L0_attn_W_qk"])
    b_qk = np.asarray(inp["L0_attn_b_qk"])
    perm = np.empty(1024, dtype=np.int64)
    for p in range(4):
        for hi in range(2):
            h = 2 * p + hi
            d = np.arange(64)
            perm[(2 * p) * 128 + hi * 64 + d] = h * 128 + 2 * d
            perm[(2 * p + 1) * 128 + hi * 64 + d] = h * 128 + 2 * d + 1
    wqk = np.ascontiguousarray(W_qk[:, perm])
    bqkp = np.ascontiguousarray(b_qk[perm])

    def chunk2(v):
        return np.ascontiguousarray(np.asarray(v).reshape(8, 128).T)

    com = {
        "wqk": wqk,
        "wv": np.ascontiguousarray(np.asarray(inp["L0_attn_W_v"])),
        "wout": np.ascontiguousarray(np.asarray(inp["L0_attn_W_out"])),
        "bv_row": np.asarray(inp["L0_attn_b_v"]).reshape(1, 512).copy(),
        "bout_row": np.asarray(inp["L0_attn_b_out"]).reshape(1, 1024).copy(),
        "gamma_row": np.asarray(inp["L0_attn_ln_gamma"]).reshape(1, 1024).copy(),
        "beta_row": np.asarray(inp["L0_attn_ln_beta"]).reshape(1, 1024).copy(),
        "spk": np.concatenate([
            chunk2(bqkp), chunk2(inp["L0_ffn_ln_g"]),
            chunk2(inp["L0_ffn_ln_b"]), chunk2(inp["L0_ffn_bn_mean"]),
            chunk2(inp["L0_ffn_bn_var"]), chunk2(inp["L0_ffn_bn_b"]),
            chunk2(inp["L0_ffn_bn_g"]),
            np.ascontiguousarray(np.asarray(inp["L0_ffn_Wr"]).reshape(8, 128, 8)
                                 .transpose(1, 0, 2).reshape(128, 64)),
        ], axis=1).astype(np.float32),
        "wd2": np.ascontiguousarray(
            np.asarray(inp["L0_ffn_W_down"]).reshape(8, 8, 128, 64)
            .transpose(0, 2, 1, 3).reshape(1024, 512)),
        "wu2": np.ascontiguousarray(
            np.asarray(inp["L0_ffn_W_up"]).reshape(512, 1024)),
        "bd2": np.asarray(inp["L0_ffn_b_down"]).reshape(512, 1).copy(),
        "bu2": np.ascontiguousarray(np.asarray(inp["L0_ffn_b_up"])),
        "identb": np.eye(128, dtype=np.float32).astype(ml_dtypes.bfloat16),
        "identf8": np.eye(8, dtype=np.float32),
        "crows": np.concatenate([
            np.full(128, 1.0, np.float32), np.full(128, -1.0, np.float32),
            np.full(128, 128.0, np.float32), np.full(64, 64.0, np.float32),
            np.full(128, 0.5, np.float32), np.full(128, RS2, np.float32),
            np.arange(8, dtype=np.float32), np.full(8, 1.0, np.float32),
            np.asarray(inp["L0_ffn_br"], dtype=np.float32).reshape(8),
            np.full(40, 1.0, np.float32),
        ]).reshape(1, 768),
        "onesb_r": np.ones((1, 128), np.float32).astype(ml_dtypes.bfloat16),
        "onesb_c": np.ones((128, 1), np.float32).astype(ml_dtypes.bfloat16),
        "iotai": np.arange(128, dtype=np.int32).reshape(128, 1),
    }
    for k, v in com.items():
        if v.dtype == np.float64:
            com[k] = v.astype(np.float32)
    return com


def kernel(**inputs):
    x = np.ascontiguousarray(np.asarray(inputs["L0_attn_x"], dtype=np.float32))
    com = _host_inputs(inputs)
    nc = build_program()
    in_maps = []
    for b in range(NCORE):
        m = dict(com)
        m["x"] = np.ascontiguousarray(x[b])
        in_maps.append(m)
    res = run_bass_kernel_spmd(nc, in_maps, core_ids=list(range(NCORE)))
    out = np.stack([np.asarray(res.results[b]["y"]) for b in range(NCORE)],
                   axis=0)
    return out.astype(np.float32)


# version 29
# speedup vs baseline: 1.0259x; 1.0259x over previous
import contextlib
from collections import defaultdict

import numpy as np
import ml_dtypes
import concourse.bass as bass
import concourse.mybir as mybir
from concourse.bass_utils import run_bass_kernel_spmd

F32 = mybir.dt.float32
BF16 = mybir.dt.bfloat16
I32 = mybir.dt.int32
AF = mybir.ActivationFunctionType
ALU = mybir.AluOpType
AX = mybir.AxisListType

NCORE = 8
S = 2048
D = 1024
NT = 16
QK_IN_LOOP = True
NC = 8
EPS = 1e-5
RS2 = 2.0 ** (-0.5)
EPSA = 1e-30


class Prog:

    def __init__(self, sems):
        self.ops = defaultdict(list)
        self.cnt = defaultdict(int)
        self.sems = sems

    def emit(self, eng, fn):
        self.ops[eng].append(fn)

    def op(self, eng, fn, sem=None):
        if sem is None:
            self.emit(eng, fn)
            return None
        self.cnt[sem] += 1
        s = self.sems[sem]

        def run(e, fn=fn, s=s):
            fn(e).then_inc(s, 1)
        self.emit(eng, run)
        return self.cnt[sem]

    def dma(self, eng, fn, sem):
        self.cnt[sem] += 16
        s = self.sems[sem]

        def run(e, fn=fn, s=s):
            fn(e).then_inc(s, 16)
        self.emit(eng, run)
        return self.cnt[sem]

    def wait(self, eng, sem, value):
        if value is None:
            return
        s = self.sems[sem]
        self.emit(eng, lambda e, s=s, v=value: e.wait_ge(s, v))

    def drain(self, eng):
        self.emit(eng, lambda e: e.drain())


def build_program():
    nc = bass.Bass("TRN2", target_bir_lowering=False)

    d_x = nc.dram_tensor("x", [S, D], F32, kind="ExternalInput").ap()
    d_wqk = nc.dram_tensor("wqk", [D, D], F32, kind="ExternalInput").ap()
    d_wv = nc.dram_tensor("wv", [D, 512], F32, kind="ExternalInput").ap()
    d_wout = nc.dram_tensor("wout", [512, D], F32, kind="ExternalInput").ap()
    d_spk = nc.dram_tensor("spk", [128, 120], F32, kind="ExternalInput").ap()
    d_bv_row = nc.dram_tensor("bv_row", [1, 512], F32, kind="ExternalInput").ap()
    d_bout_row = nc.dram_tensor("bout_row", [1, D], F32, kind="ExternalInput").ap()
    d_gamma_row = nc.dram_tensor("gamma_row", [1, D], F32, kind="ExternalInput").ap()
    d_beta_row = nc.dram_tensor("beta_row", [1, D], F32, kind="ExternalInput").ap()
    d_wd2 = nc.dram_tensor("wd2", [1024, 512], F32, kind="ExternalInput").ap()
    d_wu2 = nc.dram_tensor("wu2", [512, 1024], F32, kind="ExternalInput").ap()
    d_bd2 = nc.dram_tensor("bd2", [512, 1], F32, kind="ExternalInput").ap()
    d_bu2 = nc.dram_tensor("bu2", [8, 1024], F32, kind="ExternalInput").ap()
    d_identb = nc.dram_tensor("identb", [128, 128], BF16, kind="ExternalInput").ap()
    d_identf8 = nc.dram_tensor("identf8", [8, 8], F32, kind="ExternalInput").ap()
    d_crows = nc.dram_tensor("crows", [1, 768], F32, kind="ExternalInput").ap()
    d_onesb_r = nc.dram_tensor("onesb_r", [1, 128], BF16, kind="ExternalInput").ap()
    d_onesb_c = nc.dram_tensor("onesb_c", [128, 1], BF16, kind="ExternalInput").ap()
    d_iotai = nc.dram_tensor("iotai", [128, 1], I32, kind="ExternalInput").ap()
    d_y = nc.dram_tensor("y", [S, D], F32, kind="ExternalOutput").ap()

    st = contextlib.ExitStack()
    with st:
        E = st.enter_context
        x_sb = E(nc.sbuf_tensor("sb_x_sb", [128, NT * D], F32))
        big_a = E(nc.sbuf_tensor("sb_big_a", [128, NT * D], BF16))
        big_b = E(nc.sbuf_tensor("sb_big_b", [128, NT * D], BF16))
        vprime = E(nc.sbuf_tensor("sb_vprime", [128, 16 * 520], BF16))
        p_buf = E(nc.sbuf_tensor("sb_p_buf", [128, 3072], BF16))
        hT = p_buf[0:65, :]
        wqk_bf = E(nc.sbuf_tensor("sb_wqk_bf", [128, 8 * D], BF16))
        wv_bf = E(nc.sbuf_tensor("sb_wv_bf", [128, 8 * 512], BF16))
        wout_bf = E(nc.sbuf_tensor("sb_wout_bf", [128, 4 * D], BF16))
        stage = E(nc.sbuf_tensor("sb_stage", [128, 2048], F32))
        gamC = E(nc.sbuf_tensor("sb_gamC", [128, D], BF16))
        betC = E(nc.sbuf_tensor("sb_betC", [128, D], BF16))
        wd_bf = E(nc.sbuf_tensor("sb_wd_bf", [128, 512], BF16))
        wu_ext = E(nc.sbuf_tensor("sb_wu_ext", [72, 1024], BF16))
        bv_b = E(nc.sbuf_tensor("sb_bv_b", [1, 512], BF16))
        bout_b = E(nc.sbuf_tensor("sb_bout_b", [1, D], BF16))
        spk = E(nc.sbuf_tensor("sb_spk", [128, 120], F32))
        bqk2 = spk[:, 0:8]
        ffng2 = spk[:, 8:16]
        ffnb2 = spk[:, 16:24]
        bnm2 = spk[:, 24:32]
        bnv2 = spk[:, 32:40]
        bnb2 = spk[:, 40:48]
        bnsc2 = spk[:, 48:56]
        wr2 = spk[:, 56:120]
        bqk2s = E(nc.sbuf_tensor("sb_bqk2s", [128, 8], F32))
        tmp8 = E(nc.sbuf_tensor("sb_tmp8", [128, 8], F32))
        tmp8b = E(nc.sbuf_tensor("sb_tmp8b", [128, 8], F32))
        bnbuf = E(nc.sbuf_tensor("sb_bnbuf", [128, 12 * NT], F32))
        stat_mv = E(nc.sbuf_tensor("sb_stat_mv", [128, 2 * NT], F32))
        sc_a = E(nc.sbuf_tensor("sb_sc_a", [128, NT], F32))
        sc_b = E(nc.sbuf_tensor("sb_sc_b", [128, NT], F32))
        sc_c = E(nc.sbuf_tensor("sb_sc_c", [128, NT], F32))
        mu2_all = E(nc.sbuf_tensor("sb_mu2_all", [128, NT], F32))
        r2_all = E(nc.sbuf_tensor("sb_r2_all", [128, NT], F32))
        alpha_a = E(nc.sbuf_tensor("sb_alpha_a", [128, NT], F32))
        m2_a = E(nc.sbuf_tensor("sb_m2_a", [128, NT], F32))
        nsum = E(nc.sbuf_tensor("sb_nsum", [128, 8], F32))
        r8 = E(nc.sbuf_tensor("sb_r8", [8, 1], F32))
        rrow = E(nc.sbuf_tensor("sb_rrow", [1, 8], F32))
        rex = E(nc.sbuf_tensor("sb_rex", [1, 8], F32))
        rtmp = E(nc.sbuf_tensor("sb_rtmp", [1, 8], F32))
        rtmp2 = E(nc.sbuf_tensor("sb_rtmp2", [1, 8], F32))
        rmax = E(nc.sbuf_tensor("sb_rmax", [1, 1], F32))
        rnmax = E(nc.sbuf_tensor("sb_rnmax", [1, 1], F32))
        rsum = E(nc.sbuf_tensor("sb_rsum", [1, 1], F32))
        rrec = E(nc.sbuf_tensor("sb_rrec", [1, 1], F32))
        gate1 = E(nc.sbuf_tensor("sb_gate1", [1, 1], F32))
        idxf = E(nc.sbuf_tensor("sb_idxf", [1, 1], F32))
        idxi1 = E(nc.sbuf_tensor("sb_idxi1", [1, 1], I32))
        idxc = E(nc.sbuf_tensor("sb_idxc", [128, 1], F32))
        idxi = E(nc.sbuf_tensor("sb_idxi", [128, 1], I32))
        widx = E(nc.sbuf_tensor("sb_widx", [128, 1], I32))
        uidxc = E(nc.sbuf_tensor("sb_uidxc", [64, 1], F32))
        uidxi = E(nc.sbuf_tensor("sb_uidxi", [64, 1], I32))
        uidx = E(nc.sbuf_tensor("sb_uidx", [64, 1], I32))
        sgcol = E(nc.sbuf_tensor("sb_sgcol", [128, 1], F32))
        bd_f = E(nc.sbuf_tensor("sb_bd_f", [64, 1], F32))
        hbias = E(nc.sbuf_tensor("sb_hbias", [64, 1], F32))
        mqk8 = E(nc.sbuf_tensor("sb_mqk8", [1, 8], F32))
        c4 = E(nc.sbuf_tensor("sb_c4", [1, 4], F32))
        negc_bc = E(nc.sbuf_tensor("sb_negc_bc", [128, 4], F32))
        yve = E(nc.sbuf_tensor("sb_yve", [128, 2], F32))
        x2mv = E(nc.sbuf_tensor("sb_x2mv", [128, 2 * NT], F32))
        epscol = E(nc.sbuf_tensor("sb_epscol", [128, 1], F32))
        gsS = E(nc.sbuf_tensor("sb_gsS", [128, 8], F32))
        cC = E(nc.sbuf_tensor("sb_cC", [128, 8], F32))
        npart = E(nc.sbuf_tensor("sb_npart", [128, 32], F32))
        ylnv = E(nc.sbuf_tensor("sb_ylnv", [128, 2], F32))
        yrstd = E(nc.sbuf_tensor("sb_yrstd", [128, 2], F32))
        ynb = E(nc.sbuf_tensor("sb_ynb", [128, 2], F32))
        identb = E(nc.sbuf_tensor("sb_identb", [128, 128], BF16))
        identf8 = E(nc.sbuf_tensor("sb_identf8", [8, 8], F32))
        crows = E(nc.sbuf_tensor("sb_crows", [1, 768], F32))
        c1row = crows[0:1, 0:128]
        cn1row = crows[0:1, 128:256]
        c128row = crows[0:1, 256:384]
        c64row = crows[0:1, 384:448]
        c05row = crows[0:1, 448:576]
        crs2row = crows[0:1, 576:704]
        onesb_r = E(nc.sbuf_tensor("sb_onesb_r", [1, 128], BF16))
        onesb_c = E(nc.sbuf_tensor("sb_onesb_c", [128, 1], BF16))
        iotai = E(nc.sbuf_tensor("sb_iotai", [128, 1], I32))
        iota8f = crows[0:1, 704:712]
        br_row = crows[0:1, 720:728]
        ones11 = crows[0:1, 712:713]

        sems = {}
        for name in ["sdx", "sdxb", "sdx2", "sdx2b", "sdw", "sdl", "sdlb",
                     "sdg", "sout", "sp", "sv", "sa", "sq"]:
            sems[name] = E(nc.semaphore(name))

        P = Prog(sems)
        TEN, VEC, SCA, POOL, SYN = "tensor", "vector", "scalar", "gpsimd", "sync"

        def xt(t):
            return x_sb[:, D * t:D * (t + 1)]

        def nz_t(t):
            return big_a[:, D * t:D * (t + 1)]

        def zT_c(c):
            return big_b[:, S * c:S * (c + 1)]

        def qk_of(fc):
            return big_a[:, 2048 * fc:2048 * (fc + 1)]

        def ost_of(pr):
            return big_b[:, 2048 * pr:2048 * (pr + 1)]

        wd_f32 = stage[:, 0:512]
        wu_f32 = stage[0:64, 512:1536]
        bu_row = stage[64:65, 512:1536]

        def u_of(k):
            return stage[:, 1024 * k:1024 * (k + 1)]

        recA = stage[0:1, 0:512]
        recB = stage[0:1, 512:1024]
        bcAB = stage[0:64, 1024:1536]

        mv_mean = stat_mv.rearrange("p (t k) -> p t k", k=2)[:, :, 0:1]
        mv_var = stat_mv.rearrange("p (t k) -> p t k", k=2)[:, :, 1:2]
        mv_mean2 = mv_mean.rearrange("p t k -> p (t k)")
        mv_var2 = mv_var.rearrange("p t k -> p (t k)")

        xr = d_x.rearrange("(t p) d -> t p d", p=128)
        for sbuf, dram in [
            (spk[:], d_spk), (identb[:], d_identb), (identf8[:], d_identf8),
            (crows[:], d_crows), (onesb_r[:], d_onesb_r), (onesb_c[:], d_onesb_c),
            (iotai[:], d_iotai),
        ]:
            P.dma(POOL, lambda e, sb=sbuf, dr=dram: e.dma_start(sb, dr), "sdw")
        dw_small = P.cnt["sdw"]
        x_tile_dma = [None] * NT
        for grp, (eng, sems2) in enumerate((
                (SYN, ("sdx", "sdxb")), (POOL, ("sdx2", "sdx2b")))):
            for half in range(2):
                sname = sems2[half]
                tiles = [t for t in range(NT)
                         if t % 2 == grp and (t >= 8) == bool(half)]
                full = None
                for t in tiles:
                    full = P.dma(eng, lambda e, t=t: e.dma_start(
                        xt(t), xr[t]), sname)
                for t in tiles:
                    x_tile_dma[t] = (sname, full)
        P.wait(TEN, "sdw", dw_small)
        P.wait(VEC, "sdw", dw_small)
        P.wait(SCA, "sdw", dw_small)



        vp_ones_sq = P.op(POOL, lambda e: e.memset(
            vprime.rearrange("p (t h c) -> p t h c", t=16, h=8)[:, :, :, 64:65],
            1.0), "sq")
        eps_sq = P.op(POOL, lambda e: e.memset(epscol[:], EPS), "sq")

        P.wait(POOL, "sdw", dw_small)
        P.op(POOL, lambda e: e.tensor_scalar(bqk2s[:], bqk2, 0.125, None,
                                             ALU.mult))
        P.drain(POOL)
        sv_ve = P.op(POOL, lambda e: e.tensor_scalar(tmp8[:], bnv2, EPS, None,
                                                     ALU.add), "sq")
        P.wait(SCA, "sq", sv_ve)
        P.op(SCA, lambda e: e.activation(tmp8b[:], tmp8[:], AF.Ln))
        P.drain(SCA)
        sa_bnr = P.op(SCA, lambda e: e.activation(tmp8[:], tmp8b[:], AF.Exp,
                                                  scale=-0.5), "sa")
        P.wait(POOL, "sa", sa_bnr)
        sv_const = P.op(POOL, lambda e: e.tensor_tensor(bnsc2, bnsc2,
                                                        tmp8[:], ALU.mult),
                        "sq")
        P.drain(POOL)
        P.op(POOL, lambda e: e.tensor_tensor(gsS[:], ffng2, bnsc2, ALU.mult))
        P.op(POOL, lambda e: e.tensor_tensor(cC[:], ffnb2, bnm2, ALU.subtract))
        P.drain(POOL)
        P.op(POOL, lambda e: e.tensor_scalar(gsS[:], gsS[:], 1.0 / S, None,
                                             ALU.mult))
        P.op(POOL, lambda e: e.tensor_tensor(cC[:], cC[:], bnsc2, ALU.mult))
        P.drain(POOL)
        gs_sq = P.op(POOL, lambda e: e.tensor_tensor(cC[:], cC[:], bnb2,
                                                     ALU.add), "sq")

        bu_stage = stage[0:8, 0:1024]
        dw_bu = P.dma(POOL, lambda e: e.dma_start(bu_stage, d_bu2), "sdw")
        P.wait(POOL, "sdw", dw_bu)
        buc_sq = P.op(POOL, lambda e: e.tensor_copy(wu_ext[64:72, :], bu_stage),
                      "sq")


        def ln_stats_apply(kind, x_ready, want_nsum=False, psum_guard=None):
            for t in range(NT):
                if x_ready is not None and x_ready[t] is not None:
                    P.wait(VEC, x_ready[t][0], x_ready[t][1])
                P.op(VEC, lambda e, t=t: e.bn_stats(
                    bnbuf[:, 12 * t:12 * t + 6], xt(t)[:, 0:512]))
                P.op(VEC, lambda e, t=t: e.bn_stats(
                    bnbuf[:, 12 * t + 6:12 * t + 12], xt(t)[:, 512:1024]))
            P.drain(VEC)
            for t in range(NT):
                P.op(VEC, lambda e, t=t: e.bn_aggr(
                    stat_mv[:, 2 * t:2 * t + 2], bnbuf[:, 12 * t:12 * t + 12]))
            P.drain(VEC)
            sv1 = P.op(VEC, lambda e: e.tensor_scalar(
                sc_a[:], mv_var2, EPS, None, ALU.add), "sv")
            P.wait(SCA, "sv", sv1)
            P.op(SCA, lambda e: e.activation(sc_c[:], sc_a[:], AF.Ln))
            P.drain(SCA)
            sa1 = P.op(SCA, lambda e: e.activation(sc_b[:], sc_c[:], AF.Exp,
                                                   scale=-0.5), "sa")
            P.wait(VEC, "sa", sa1)
            if kind == "ffn":
                P.op(VEC, lambda e: e.tensor_tensor(sc_c[:], mv_var2, sc_b[:],
                                                    ALU.mult))
                P.drain(VEC)
                P.op(VEC, lambda e: e.tensor_tensor(sc_c[:], sc_c[:], sc_b[:],
                                                    ALU.mult))
                P.drain(VEC)
                sv2 = P.op(VEC, lambda e: e.tensor_scalar(
                    sc_c[:], sc_c[:], EPS, None, ALU.add), "sv")
                P.wait(SCA, "sv", sv2)
                P.op(SCA, lambda e: e.activation(sc_a[:], sc_c[:], AF.Ln))
                P.drain(SCA)
                sa2 = P.op(SCA, lambda e: e.activation(
                    sc_c[:], sc_a[:], AF.Exp, scale=-0.5), "sa")
                P.wait(VEC, "sa", sa2)
                P.op(VEC, lambda e: e.tensor_tensor(sc_b[:], sc_b[:], sc_c[:],
                                                    ALU.mult))
                P.drain(VEC)
            else:
                P.op(VEC, lambda e: e.tensor_copy(mu2_all[:], mv_mean2))
                P.op(VEC, lambda e: e.tensor_copy(r2_all[:], sc_b[:]))
                P.drain(VEC)
            nb_sv = P.op(VEC, lambda e: e.scalar_tensor_tensor(
                sc_c[:], mv_mean2, -1.0, sc_b[:], ALU.mult, ALU.mult), "sv")
            P.drain(VEC)
            zview = big_b.rearrange("p (c tt) -> p c tt", c=8)
            last = None
            nsum_sq = None
            ev_t = [None] * NT
            with contextlib.ExitStack() as ps:
                p_t2 = [ps.enter_context(nc.psum_tensor(
                    f"p_t2{k}_{P.cnt['sp']}", [128, 1024], BF16))
                    for k in range(2)]
                ev_hist = [None, None]
                if psum_guard is not None:
                    P.wait(TEN, psum_guard[0], psum_guard[1])
                P.wait(SCA, "sv", nb_sv)
                for t in range(NT):
                    ap_ms = P.op(SCA, lambda e, t=t: e.activation(
                        nz_t(t), xt(t), AF.Identity,
                        bias=sc_c[:, t:t + 1], scale=sc_b[:, t:t + 1]), "sa")
                    P.wait(TEN, "sa", ap_ms)
                    P.wait(TEN, "sv", ev_hist[t % 2])
                    ms = None
                    for c in range(NC):
                        ms = P.op(TEN, lambda e, t=t, c=c, pt=p_t2: e.transpose(
                            pt[t % 2][:, 128 * c:128 * (c + 1)],
                            big_a[:, D * t + 128 * c:D * t + 128 * (c + 1)],
                            identb[:]), "sp" if c == NC - 1 else None)
                    P.wait(VEC, "sp", ms)
                    last = P.op(VEC, lambda e, t=t, pt=p_t2: e.tensor_copy(
                        zview[:, :, 128 * t:128 * (t + 1)],
                        pt[t % 2][:].rearrange("p (c tt) -> p c tt", c=8)),
                        "sv")
                    ev_hist[t % 2] = last
                    ev_t[t] = last
            if want_nsum:
                P.wait(SCA, "sv", last)
                for c in range(4):
                    nsum_sq = P.op(SCA, lambda e, c=c: e.activation(
                        big_a[:, 2048 * c:2048 * (c + 1)], zT_c(c), AF.Identity,
                        scale=1.0, accum_out=nsum[:, c:c + 1]), "sa")
                P.drain(VEC)
                for c in range(4, NC):
                    P.op(VEC, lambda e, c=c: e.reduce_sum(
                        nsum[:, c:c + 1], zT_c(c), axis=AX.X))
            return last, nsum_sq

        ffn_state = {}

        def emit_ffn_back(nsum_sq, stage_guard=None, rt_guards=(),
                          with_n2=False):
            vp_ones_sv = P.op(POOL, lambda e: e.memset(p_buf[64:72, :], 1.0),
                              "sq")
            sel_sv = None

            P.wait(VEC, "sa", nsum_sq)
            P.wait(VEC, "sq", gs_sq)
            P.drain(VEC)
            P.op(VEC, lambda e: e.tensor_tensor(tmp8[:], nsum[:], gsS[:],
                                                ALU.mult))
            P.drain(VEC)
            zbn_sv = P.op(VEC, lambda e: e.tensor_tensor(
                tmp8[:], tmp8[:], cC[:], ALU.add), "sv")

            with contextlib.ExitStack() as ps:
                p_r = ps.enter_context(
                    nc.psum_tensor(f"p_r_{P.cnt['sp']}", [8, 1], F32))
                p_rt = ps.enter_context(
                    nc.psum_tensor(f"p_rt_{P.cnt['sp']}", [1, 8], F32))
                p_b1 = ps.enter_context(
                    nc.psum_tensor(f"p_b1_{P.cnt['sp']}", [128, 1], F32))
                p_b2 = ps.enter_context(
                    nc.psum_tensor(f"p_b2_{P.cnt['sp']}", [64, 1], F32))
                p_b3 = ps.enter_context(
                    nc.psum_tensor(f"p_b3_{P.cnt['sp']}", [128, 1], F32))
                p_hb = ps.enter_context(
                    nc.psum_tensor(f"p_hb_{P.cnt['sp']}", [64, 1], F32))

                for g_sem, g_val in rt_guards:
                    P.wait(TEN, g_sem, g_val)
                P.wait(TEN, "sv", zbn_sv)
                for c in range(NC):
                    P.op(TEN, lambda e, c=c: e.matmul(
                        p_r[:], wr2[:, 8 * c:8 * (c + 1)], tmp8[:, c:c + 1],
                        start=(c == 0), stop=False))
                r_ms = P.op(TEN, lambda e: e.matmul(
                    p_r[:], br_row, ones11, start=False, stop=True), "sp")
                P.wait(VEC, "sp", r_ms)
                r8_sv = P.op(VEC, lambda e: e.tensor_copy(r8[:], p_r[:]), "sv")
                P.wait(TEN, "sv", r8_sv)
                rt_ms = P.op(TEN, lambda e: e.transpose(
                    p_rt[:], r8[:], identf8[:]), "sp")
                P.wait(VEC, "sp", rt_ms)
                P.op(VEC, lambda e: e.tensor_copy(rrow[:], p_rt[:]))
                P.drain(VEC)
                P.op(VEC, lambda e: e.reduce_max(rmax[:], rrow[:], axis=AX.X))
                P.drain(VEC)
                nm_sv = P.op(VEC, lambda e: e.tensor_scalar(
                    rnmax[:], rmax[:], -1.0, None, ALU.mult), "sv")
                P.wait(SCA, "sv", nm_sv)
                ex_sa = P.op(SCA, lambda e: e.activation(
                    rex[:], rrow[:], AF.Exp, bias=rnmax[:], scale=1.0), "sa")
                P.wait(VEC, "sa", ex_sa)
                P.op(VEC, lambda e: e.reduce_sum(rsum[:], rex[:], axis=AX.X))
                P.op(VEC, lambda e: e.tensor_scalar(
                    rtmp[:], rex[:], 1.0, None, ALU.subtract))
                P.drain(VEC)
                P.op(VEC, lambda e: e.reciprocal(rrec[:], rsum[:]))
                P.op(VEC, lambda e: e.tensor_tensor(rtmp2[:], rtmp[:], rtmp[:],
                                                    ALU.mult))
                P.drain(VEC)
                P.op(VEC, lambda e: e.tensor_scalar(
                    rtmp2[:], rtmp2[:], EPSA, None, ALU.add))
                P.drain(VEC)
                P.op(VEC, lambda e: e.reciprocal(rtmp[:], rtmp2[:]))
                P.drain(VEC)
                mask_sv = P.op(VEC, lambda e: e.tensor_scalar(
                    rtmp[:], rtmp[:], EPSA, None, ALU.mult), "sv")
                P.drain(VEC)
                P.op(VEC, lambda e: e.tensor_tensor(rtmp2[:], rtmp[:], iota8f,
                                                    ALU.mult))
                P.drain(VEC)
                idx_sv = P.op(VEC, lambda e: e.reduce_sum(
                    idxf[:], rtmp2[:], axis=AX.X), "sv")
                P.wait(TEN, "sv", mask_sv)
                mt_ms = P.op(TEN, lambda e: e.transpose(
                    p_r[:], rtmp[:], identf8[0:1, 0:1]), "sp")
                P.wait(VEC, "sp", mt_ms)
                P.op(VEC, lambda e: e.tensor_copy(tmp8b[0:8, 0:1], p_r[:]))
                P.drain(VEC)
                P.wait(VEC, "sq", vp_ones_sv)
                sel_sv = P.op(VEC, lambda e: e.tensor_scalar(
                    p_buf[64:72, :], p_buf[64:72, :], tmp8b[0:8, 0:1], None,
                    ALU.mult), "sv")
                P.wait(TEN, "sv", idx_sv)
                b1_ms = P.op(TEN, lambda e: e.matmul(
                    p_b1[:], c128row, idxf[:], start=True, stop=True), "sp")
                b2_ms = P.op(TEN, lambda e: e.matmul(
                    p_b2[:], c64row, idxf[:], start=True, stop=True), "sp")
                b3_ms = P.op(TEN, lambda e: e.matmul(
                    p_b3[:], c05row, rrec[:], start=True, stop=True), "sp")
                P.wait(VEC, "sp", b3_ms)
                P.op(VEC, lambda e: e.tensor_copy(idxc[:], p_b1[:]))
                P.op(VEC, lambda e: e.tensor_copy(uidxc[:], p_b2[:]))
                P.op(VEC, lambda e: e.tensor_copy(sgcol[:], p_b3[:]))
                P.op(VEC, lambda e: e.tensor_copy(idxi1[:], idxf[:]))
                P.drain(VEC)
                P.op(VEC, lambda e: e.tensor_copy(idxi[:], idxc[:]))
                P.op(VEC, lambda e: e.tensor_copy(uidxi[:], uidxc[:]))
                P.drain(VEC)
                P.op(VEC, lambda e: e.tensor_tensor(widx[:], idxi[:], iotai[:],
                                                    ALU.add))
                wix_sv = P.op(VEC, lambda e: e.tensor_tensor(
                    uidx[:], uidxi[:], iotai[0:64, :], ALU.add), "sv")

                P.wait(POOL, "sv", wix_sv)
                if stage_guard is not None:
                    P.wait(POOL, stage_guard[0], stage_guard[1])
                P.dma(POOL, lambda e: e.indirect_dma_start(
                    wd_f32, None, d_wd2,
                    bass.IndirectOffsetOnAxis(ap=widx[:], axis=0)), "sdg")
                P.dma(POOL, lambda e: e.indirect_dma_start(
                    wu_f32, None, d_wu2,
                    bass.IndirectOffsetOnAxis(ap=uidx[:], axis=0)), "sdg")
                g_ms = P.dma(POOL, lambda e: e.indirect_dma_start(
                    bd_f[:], None, d_bd2,
                    bass.IndirectOffsetOnAxis(ap=uidx[:], axis=0)), "sdg")

                P.wait(VEC, "sdg", g_ms)
                for c in range(NC):
                    P.op(VEC, lambda e, c=c: e.tensor_scalar(
                        wd_bf[:, 64 * c:64 * (c + 1)],
                        wd_f32[:, 64 * c:64 * (c + 1)],
                        ffng2[:, c:c + 1], None, ALU.mult))
                wprep_sv = P.op(VEC, lambda e: e.tensor_copy(
                    wu_ext[0:64, :], wu_f32), "sv")
                P.wait(TEN, "sdg", g_ms)
                for c in range(NC):
                    P.op(TEN, lambda e, c=c: e.matmul(
                        p_hb[:], wd_f32[:, 64 * c:64 * (c + 1)],
                        ffnb2[:, c:c + 1], start=(c == 0), stop=(c == NC - 1)),
                        "sp" if c == NC - 1 else None)
                hb_ms = P.cnt["sp"]
                P.wait(VEC, "sp", hb_ms)
                hb_sv = P.op(VEC, lambda e: e.tensor_tensor(
                    hbias[:], p_hb[:], bd_f[:], ALU.add), "sv")
                ffn_state["stage_last_sv"] = wprep_sv
                ffn_state["stage_last_sp"] = hb_ms

            with contextlib.ExitStack() as psh:
                p_h = [psh.enter_context(nc.psum_tensor(
                    f"p_h{i}_{P.cnt['sp']}", [64, 512], F32)) for i in range(2)]
                P.wait(TEN, "sv", wprep_sv)
                P.wait(TEN, "sv", hb_sv)
                relu_ms = [None, None]
                for tc in range(4):
                    if relu_ms[tc % 2] is not None:
                        P.wait(TEN, "sa", relu_ms[tc % 2])
                    ms = None
                    for c in range(NC):
                        ms = P.op(TEN, lambda e, c=c, tc=tc: e.matmul(
                            p_h[tc % 2][:],
                            wd_bf[:, 64 * c:64 * (c + 1)],
                            zT_c(c)[:, 512 * tc:512 * (tc + 1)],
                            start=(c == 0), stop=(c == NC - 1)),
                            "sp" if c == NC - 1 else None)
                    P.wait(SCA, "sp", ms)
                    P.wait(SCA, "sv", hb_sv)
                    relu_ms[tc % 2] = P.op(SCA, lambda e, tc=tc: e.activation(
                        p_buf[0:64, 512 * tc:512 * (tc + 1)], p_h[tc % 2][:],
                        AF.Relu, bias=hbias[:], scale=1.0), "sa")
            with contextlib.ExitStack() as ps:
                p_y = [ps.enter_context(nc.psum_tensor(
                    f"p_yf{i}_{P.cnt['sp']}", [128, 1024], F32)) for i in range(2)]
                if with_n2:
                    p_t2n = [ps.enter_context(nc.psum_tensor(
                        f"p_t2n{k}_{P.cnt['sp']}", [128, 1024], BF16))
                        for k in range(2)]
                    p_qk2 = [ps.enter_context(nc.psum_tensor(
                        f"p_qk2{k}_{P.cnt['sp']}", [128, 512], F32))
                        for k in range(2)]
                    zview_n = big_b.rearrange("p (c tt) -> p c tt", c=8)
                    vpb = vprime
                    P.wait(SCA, "sq", eps_sq)
                    P.wait(SCA, "sq", vp_ones_sq)
                    P.wait(VEC, "sq", sv_const)
                    wplan = []
                    for dram_w, n_ch, ccols, dstw in (
                            (d_wqk, 8, 1024, wqk_bf), (d_wv, 8, 512, wv_bf),
                            (d_wout, 4, 1024, wout_bf)):
                        dr = dram_w.rearrange("(c p) f -> c p f", p=128)
                        for cch in range(n_ch):
                            wplan.append((dr, cch, ccols, dstw))
                    wdma_ms = [None] * len(wplan)
                    wcast_ms = [None] * len(wplan)
                    n_dma = [0]
                    n_cast = [0]

                    def emit_wdma():
                        c = n_dma[0]
                        if c >= len(wplan):
                            return
                        dr, cch, ccols, dstw = wplan[c]
                        sl_ = stage[:, 1024 * (c % 2):1024 * (c % 2) + ccols]
                        if c < 2:
                            P.wait(SYN, "sv", wprep_sv)
                            P.wait(SYN, "sv", hb_sv)
                            P.wait(SYN, "sp", ffn_state["stage_last_sp"])
                        else:
                            P.wait(SYN, "sq", wcast_ms[c - 2])
                        wdma_ms[c] = P.dma(SYN, lambda e, sl_=sl_, dr=dr,
                                           cch=cch: e.dma_start(sl_, dr[cch]),
                                           "sdl" if c % 2 == 0 else "sdlb")
                        n_dma[0] += 1

                    def emit_wcast():
                        c = n_cast[0]
                        if c >= len(wplan) or wdma_ms[c] is None:
                            return
                        dr, cch, ccols, dstw = wplan[c]
                        sl_ = stage[:, 1024 * (c % 2):1024 * (c % 2) + ccols]
                        P.wait(POOL, "sdl" if c % 2 == 0 else "sdlb",
                               wdma_ms[c])
                        wcast_ms[c] = P.op(POOL, lambda e, sl_=sl_, dstw=dstw,
                                           cch=cch, ccols=ccols: e.tensor_copy(
                            dstw[:, ccols * cch:ccols * (cch + 1)], sl_), "sq")
                        n_cast[0] += 1

                    emit_wdma()
                    emit_wdma()
                    qk_batches = [(tc, fc) for tc in range(4)
                                  for fc in range(8)]
                    qb_next = [0]
                    qk_ev2 = [None, None]
                    qk_started = [False]

                    def emit_qk_batch():
                        b = qb_next[0]
                        if b >= len(qk_batches):
                            return False
                        tc, fc = qk_batches[b]
                        if n2ev_t[4 * tc + 3] is None:
                            return False
                        if n_cast[0] < 8:
                            return False
                        if not qk_started[0]:
                            P.wait(TEN, "sq", wcast_ms[7])
                            qk_started[0] = True
                        sl_ = b % 2
                        P.wait(TEN, "sv", qk_ev2[sl_])
                        if fc == 0:
                            P.wait(TEN, "sv", n2ev_t[4 * tc + 3])
                        ms = None
                        for c in range(NC):
                            ms = P.op(TEN, lambda e, c=c, fc=fc, tc=tc,
                                      sl_=sl_: e.matmul(
                                p_qk2[sl_][:],
                                wqk_bf[:, D * c + 128 * fc:
                                       D * c + 128 * (fc + 1)],
                                zT_c(c)[:, 512 * tc:512 * (tc + 1)],
                                start=(c == 0), stop=(c == NC - 1),
                                skip_group_check=True),
                                "sp" if c == NC - 1 else None)
                        P.wait(VEC, "sp", ms)
                        if fc % 2 == 0:
                            qk_ev2[sl_] = P.op(VEC, lambda e, fc=fc, tc=tc,
                                               sl_=sl_: e.tensor_scalar(
                                qk_of(fc)[:, 512 * tc:512 * (tc + 1)],
                                p_qk2[sl_][:], 0.125, bqk2s[:, fc:fc + 1],
                                ALU.mult, ALU.add), "sv")
                        else:
                            qk_ev2[sl_] = P.op(VEC, lambda e, fc=fc, tc=tc,
                                               sl_=sl_: e.tensor_scalar(
                                qk_of(fc)[:, 512 * tc:512 * (tc + 1)],
                                p_qk2[sl_][:], 1.0, bqk2[:, fc:fc + 1],
                                ALU.mult, ALU.add), "sv")
                        qb_next[0] += 1
                        return True
                P.wait(TEN, "sa", relu_ms[0])
                P.wait(TEN, "sa", relu_ms[1])
                P.wait(TEN, "sq", buc_sq)
                P.wait(TEN, "sv", sel_sv)
                up_ev = [None, None]
                x_done = []
                n2ap = [None] * NT
                n2r1 = [None] * NT
                n2ag = [None] * NT
                n2nb = [None] * NT
                n2tr = [None] * NT
                n2ev_t = [None] * NT
                n2_last = None
                n_it = NT + 6 if with_n2 else NT
                for i in range(n_it):
                    if i < NT:
                        t = i
                        if up_ev[t % 2] is not None:
                            P.wait(TEN, "sv", up_ev[t % 2])
                        ms = None
                        for h in range(2):
                            ms = P.op(TEN, lambda e, t=t, h=h: e.matmul(
                                p_y[t % 2][:, 512 * h:512 * (h + 1)],
                                p_buf[0:72, 128 * t:128 * (t + 1)],
                                wu_ext[0:72, 512 * h:512 * (h + 1)],
                                start=True, stop=True, skip_group_check=True),
                                "sp" if h == 1 else None)
                    if with_n2 and 0 <= i - 5 < NT:
                        td = i - 5
                        P.wait(TEN, "sa", n2ap[td])
                        P.wait(TEN, "sv", n2ev_t[td - 2] if td >= 2 else None)
                        for c in range(NC):
                            n2tr[td] = P.op(TEN, lambda e, td=td, c=c:
                                            e.transpose(
                                p_t2n[td % 2][:, 128 * c:128 * (c + 1)],
                                vpb[:, 1024 * (td % 2) + 128 * c:
                                    1024 * (td % 2) + 128 * (c + 1)],
                                identb[:]), "sp" if c == NC - 1 else None)
                    if i < NT:
                        P.wait(VEC, "sp", ms)
                        up_ev[t % 2] = P.op(
                            VEC, lambda e, t=t: e.scalar_tensor_tensor(
                                xt(t), p_y[t % 2][:], sgcol[:], xt(t),
                                ALU.mult, ALU.add), "sv")
                        x_done.append(("sv", up_ev[t % 2]))
                    if with_n2:
                        P.drain(VEC)
                        if 0 <= i - 1 < NT:
                            td = i - 1
                            if td >= 2:
                                P.wait(VEC, "sv", n2ag[td - 2])
                            P.op(VEC, lambda e, td=td: e.bn_stats(
                                bnbuf[:, 12 * (td % 2):12 * (td % 2) + 6],
                                xt(td)[:, 0:512]))
                            P.op(VEC, lambda e, td=td: e.bn_stats(
                                bnbuf[:, 12 * (td % 2) + 6:12 * (td % 2) + 12],
                                xt(td)[:, 512:1024]))
                        if 0 <= i - 2 < NT:
                            td = i - 2
                            n2ag[td] = P.op(VEC, lambda e, td=td: e.bn_aggr(
                                stat_mv[:, 2 * td:2 * td + 2],
                                bnbuf[:, 12 * (td % 2):12 * (td % 2) + 12]),
                                "sv")
                        if 0 <= i - 4 < NT:
                            td = i - 4
                            P.wait(VEC, "sa", n2r1[td])
                            n2nb[td] = P.op(VEC, lambda e, td=td:
                                            e.scalar_tensor_tensor(
                                sc_c[:, td:td + 1], stat_mv[:, 2 * td:
                                                            2 * td + 1],
                                -1.0, r2_all[:, td:td + 1],
                                ALU.mult, ALU.mult), "sv")
                        if 0 <= i - 5 < NT:
                            td = i - 5
                            P.wait(VEC, "sp", n2tr[td])
                            n2_last = P.op(VEC, lambda e, td=td: e.tensor_copy(
                                zview_n[:, :, 128 * td:128 * (td + 1)],
                                p_t2n[td % 2][:].rearrange(
                                    "p (c tt) -> p c tt", c=8)), "sv")
                            n2ev_t[td] = n2_last
                        if 0 <= i - 3 < NT:
                            td = i - 3
                            P.wait(SCA, "sv", n2ag[td])
                            P.op(SCA, lambda e, td=td: e.activation(
                                sc_a[:, td:td + 1],
                                stat_mv[:, 2 * td + 1:2 * td + 2], AF.Ln,
                                bias=epscol[:], scale=1.0))
                        P.drain(SCA)
                        if 0 <= i - 3 < NT:
                            td = i - 3
                            n2r1[td] = P.op(SCA, lambda e, td=td: e.activation(
                                r2_all[:, td:td + 1], sc_a[:, td:td + 1],
                                AF.Exp, scale=-0.5), "sa")
                        if 0 <= i - 4 < NT:
                            td = i - 4
                            P.wait(SCA, "sv", n2nb[td])
                            if td >= 2:
                                P.wait(SCA, "sp", n2tr[td - 2])
                            n2ap[td] = P.op(SCA, lambda e, td=td: e.activation(
                                vpb[:, 1024 * (td % 2):1024 * (td % 2 + 1)],
                                xt(td), AF.Identity,
                                bias=sc_c[:, td:td + 1],
                                scale=r2_all[:, td:td + 1]), "sa")
                        emit_wcast()
                        emit_wdma()
                        emit_wcast()
                        emit_wdma()
                        if QK_IN_LOOP:
                            for _ in range(3):
                                if not emit_qk_batch():
                                    break
                if with_n2:
                    while n_cast[0] < len(wplan):
                        emit_wdma()
                        emit_wcast()
                    while emit_qk_batch():
                        pass
                    ffn_state["sq_wqk"] = wcast_ms[7]
                    ffn_state["sq_wv"] = wcast_ms[15]
                    ffn_state["sq_wout"] = wcast_ms[19]
                    ffn_state["qkproj_sv"] = max(
                        v for v in qk_ev2 if v is not None)
                    P.wait(POOL, "sv", n2_last)
                    ffn_state["vp2_sq"] = P.op(POOL, lambda e: e.memset(
                        vprime.rearrange("p (t h c) -> p t h c", t=16, h=8)
                        [:, :, :, 64:65], 1.0), "sq")
            return x_done, n2_last

        def emit_ffn(x_ready, stage_guard=None, psum_guard=None,
                     with_n2=False):
            zt_sv, nsum_sq = ln_stats_apply("ffn", x_ready, want_nsum=True,
                                            psum_guard=psum_guard)
            return emit_ffn_back(nsum_sq, stage_guard=stage_guard,
                                 with_n2=with_n2)

        x_done, n2_ev = emit_ffn(x_tile_dma, stage_guard=("sq", buc_sq),
                                 with_n2=True)

        sq_wqk = ffn_state["sq_wqk"]
        sq_wv = ffn_state["sq_wv"]
        sq_wout = ffn_state["sq_wout"]
        qkproj_sv = ffn_state["qkproj_sv"]
        vp2_sq = ffn_state["vp2_sq"]
        P.wait(POOL, "sq", sq_wout)
        ffn1_done_sv = x_done[NT - 1][1]
        P.wait(TEN, "sv", qkproj_sv)
        g_row = stage[0:1, 0:1024]
        b_row = stage[0:1, 1024:2048]
        P.dma(POOL, lambda e: e.dma_start(g_row, d_gamma_row), "sdw")
        dw_gb = P.dma(POOL, lambda e: e.dma_start(b_row, d_beta_row), "sdw")
        with nc.psum_tensor("p_bcg", [128, 512], F32) as p_bcg:
            P.wait(TEN, "sdw", dw_gb)
            for src_row, dst in ((g_row, gamC), (b_row, betC)):
                for half in range(2):
                    ms = P.op(TEN, lambda e, h=half, sr=src_row, pb=p_bcg:
                              e.matmul(
                        pb[:], crs2row, sr[:, 512 * h:512 * (h + 1)],
                        start=True, stop=True), "sp")
                    P.wait(VEC, "sp", ms)
                    ms_v = P.op(VEC, lambda e, h=half, d=dst, pb=p_bcg:
                                e.tensor_copy(
                        d[:, 512 * h:512 * (h + 1)], pb[:]), "sv")
                    P.wait(TEN, "sv", ms_v)
        sv_gb = P.cnt["sv"]
        P.wait(POOL, "sv", sv_gb)
        P.dma(POOL, lambda e: e.dma_start(g_row[:, 0:512], d_bv_row), "sdw")
        dw_bvo = P.dma(POOL, lambda e: e.dma_start(b_row, d_bout_row), "sdw")
        P.wait(VEC, "sdw", dw_bvo)
        P.op(VEC, lambda e: e.tensor_copy(bv_b[:], g_row[:, 0:512]))
        sv_rows = P.op(VEC, lambda e: e.tensor_copy(bout_b[:], b_row), "sv")

        with contextlib.ExitStack() as ps:
            p_v = [ps.enter_context(nc.psum_tensor(
                f"p_v{i}", [128, 512], F32)) for i in range(2)]
            p_q2 = ps.enter_context(nc.psum_tensor("p_q2", [1, 2048], F32))
            p_nc_ = ps.enter_context(nc.psum_tensor("p_nc", [128, 4], F32))
            stb = stage.bitcast(BF16)
            bsl = [stb[:, 0:2048], stb[:, 2048:4096]]
            cs_list = [None] * 8
            mx_list = [None] * 8
            P.wait(TEN, "sq", sq_wv)
            P.wait(TEN, "sv", qkproj_sv)
            P.wait(TEN, "sv", sv_rows)
            P.wait(VEC, "sq", vp2_sq)
            P.wait(VEC, "sq", sq_wout)
            P.wait(VEC, "sv", sv_rows)
            v_ev = [None, None]

            def emit_mx(fc):
                P.wait(VEC, "sp", cs_list[fc])
                mx_list[fc] = P.op(VEC, lambda e, fc=fc: e.reduce_max(
                    mqk8[:, fc:fc + 1], p_q2[:], axis=AX.X), "sv")

            def emit_vtile(t):
                sl = t % 2
                if v_ev[sl] is not None:
                    P.wait(TEN, "sv", v_ev[sl])
                for c in range(NC):
                    P.op(TEN, lambda e, c=c, t=t, sl=sl: e.matmul(
                        p_v[sl][:],
                        big_b[:, S * c + 128 * t:S * c + 128 * (t + 1)],
                        wv_bf[:, 512 * c:512 * (c + 1)],
                        start=(c == 0), stop=False))
                ms = P.op(TEN, lambda e, sl=sl: e.matmul(
                    p_v[sl][:], onesb_r[:], bv_b[:], start=False, stop=True),
                    "sp")
                P.wait(VEC, "sp", ms)
                v_ev[sl] = P.op(VEC, lambda e, t=t, sl=sl: e.tensor_copy(
                    vprime[:, 520 * t:520 * (t + 1)].rearrange(
                        "p (h c) -> p h c", h=8)[:, :, 0:64],
                    p_v[sl][:].rearrange("p (h c) -> p h c", h=8)), "sv")

            for fc in range(8):
                if fc >= 1:
                    emit_mx(fc - 1)
                if fc == 0:
                    P.wait(VEC, "sv", qkproj_sv)
                if fc >= 2:
                    P.wait(VEC, "sp", cs_list[fc - 2])
                s2_sv = P.op(VEC, lambda e, fc=fc: e.tensor_tensor(
                    bsl[fc % 2], qk_of(fc), qk_of(fc), ALU.mult), "sv")
                emit_vtile(2 * fc)
                emit_vtile(2 * fc + 1)
                P.wait(TEN, "sv", s2_sv)
                if fc >= 1:
                    P.wait(TEN, "sv", mx_list[fc - 1])
                ms = None
                for sp_ in range(4):
                    ms = P.op(TEN, lambda e, sp_=sp_, fc=fc: e.matmul(
                        p_q2[:, 512 * sp_:512 * (sp_ + 1)], onesb_c[:],
                        bsl[fc % 2][:, 512 * sp_:512 * (sp_ + 1)],
                        start=True, stop=True, skip_group_check=True),
                        "sp" if sp_ == 3 else None)
                cs_list[fc] = ms
            emit_mx(7)
            vproj_sv = max(v for v in v_ev if v is not None)
            P.drain(VEC)
            mq_sv = P.op(VEC, lambda e: e.tensor_copy(rtmp[:], mqk8[:]), "sv")
            P.wait(SCA, "sv", mq_sv)
            ln8_sa = P.op(SCA, lambda e: e.activation(rtmp2[:], rtmp[:], AF.Ln),
                          "sa")
            P.wait(VEC, "sa", ln8_sa)
            pr_view = rtmp2.rearrange("p (k two) -> p k two", two=2)
            sum_sv = P.op(VEC, lambda e: e.tensor_tensor(
                c4[:], pr_view[:, :, 0:1].rearrange("p k two -> p (k two)"),
                pr_view[:, :, 1:2].rearrange("p k two -> p (k two)"),
                ALU.add), "sv")
            P.wait(SCA, "sv", sum_sv)
            c4_sa = P.op(SCA, lambda e: e.activation(
                rtmp[0:1, 0:4], c4[:], AF.Exp, scale=0.5), "sa")
            P.wait(TEN, "sa", c4_sa)
            ms = P.op(TEN, lambda e: e.matmul(
                p_nc_[:], cn1row, rtmp[0:1, 0:4], start=True,
                stop=True), "sp")
            P.wait(VEC, "sp", ms)
            negc_sv = P.op(VEC, lambda e: e.tensor_copy(
                negc_bc[:], p_nc_[:]), "sv")

        with contextlib.ExitStack() as ps:
            reg = [ps.enter_context(nc.psum_tensor(
                f"p_reg{i}", [128, 1024], F32)) for i in range(2)]
            p_oA = ps.enter_context(nc.psum_tensor("p_oA", [65, 512], F32))
            p_oB = ps.enter_context(nc.psum_tensor("p_oB", [65, 512], F32))
            p_bc0 = ps.enter_context(nc.psum_tensor("p_bcst0", [64, 512], F32))
            p_bc1 = ps.enter_context(nc.psum_tensor("p_bcst1", [64, 512], F32))

            P.wait(TEN, "sv", max(vproj_sv, negc_sv))
            P.wait(TEN, "sq", vp2_sq)
            P.wait(SCA, "sv", negc_sv)
            P.wait(SCA, "sq", sq_wout)
            P.wait(VEC, "sq", sq_wout)
            reg_free = [None, None]
            pslot_free = [None, None, None]
            rowtmp = stage[0:1, 1536:2048]
            rowtmpB = wqk_bf.bitcast(F32)[0:1, 0:512]
            bcA_sb = stage[0:64, 1024:1536]
            bcB_sb = stage[64:128, 1024:1536]

            def vsl(j, h):
                return vprime[:, 520 * j + 65 * h:520 * j + 65 * h + 65]

            state = {"pend": None, "o_free": None, "bc_free": None}

            def onorm_act(pend):
                _, _, pv15 = pend
                P.wait(SCA, "sp", pv15)
                P.op(SCA, lambda e: e.activation(rowtmp, p_oA[64:65, :], AF.Ln))
                P.op(SCA, lambda e: e.activation(rowtmpB, p_oB[64:65, :],
                                                 AF.Ln))
                P.drain(SCA)
                P.op(SCA, lambda e: e.activation(recA, rowtmp, AF.Exp,
                                                 scale=-1.0))
                return P.op(SCA, lambda e: e.activation(
                    recB[:], rowtmpB, AF.Exp, scale=-1.0), "sa")

            def onorm_pe(pend, rec_ms):
                P.wait(TEN, "sa", rec_ms)
                P.wait(TEN, "sv", state["bc_free"])
                P.op(TEN, lambda e: e.matmul(
                    p_bc0[:], crows[0:1, 0:64], recA, start=True, stop=True))
                return P.op(TEN, lambda e: e.matmul(
                    p_bc1[:], crows[0:1, 0:64], recB[:], start=True,
                    stop=True), "sp")

            def onorm_vec(pend, bc_ms):
                pr, i, _ = pend
                P.wait(VEC, "sp", bc_ms)
                P.op(VEC, lambda e: e.tensor_copy(bcA_sb, p_bc0[:]))
                P.op(VEC, lambda e: e.tensor_copy(bcB_sb, p_bc1[:]))
                P.drain(VEC)
                P.op(VEC, lambda e, pr=pr, i=i: e.tensor_tensor(
                    ost_of(pr)[0:64, 512 * i:512 * (i + 1)], p_oA[0:64, :],
                    bcA_sb, ALU.mult))
                ev = P.op(VEC, lambda e, pr=pr, i=i: e.tensor_tensor(
                    ost_of(pr)[64:128, 512 * i:512 * (i + 1)],
                    p_oB[0:64, :], bcB_sb, ALU.mult), "sv")
                state["o_free"] = ev
                state["bc_free"] = ev

            blocks = [(i, pr) for i in range(4) for pr in range(4)]
            TOT = 16 * 16
            exp_hist = {}
            qk_hist = {}
            rec_pend = None
            for g in range(TOT + 3):
                if g >= TOT:
                    P.wait(TEN, "sa", exp_hist.get(g - 2))
                if g < TOT:
                    b, j = blocks[g // 16], g % 16
                    i, pr = b
                    sl = g % 2
                    P.wait(TEN, "sa", exp_hist.get(g - 2))
                    P.op(TEN, lambda e, pr=pr, i=i, j=j, sl=sl: e.matmul(
                        reg[sl][:, 0:512],
                        qk_of(2 * pr + 1)[0:64, 128 * j:128 * (j + 1)],
                        qk_of(2 * pr)[0:64, 512 * i:512 * (i + 1)],
                        start=True, stop=True, tile_position=(0, 0),
                        skip_group_check=True))
                    qk_hist[g] = P.op(TEN, lambda e, pr=pr, i=i, j=j, sl=sl:
                                      e.matmul(
                        reg[sl][:, 512:1024],
                        qk_of(2 * pr + 1)[64:128, 128 * j:128 * (j + 1)],
                        qk_of(2 * pr)[64:128, 512 * i:512 * (i + 1)],
                        start=True, stop=True, tile_position=(64, 0),
                        skip_group_check=True), "sp")
                if rec_pend is not None and g % 16 == 3:
                    pend, rec_ms = rec_pend
                    bc_ms = onorm_pe(pend, rec_ms)
                    onorm_vec(pend, bc_ms)
                    rec_pend = None
                pg = g - 3
                if pg >= 0:
                    pb, pj = blocks[pg // 16], pg % 16
                    pi, ppr = pb
                    hA, hB = 2 * ppr, 2 * ppr + 1
                    psl = pg % 3
                    if pj == 0:
                        P.wait(TEN, "sv", state["o_free"])
                    P.op(TEN, lambda e, pj=pj, hA=hA, psl=psl: e.matmul(
                        p_oA[:], vsl(pj, hA),
                        p_buf[:, 1024 * psl:1024 * psl + 512],
                        start=(pj == 0), stop=(pj == 15),
                        skip_group_check=True))
                    pv_ms = P.op(TEN, lambda e, pj=pj, hB=hB, psl=psl: e.matmul(
                        p_oB[:], vsl(pj, hB),
                        p_buf[:, 1024 * psl + 512:1024 * (psl + 1)],
                        start=(pj == 0), stop=(pj == 15),
                        skip_group_check=True), "sp")
                    pslot_free[psl] = pv_ms
                    if pj == 15:
                        pend = (ppr, pi, pv_ms)
                        rec_pend = (pend, onorm_act(pend))
                if g < TOT:
                    if pslot_free[g % 3] is not None and \
                            pslot_free[g % 3] > qk_hist[g]:
                        P.wait(SCA, "sp", pslot_free[g % 3])
                    else:
                        P.wait(SCA, "sp", qk_hist[g])
                        P.wait(SCA, "sp", pslot_free[g % 3])
                    exp_hist[g] = P.op(SCA, lambda e, pr=pr, sl=sl, es=(g % 3):
                                       e.activation(
                        p_buf[:, 1024 * es:1024 * (es + 1)], reg[sl][:],
                        AF.Exp, bias=negc_bc[:, pr:pr + 1], scale=1.0), "sa")
            pend, rec_ms = rec_pend
            bc_ms = onorm_pe(pend, rec_ms)
            onorm_vec(pend, bc_ms)
            ostack_sv = P.cnt["sv"]
            p4_sp_last = P.cnt["sp"]

        P.op(VEC, lambda e: e.tensor_scalar(
            alpha_a[:], r2_all[:], RS2, 1.0, ALU.mult, ALU.add))
        al_sv = P.op(VEC, lambda e: e.scalar_tensor_tensor(
            m2_a[:], mv_mean2, -RS2, r2_all[:], ALU.mult, ALU.mult), "sv")
        P.drain(VEC)

        xbt = vprime.bitcast(F32)

        def xa_of(k):
            return xbt[:, 1024 * k:1024 * (k + 1)]

        zview = big_b.rearrange("p (c tt) -> p c tt", c=8)
        with contextlib.ExitStack() as ps:
            p_y = [ps.enter_context(nc.psum_tensor(
                f"p_yo{i}", [128, 1024], F32)) for i in range(3)]
            p_t2f = [ps.enter_context(nc.psum_tensor(
                f"p_t2f{k}", [128, 1024], BF16)) for k in range(2)]
            P.wait(TEN, "sv", ostack_sv)
            P.wait(TEN, "sq", sq_wout)
            P.wait(SCA, "sv", ostack_sv)
            P.wait(SCA, "sp", p4_sp_last)
            P.wait(SCA, "sq", eps_sq)
            w1a_hist = [None] * NT
            fin_hist = [None] * NT
            yr_hist = [None] * NT
            ap_hist = [None] * NT
            ag_hist = [None] * NT
            x2ag_hist = [None] * NT
            lnb_hist = [None] * NT
            sum2_hist = [None] * NT
            sexp_hist = [None] * NT
            nb_hist = [None] * NT
            tr_hist = [None] * NT
            xb_pend = [None] * NT
            w1b_pend = [None] * NT
            ev_hist = [None, None]
            last_ev = None
            nacc_q = []
            nsum_sq2 = None

            wq_f = wqk_bf.bitcast(F32)

            def u4(k):
                return wq_f[:, 1024 * k:1024 * (k + 1)]

            def ymv(k):
                return stat_mv[:, 4 + 2 * k:6 + 2 * k]

            def x2m(td):
                return x2mv[:, 2 * td:2 * td + 1]

            def x2v(td):
                return x2mv[:, 2 * td + 1:2 * td + 2]

            def scol(buf, td):
                return buf[:, td:td + 1]

            for i in range(NT + 10):
                if i < NT:
                    t = i
                    sl = t % 3
                    if t >= 3:
                        P.wait(TEN, "sa", w1a_hist[t - 3])
                        P.wait(TEN, "sv", ag_hist[t - 3])
                    for h in range(2):
                        for pr in range(4):
                            P.op(TEN, lambda e, t=t, h=h, pr=pr, sl=sl:
                                 e.matmul(
                                p_y[sl][:, 512 * h:512 * (h + 1)],
                                ost_of(pr)[:, 128 * t:128 * (t + 1)],
                                wout_bf[:, D * pr + 512 * h:
                                        D * pr + 512 * (h + 1)],
                                start=(pr == 0), stop=False,
                                skip_group_check=True))
                        y_ms = P.op(TEN, lambda e, h=h, sl=sl: e.matmul(
                            p_y[sl][:, 512 * h:512 * (h + 1)], onesb_r[:],
                            bout_b[:, 512 * h:512 * (h + 1)], start=False,
                            stop=True, skip_group_check=True), "sp")
                if 0 <= i - 10 < NT:
                    td = i - 10
                    P.wait(TEN, "sa", ap_hist[td])
                    P.wait(TEN, "sv", ev_hist[td % 2])
                    for c in range(NC):
                        tr_hist[td] = P.op(TEN, lambda e, td=td, c=c:
                                           e.transpose(
                            p_t2f[td % 2][:, 128 * c:128 * (c + 1)],
                            big_a[:, D * td + 128 * c:D * td + 128 * (c + 1)],
                            identb[:]), "sp" if c == NC - 1 else None)
                if i < NT:
                    P.wait(VEC, "sp", y_ms)
                    if i >= 1:
                        P.wait(VEC, "sv", ag_hist[i - 1])
                    P.op(VEC, lambda e, sl=sl: e.bn_stats(
                        bnbuf[:, 0:6], p_y[sl][:, 0:512]))
                    P.op(VEC, lambda e, sl=sl: e.bn_stats(
                        bnbuf[:, 6:12], p_y[sl][:, 512:1024]))
                P.drain(VEC)
                if i < NT:
                    ag_hist[i] = P.op(VEC, lambda e, i=i: e.bn_aggr(
                        ymv(i % 2), bnbuf[:, 0:12]), "sv")
                if 0 <= i - 2 < NT:
                    td = i - 2
                    P.wait(VEC, "sq", xb_pend[td])
                    P.op(VEC, lambda e, td=td: e.scalar_tensor_tensor(
                        xa_of(td % 4), gamC[:], ynb[:, td % 2:td % 2 + 1],
                        xa_of(td % 4), ALU.mult, ALU.add))
                if 0 <= i - 3 < NT:
                    td = i - 3
                    P.wait(VEC, "sq", w1b_pend[td])
                    fin_hist[td] = P.op(VEC, lambda e, td=td: e.tensor_tensor(
                        xt(td), xa_of(td % 4), u4(td % 4), ALU.add), "sv")
                if 0 <= i - 4 < NT:
                    td = i - 4
                    if td >= 2:
                        P.wait(VEC, "sv", x2ag_hist[td - 2])
                    P.op(VEC, lambda e, td=td: e.bn_stats(
                        bnbuf[:, 12 + 12 * (td % 2):18 + 12 * (td % 2)],
                        xt(td)[:, 0:512]))
                    P.op(VEC, lambda e, td=td: e.bn_stats(
                        bnbuf[:, 18 + 12 * (td % 2):24 + 12 * (td % 2)],
                        xt(td)[:, 512:1024]))
                if 0 <= i - 5 < NT:
                    td = i - 5
                    x2ag_hist[td] = P.op(VEC, lambda e, td=td: e.bn_aggr(
                        x2mv[:, 2 * td:2 * td + 2],
                        bnbuf[:, 12 + 12 * (td % 2):24 + 12 * (td % 2)]), "sv")
                if 0 <= i - 8 < NT:
                    td = i - 8
                    P.wait(VEC, "sa", lnb_hist[td])
                    sum2_hist[td] = P.op(VEC, lambda e, td=td: e.tensor_tensor(
                        scol(sc_a, td), scol(sc_a, td), scol(sc_c, td),
                        ALU.add), "sv")
                if 0 <= i - 9 < NT:
                    td = i - 9
                    P.wait(VEC, "sa", sexp_hist[td])
                    nb_hist[td] = P.op(VEC, lambda e, td=td:
                                       e.scalar_tensor_tensor(
                        scol(sc_c, td), x2m(td), -1.0, scol(sc_b, td),
                        ALU.mult, ALU.mult), "sv")
                if 0 <= i - 1 < NT:
                    td = i - 1
                    P.wait(VEC, "sa", yr_hist[td])
                    P.op(VEC, lambda e, td=td: e.scalar_tensor_tensor(
                        ynb[:, td % 2:td % 2 + 1], ymv(td % 2)[:, 0:1], -1.0,
                        yrstd[:, td % 2:td % 2 + 1], ALU.mult, ALU.mult))
                if 0 <= i - 10 < NT:
                    td = i - 10
                    P.wait(VEC, "sp", tr_hist[td])
                    last_ev = P.op(VEC, lambda e, td=td: e.tensor_copy(
                        zview[:, :, 128 * td:128 * (td + 1)],
                        p_t2f[td % 2][:].rearrange("p (c tt) -> p c tt", c=8)),
                        "sv")
                    ev_hist[td % 2] = last_ev
                if i < NT:
                    t = i
                    P.wait(SCA, "sv", ag_hist[t])
                    P.op(SCA, lambda e, t=t: e.activation(
                        ylnv[:, t % 2:t % 2 + 1], ymv(t % 2)[:, 1:2], AF.Ln,
                        bias=epscol[:], scale=1.0))
                if 0 <= i - 6 < NT:
                    td = i - 6
                    P.wait(SCA, "sv", x2ag_hist[td])
                    P.op(SCA, lambda e, td=td: e.activation(
                        scol(sc_a, td), x2v(td), AF.Ln,
                        bias=epscol[:], scale=1.0))
                P.drain(SCA)
                if i < NT:
                    t = i
                    yr_hist[t] = P.op(SCA, lambda e, t=t: e.activation(
                        yrstd[:, t % 2:t % 2 + 1], ylnv[:, t % 2:t % 2 + 1],
                        AF.Exp, scale=-0.5), "sa")
                if 0 <= i - 6 < NT:
                    td = i - 6
                    P.op(SCA, lambda e, td=td: e.activation(
                        scol(sc_b, td), scol(sc_a, td), AF.Exp, scale=-1.0))
                if 0 <= i - 7 < NT:
                    td = i - 7
                    lnb_hist[td] = P.op(SCA, lambda e, td=td: e.activation(
                        scol(sc_c, td), x2v(td), AF.Ln,
                        bias=epscol[:], scale=scol(sc_b, td)), "sa")
                P.drain(SCA)
                if i < NT:
                    t = i
                    if t == 0:
                        P.wait(SCA, "sv", al_sv)
                    if t >= 4:
                        P.wait(SCA, "sv", fin_hist[t - 4])
                    xa_sa = P.op(SCA, lambda e, t=t: e.activation(
                        xa_of(t % 4), xt(t), AF.Identity,
                        bias=m2_a[:, t:t + 1], scale=alpha_a[:, t:t + 1]),
                        "sa")
                    w1a_hist[t] = P.op(SCA, lambda e, t=t: e.activation(
                        u4(t % 4), p_y[t % 3][:], AF.Identity,
                        scale=yrstd[:, t % 2:t % 2 + 1]), "sa")
                if 0 <= i - 8 < NT:
                    td = i - 8
                    P.wait(SCA, "sv", sum2_hist[td])
                    sexp_hist[td] = P.op(SCA, lambda e, td=td: e.activation(
                        scol(sc_b, td), scol(sc_a, td), AF.Exp, scale=-0.5),
                        "sa")
                if 0 <= i - 9 < NT:
                    td = i - 9
                    P.wait(SCA, "sv", nb_hist[td])
                    ap_hist[td] = P.op(SCA, lambda e, td=td: e.activation(
                        nz_t(td), xt(td), AF.Identity,
                        bias=scol(sc_c, td), scale=scol(sc_b, td)), "sa")
                if i < NT:
                    t = i
                    P.wait(POOL, "sa", xa_sa)
                    xb_pend[t] = P.op(POOL, lambda e, t=t: e.tensor_tensor(
                        xa_of(t % 4), xa_of(t % 4), betC[:], ALU.add), "sq")
                    P.wait(POOL, "sa", w1a_hist[t])
                    w1b_pend[t] = P.op(POOL, lambda e, t=t: e.tensor_tensor(
                        u4(t % 4), u4(t % 4), gamC[:], ALU.mult), "sq")
            P.wait(SCA, "sv", last_ev)
            for c in range(4):
                nsum_sq2 = P.op(SCA, lambda e, c=c: e.activation(
                    big_a[:, 2048 * c:2048 * (c + 1)], zT_c(c), AF.Identity,
                    scale=1.0, accum_out=nsum[:, c:c + 1]), "sa")
            P.drain(VEC)
            for c in range(4, NC):
                P.op(VEC, lambda e, c=c: e.reduce_sum(
                    nsum[:, c:c + 1], zT_c(c), axis=AX.X))

        w1a_last = w1a_hist[NT - 1]
        x_done3, _ = emit_ffn_back(nsum_sq2, stage_guard=("sa", w1a_last),
                                   rt_guards=[("sa", w1a_last)])

        yr_ = d_y.rearrange("(t p) d -> t p d", p=128)
        for t in range(NT):
            P.wait(SYN, x_done3[t][0], x_done3[t][1])
            P.dma(SYN, lambda e, t=t: e.dma_start(yr_[t], xt(t)), "sout")
        P.wait(SYN, "sout", P.cnt["sout"])

        block = E(nc.Block())

        @block.sync
        def _(eng):
            for fn in P.ops[SYN]:
                fn(eng)

        @block.tensor
        def _(eng):
            for fn in P.ops[TEN]:
                fn(eng)

        @block.vector
        def _(eng):
            for fn in P.ops[VEC]:
                fn(eng)

        @block.scalar
        def _(eng):
            for fn in P.ops[SCA]:
                fn(eng)

        @block.gpsimd
        def _(eng):
            for fn in P.ops[POOL]:
                fn(eng)

    return nc


def _host_inputs(inp):
    W_qk = np.asarray(inp["L0_attn_W_qk"])
    b_qk = np.asarray(inp["L0_attn_b_qk"])
    perm = np.empty(1024, dtype=np.int64)
    for p in range(4):
        for hi in range(2):
            h = 2 * p + hi
            d = np.arange(64)
            perm[(2 * p) * 128 + hi * 64 + d] = h * 128 + 2 * d
            perm[(2 * p + 1) * 128 + hi * 64 + d] = h * 128 + 2 * d + 1
    wqk = np.ascontiguousarray(W_qk[:, perm])
    bqkp = np.ascontiguousarray(b_qk[perm])

    def chunk2(v):
        return np.ascontiguousarray(np.asarray(v).reshape(8, 128).T)

    com = {
        "wqk": wqk,
        "wv": np.ascontiguousarray(np.asarray(inp["L0_attn_W_v"])),
        "wout": np.ascontiguousarray(np.asarray(inp["L0_attn_W_out"])),
        "bv_row": np.asarray(inp["L0_attn_b_v"]).reshape(1, 512).copy(),
        "bout_row": np.asarray(inp["L0_attn_b_out"]).reshape(1, 1024).copy(),
        "gamma_row": np.asarray(inp["L0_attn_ln_gamma"]).reshape(1, 1024).copy(),
        "beta_row": np.asarray(inp["L0_attn_ln_beta"]).reshape(1, 1024).copy(),
        "spk": np.concatenate([
            chunk2(bqkp), chunk2(inp["L0_ffn_ln_g"]),
            chunk2(inp["L0_ffn_ln_b"]), chunk2(inp["L0_ffn_bn_mean"]),
            chunk2(inp["L0_ffn_bn_var"]), chunk2(inp["L0_ffn_bn_b"]),
            chunk2(inp["L0_ffn_bn_g"]),
            np.ascontiguousarray(np.asarray(inp["L0_ffn_Wr"]).reshape(8, 128, 8)
                                 .transpose(1, 0, 2).reshape(128, 64)),
        ], axis=1).astype(np.float32),
        "wd2": np.ascontiguousarray(
            np.asarray(inp["L0_ffn_W_down"]).reshape(8, 8, 128, 64)
            .transpose(0, 2, 1, 3).reshape(1024, 512)),
        "wu2": np.ascontiguousarray(
            np.asarray(inp["L0_ffn_W_up"]).reshape(512, 1024)),
        "bd2": np.asarray(inp["L0_ffn_b_down"]).reshape(512, 1).copy(),
        "bu2": np.ascontiguousarray(np.asarray(inp["L0_ffn_b_up"])),
        "identb": np.eye(128, dtype=np.float32).astype(ml_dtypes.bfloat16),
        "identf8": np.eye(8, dtype=np.float32),
        "crows": np.concatenate([
            np.full(128, 1.0, np.float32), np.full(128, -1.0, np.float32),
            np.full(128, 128.0, np.float32), np.full(64, 64.0, np.float32),
            np.full(128, 0.5, np.float32), np.full(128, RS2, np.float32),
            np.arange(8, dtype=np.float32), np.full(8, 1.0, np.float32),
            np.asarray(inp["L0_ffn_br"], dtype=np.float32).reshape(8),
            np.full(40, 1.0, np.float32),
        ]).reshape(1, 768),
        "onesb_r": np.ones((1, 128), np.float32).astype(ml_dtypes.bfloat16),
        "onesb_c": np.ones((128, 1), np.float32).astype(ml_dtypes.bfloat16),
        "iotai": np.arange(128, dtype=np.int32).reshape(128, 1),
    }
    for k, v in com.items():
        if v.dtype == np.float64:
            com[k] = v.astype(np.float32)
    return com


def kernel(**inputs):
    x = np.ascontiguousarray(np.asarray(inputs["L0_attn_x"], dtype=np.float32))
    com = _host_inputs(inputs)
    nc = build_program()
    in_maps = []
    for b in range(NCORE):
        m = dict(com)
        m["x"] = np.ascontiguousarray(x[b])
        in_maps.append(m)
    res = run_bass_kernel_spmd(nc, in_maps, core_ids=list(range(NCORE)))
    out = np.stack([np.asarray(res.results[b]["y"]) for b in range(NCORE)],
                   axis=0)
    return out.astype(np.float32)


# version 30
# speedup vs baseline: 1.0509x; 1.0244x over previous
import contextlib
from collections import defaultdict

import numpy as np
import ml_dtypes
import concourse.bass as bass
import concourse.mybir as mybir
from concourse.bass_utils import run_bass_kernel_spmd

F32 = mybir.dt.float32
BF16 = mybir.dt.bfloat16
I32 = mybir.dt.int32
AF = mybir.ActivationFunctionType
ALU = mybir.AluOpType
AX = mybir.AxisListType

NCORE = 8
S = 2048
D = 1024
NT = 16
QK_IN_LOOP = True
NC = 8
EPS = 1e-5
RS2 = 2.0 ** (-0.5)
EPSA = 1e-30


class Prog:

    def __init__(self, sems):
        self.ops = defaultdict(list)
        self.cnt = defaultdict(int)
        self.sems = sems

    def emit(self, eng, fn):
        self.ops[eng].append(fn)

    def op(self, eng, fn, sem=None):
        if sem is None:
            self.emit(eng, fn)
            return None
        self.cnt[sem] += 1
        s = self.sems[sem]

        def run(e, fn=fn, s=s):
            fn(e).then_inc(s, 1)
        self.emit(eng, run)
        return self.cnt[sem]

    def dma(self, eng, fn, sem):
        self.cnt[sem] += 16
        s = self.sems[sem]

        def run(e, fn=fn, s=s):
            fn(e).then_inc(s, 16)
        self.emit(eng, run)
        return self.cnt[sem]

    def wait(self, eng, sem, value):
        if value is None:
            return
        s = self.sems[sem]
        self.emit(eng, lambda e, s=s, v=value: e.wait_ge(s, v))

    def drain(self, eng):
        self.emit(eng, lambda e: e.drain())


def build_program():
    nc = bass.Bass("TRN2", target_bir_lowering=False)

    d_x = nc.dram_tensor("x", [S, D], F32, kind="ExternalInput").ap()
    d_wqk = nc.dram_tensor("wqk", [D, D], F32, kind="ExternalInput").ap()
    d_wv = nc.dram_tensor("wv", [D, 512], F32, kind="ExternalInput").ap()
    d_wout = nc.dram_tensor("wout", [512, D], F32, kind="ExternalInput").ap()
    d_spk = nc.dram_tensor("spk", [128, 120], F32, kind="ExternalInput").ap()
    d_bv_row = nc.dram_tensor("bv_row", [1, 512], F32, kind="ExternalInput").ap()
    d_bout_row = nc.dram_tensor("bout_row", [1, D], F32, kind="ExternalInput").ap()
    d_gamma_row = nc.dram_tensor("gamma_row", [1, D], F32, kind="ExternalInput").ap()
    d_beta_row = nc.dram_tensor("beta_row", [1, D], F32, kind="ExternalInput").ap()
    d_wd2 = nc.dram_tensor("wd2", [1024, 512], F32, kind="ExternalInput").ap()
    d_wu2 = nc.dram_tensor("wu2", [512, 1024], F32, kind="ExternalInput").ap()
    d_bd2 = nc.dram_tensor("bd2", [512, 1], F32, kind="ExternalInput").ap()
    d_bu2 = nc.dram_tensor("bu2", [8, 1024], F32, kind="ExternalInput").ap()
    d_identb = nc.dram_tensor("identb", [128, 128], BF16, kind="ExternalInput").ap()
    d_identf8 = nc.dram_tensor("identf8", [8, 8], F32, kind="ExternalInput").ap()
    d_crows = nc.dram_tensor("crows", [1, 768], F32, kind="ExternalInput").ap()
    d_onesb_r = nc.dram_tensor("onesb_r", [1, 128], BF16, kind="ExternalInput").ap()
    d_onesb_c = nc.dram_tensor("onesb_c", [128, 1], BF16, kind="ExternalInput").ap()
    d_iotai = nc.dram_tensor("iotai", [128, 1], I32, kind="ExternalInput").ap()
    d_y = nc.dram_tensor("y", [S, D], F32, kind="ExternalOutput").ap()

    st = contextlib.ExitStack()
    with st:
        E = st.enter_context
        x_sb = E(nc.sbuf_tensor("sb_x_sb", [128, NT * D], F32))
        big_a = E(nc.sbuf_tensor("sb_big_a", [128, NT * D], BF16))
        big_b = E(nc.sbuf_tensor("sb_big_b", [128, NT * D], BF16))
        vprime = E(nc.sbuf_tensor("sb_vprime", [128, 16 * 520], BF16))
        p_buf = E(nc.sbuf_tensor("sb_p_buf", [128, 3072], BF16))
        hT = p_buf[0:65, :]
        wqk_bf = E(nc.sbuf_tensor("sb_wqk_bf", [128, 8 * D], BF16))
        wv_bf = E(nc.sbuf_tensor("sb_wv_bf", [128, 8 * 512], BF16))
        wout_bf = E(nc.sbuf_tensor("sb_wout_bf", [128, 4 * D], BF16))
        stage = E(nc.sbuf_tensor("sb_stage", [128, 2048], F32))
        gamC = E(nc.sbuf_tensor("sb_gamC", [128, D], BF16))
        betC = E(nc.sbuf_tensor("sb_betC", [128, D], BF16))
        wd_bf = E(nc.sbuf_tensor("sb_wd_bf", [128, 512], BF16))
        wu_ext = E(nc.sbuf_tensor("sb_wu_ext", [72, 1024], BF16))
        bv_b = E(nc.sbuf_tensor("sb_bv_b", [1, 512], BF16))
        bout_b = E(nc.sbuf_tensor("sb_bout_b", [1, D], BF16))
        spk = E(nc.sbuf_tensor("sb_spk", [128, 120], F32))
        bqk2 = spk[:, 0:8]
        ffng2 = spk[:, 8:16]
        ffnb2 = spk[:, 16:24]
        bnm2 = spk[:, 24:32]
        bnv2 = spk[:, 32:40]
        bnb2 = spk[:, 40:48]
        bnsc2 = spk[:, 48:56]
        wr2 = spk[:, 56:120]
        bqk2s = E(nc.sbuf_tensor("sb_bqk2s", [128, 8], F32))
        tmp8 = E(nc.sbuf_tensor("sb_tmp8", [128, 8], F32))
        tmp8b = E(nc.sbuf_tensor("sb_tmp8b", [128, 8], F32))
        bnbuf = E(nc.sbuf_tensor("sb_bnbuf", [128, 12 * NT], F32))
        stat_mv = E(nc.sbuf_tensor("sb_stat_mv", [128, 2 * NT], F32))
        sc_a = E(nc.sbuf_tensor("sb_sc_a", [128, NT], F32))
        sc_b = E(nc.sbuf_tensor("sb_sc_b", [128, NT], F32))
        sc_c = E(nc.sbuf_tensor("sb_sc_c", [128, NT], F32))
        mu2_all = E(nc.sbuf_tensor("sb_mu2_all", [128, NT], F32))
        r2_all = E(nc.sbuf_tensor("sb_r2_all", [128, NT], F32))
        alpha_a = E(nc.sbuf_tensor("sb_alpha_a", [128, NT], F32))
        m2_a = E(nc.sbuf_tensor("sb_m2_a", [128, NT], F32))
        nsum = E(nc.sbuf_tensor("sb_nsum", [128, 8], F32))
        r8 = E(nc.sbuf_tensor("sb_r8", [8, 1], F32))
        rrow = E(nc.sbuf_tensor("sb_rrow", [1, 8], F32))
        rex = E(nc.sbuf_tensor("sb_rex", [1, 8], F32))
        rtmp = E(nc.sbuf_tensor("sb_rtmp", [1, 8], F32))
        rtmp2 = E(nc.sbuf_tensor("sb_rtmp2", [1, 8], F32))
        rmax = E(nc.sbuf_tensor("sb_rmax", [1, 1], F32))
        rnmax = E(nc.sbuf_tensor("sb_rnmax", [1, 1], F32))
        rsum = E(nc.sbuf_tensor("sb_rsum", [1, 1], F32))
        rrec = E(nc.sbuf_tensor("sb_rrec", [1, 1], F32))
        gate1 = E(nc.sbuf_tensor("sb_gate1", [1, 1], F32))
        idxf = E(nc.sbuf_tensor("sb_idxf", [1, 1], F32))
        idxi1 = E(nc.sbuf_tensor("sb_idxi1", [1, 1], I32))
        idxc = E(nc.sbuf_tensor("sb_idxc", [128, 1], F32))
        idxi = E(nc.sbuf_tensor("sb_idxi", [128, 1], I32))
        widx = E(nc.sbuf_tensor("sb_widx", [128, 1], I32))
        uidxc = E(nc.sbuf_tensor("sb_uidxc", [64, 1], F32))
        uidxi = E(nc.sbuf_tensor("sb_uidxi", [64, 1], I32))
        uidx = E(nc.sbuf_tensor("sb_uidx", [64, 1], I32))
        sgcol = E(nc.sbuf_tensor("sb_sgcol", [128, 1], F32))
        bd_f = E(nc.sbuf_tensor("sb_bd_f", [64, 1], F32))
        hbias = E(nc.sbuf_tensor("sb_hbias", [64, 1], F32))
        mqk8 = E(nc.sbuf_tensor("sb_mqk8", [1, 8], F32))
        c4 = E(nc.sbuf_tensor("sb_c4", [1, 4], F32))
        negc_bc = E(nc.sbuf_tensor("sb_negc_bc", [128, 4], F32))
        yve = E(nc.sbuf_tensor("sb_yve", [128, 2], F32))
        x2mv = E(nc.sbuf_tensor("sb_x2mv", [128, 2 * NT], F32))
        epscol = E(nc.sbuf_tensor("sb_epscol", [128, 1], F32))
        gsS = E(nc.sbuf_tensor("sb_gsS", [128, 8], F32))
        cC = E(nc.sbuf_tensor("sb_cC", [128, 8], F32))
        npart = E(nc.sbuf_tensor("sb_npart", [128, 32], F32))
        ylnv = E(nc.sbuf_tensor("sb_ylnv", [128, 2], F32))
        yrstd = E(nc.sbuf_tensor("sb_yrstd", [128, 2], F32))
        ynb = E(nc.sbuf_tensor("sb_ynb", [128, 2], F32))
        identb = E(nc.sbuf_tensor("sb_identb", [128, 128], BF16))
        identf8 = E(nc.sbuf_tensor("sb_identf8", [8, 8], F32))
        crows = E(nc.sbuf_tensor("sb_crows", [1, 768], F32))
        c1row = crows[0:1, 0:128]
        cn1row = crows[0:1, 128:256]
        c128row = crows[0:1, 256:384]
        c64row = crows[0:1, 384:448]
        c05row = crows[0:1, 448:576]
        crs2row = crows[0:1, 576:704]
        onesb_r = E(nc.sbuf_tensor("sb_onesb_r", [1, 128], BF16))
        onesb_c = E(nc.sbuf_tensor("sb_onesb_c", [128, 1], BF16))
        iotai = E(nc.sbuf_tensor("sb_iotai", [128, 1], I32))
        iota8f = crows[0:1, 704:712]
        br_row = crows[0:1, 720:728]
        ones11 = crows[0:1, 712:713]

        sems = {}
        for name in ["sdx", "sdxb", "sdx2", "sdx2b", "sdw", "sdl", "sdlb",
                     "sdg", "sout", "sp", "sv", "sa", "sq"]:
            sems[name] = E(nc.semaphore(name))

        P = Prog(sems)
        TEN, VEC, SCA, POOL, SYN = "tensor", "vector", "scalar", "gpsimd", "sync"

        def xt(t):
            return x_sb[:, D * t:D * (t + 1)]

        def nz_t(t):
            return big_a[:, D * t:D * (t + 1)]

        def zT_c(c):
            return big_b[:, S * c:S * (c + 1)]

        def qk_of(fc):
            return big_a[:, 2048 * fc:2048 * (fc + 1)]

        def ost_of(pr):
            return big_b[:, 2048 * pr:2048 * (pr + 1)]

        wd_f32 = stage[:, 0:512]
        wu_f32 = stage[0:64, 512:1536]
        bu_row = stage[64:65, 512:1536]

        def u_of(k):
            return stage[:, 1024 * k:1024 * (k + 1)]

        recA = stage[0:1, 0:512]
        recB = stage[0:1, 512:1024]
        bcAB = stage[0:64, 1024:1536]

        mv_mean = stat_mv.rearrange("p (t k) -> p t k", k=2)[:, :, 0:1]
        mv_var = stat_mv.rearrange("p (t k) -> p t k", k=2)[:, :, 1:2]
        mv_mean2 = mv_mean.rearrange("p t k -> p (t k)")
        mv_var2 = mv_var.rearrange("p t k -> p (t k)")

        xr = d_x.rearrange("(t p) d -> t p d", p=128)
        for sbuf, dram in [
            (spk[:], d_spk), (identb[:], d_identb), (identf8[:], d_identf8),
            (crows[:], d_crows), (onesb_r[:], d_onesb_r), (onesb_c[:], d_onesb_c),
            (iotai[:], d_iotai),
        ]:
            P.dma(POOL, lambda e, sb=sbuf, dr=dram: e.dma_start(sb, dr), "sdw")
        dw_small = P.cnt["sdw"]
        x_tile_dma = [None] * NT
        for grp, (eng, sems2) in enumerate((
                (SYN, ("sdx", "sdxb")), (POOL, ("sdx2", "sdx2b")))):
            for half in range(2):
                sname = sems2[half]
                tiles = [t for t in range(NT)
                         if t % 2 == grp and (t >= 8) == bool(half)]
                full = None
                for t in tiles:
                    full = P.dma(eng, lambda e, t=t: e.dma_start(
                        xt(t), xr[t]), sname)
                for t in tiles:
                    x_tile_dma[t] = (sname, full)
        P.wait(TEN, "sdw", dw_small)
        P.wait(VEC, "sdw", dw_small)
        P.wait(SCA, "sdw", dw_small)



        vp_ones_sq = P.op(POOL, lambda e: e.memset(
            vprime.rearrange("p (t h c) -> p t h c", t=16, h=8)[:, :, :, 64:65],
            1.0), "sq")
        eps_sq = P.op(POOL, lambda e: e.memset(epscol[:], EPS), "sq")

        P.wait(POOL, "sdw", dw_small)
        P.op(POOL, lambda e: e.tensor_scalar(bqk2s[:], bqk2, 0.125, None,
                                             ALU.mult))
        P.drain(POOL)
        sv_ve = P.op(POOL, lambda e: e.tensor_scalar(tmp8[:], bnv2, EPS, None,
                                                     ALU.add), "sq")
        P.wait(SCA, "sq", sv_ve)
        P.op(SCA, lambda e: e.activation(tmp8b[:], tmp8[:], AF.Ln))
        P.drain(SCA)
        sa_bnr = P.op(SCA, lambda e: e.activation(tmp8[:], tmp8b[:], AF.Exp,
                                                  scale=-0.5), "sa")
        P.wait(POOL, "sa", sa_bnr)
        sv_const = P.op(POOL, lambda e: e.tensor_tensor(bnsc2, bnsc2,
                                                        tmp8[:], ALU.mult),
                        "sq")
        P.drain(POOL)
        P.op(POOL, lambda e: e.tensor_tensor(gsS[:], ffng2, bnsc2, ALU.mult))
        P.op(POOL, lambda e: e.tensor_tensor(cC[:], ffnb2, bnm2, ALU.subtract))
        P.drain(POOL)
        P.op(POOL, lambda e: e.tensor_scalar(gsS[:], gsS[:], 1.0 / S, None,
                                             ALU.mult))
        P.op(POOL, lambda e: e.tensor_tensor(cC[:], cC[:], bnsc2, ALU.mult))
        P.drain(POOL)
        gs_sq = P.op(POOL, lambda e: e.tensor_tensor(cC[:], cC[:], bnb2,
                                                     ALU.add), "sq")

        bu_stage = stage[0:8, 0:1024]
        dw_bu = P.dma(POOL, lambda e: e.dma_start(bu_stage, d_bu2), "sdw")
        P.wait(POOL, "sdw", dw_bu)
        buc_sq = P.op(POOL, lambda e: e.tensor_copy(wu_ext[64:72, :], bu_stage),
                      "sq")


        def ln_stats_apply(kind, x_ready, want_nsum=False, psum_guard=None):
            for t in range(NT):
                if x_ready is not None and x_ready[t] is not None:
                    P.wait(VEC, x_ready[t][0], x_ready[t][1])
                P.op(VEC, lambda e, t=t: e.bn_stats(
                    bnbuf[:, 12 * t:12 * t + 6], xt(t)[:, 0:512]))
                P.op(VEC, lambda e, t=t: e.bn_stats(
                    bnbuf[:, 12 * t + 6:12 * t + 12], xt(t)[:, 512:1024]))
            P.drain(VEC)
            for t in range(NT):
                P.op(VEC, lambda e, t=t: e.bn_aggr(
                    stat_mv[:, 2 * t:2 * t + 2], bnbuf[:, 12 * t:12 * t + 12]))
            P.drain(VEC)
            sv1 = P.op(VEC, lambda e: e.tensor_scalar(
                sc_a[:], mv_var2, EPS, None, ALU.add), "sv")
            P.wait(SCA, "sv", sv1)
            P.op(SCA, lambda e: e.activation(sc_c[:], sc_a[:], AF.Ln))
            P.drain(SCA)
            sa1 = P.op(SCA, lambda e: e.activation(sc_b[:], sc_c[:], AF.Exp,
                                                   scale=-0.5), "sa")
            P.wait(VEC, "sa", sa1)
            if kind == "ffn":
                P.op(VEC, lambda e: e.tensor_tensor(sc_c[:], mv_var2, sc_b[:],
                                                    ALU.mult))
                P.drain(VEC)
                P.op(VEC, lambda e: e.tensor_tensor(sc_c[:], sc_c[:], sc_b[:],
                                                    ALU.mult))
                P.drain(VEC)
                sv2 = P.op(VEC, lambda e: e.tensor_scalar(
                    sc_c[:], sc_c[:], EPS, None, ALU.add), "sv")
                P.wait(SCA, "sv", sv2)
                P.op(SCA, lambda e: e.activation(sc_a[:], sc_c[:], AF.Ln))
                P.drain(SCA)
                sa2 = P.op(SCA, lambda e: e.activation(
                    sc_c[:], sc_a[:], AF.Exp, scale=-0.5), "sa")
                P.wait(VEC, "sa", sa2)
                P.op(VEC, lambda e: e.tensor_tensor(sc_b[:], sc_b[:], sc_c[:],
                                                    ALU.mult))
                P.drain(VEC)
            else:
                P.op(VEC, lambda e: e.tensor_copy(mu2_all[:], mv_mean2))
                P.op(VEC, lambda e: e.tensor_copy(r2_all[:], sc_b[:]))
                P.drain(VEC)
            nb_sv = P.op(VEC, lambda e: e.scalar_tensor_tensor(
                sc_c[:], mv_mean2, -1.0, sc_b[:], ALU.mult, ALU.mult), "sv")
            P.drain(VEC)
            zview = big_b.rearrange("p (c tt) -> p c tt", c=8)
            last = None
            nsum_sq = None
            ev_t = [None] * NT
            with contextlib.ExitStack() as ps:
                p_t2 = [ps.enter_context(nc.psum_tensor(
                    f"p_t2{k}_{P.cnt['sp']}", [128, 1024], BF16))
                    for k in range(2)]
                ev_hist = [None, None]
                if psum_guard is not None:
                    P.wait(TEN, psum_guard[0], psum_guard[1])
                P.wait(SCA, "sv", nb_sv)
                for t in range(NT):
                    ap_ms = P.op(SCA, lambda e, t=t: e.activation(
                        nz_t(t), xt(t), AF.Identity,
                        bias=sc_c[:, t:t + 1], scale=sc_b[:, t:t + 1]), "sa")
                    P.wait(TEN, "sa", ap_ms)
                    P.wait(TEN, "sv", ev_hist[t % 2])
                    ms = None
                    for c in range(NC):
                        ms = P.op(TEN, lambda e, t=t, c=c, pt=p_t2: e.transpose(
                            pt[t % 2][:, 128 * c:128 * (c + 1)],
                            big_a[:, D * t + 128 * c:D * t + 128 * (c + 1)],
                            identb[:]), "sp" if c == NC - 1 else None)
                    P.wait(VEC, "sp", ms)
                    last = P.op(VEC, lambda e, t=t, pt=p_t2: e.tensor_copy(
                        zview[:, :, 128 * t:128 * (t + 1)],
                        pt[t % 2][:].rearrange("p (c tt) -> p c tt", c=8)),
                        "sv")
                    ev_hist[t % 2] = last
                    ev_t[t] = last
            if want_nsum:
                P.wait(SCA, "sv", last)
                for c in range(4):
                    nsum_sq = P.op(SCA, lambda e, c=c: e.activation(
                        big_a[:, 2048 * c:2048 * (c + 1)], zT_c(c), AF.Identity,
                        scale=1.0, accum_out=nsum[:, c:c + 1]), "sa")
                P.drain(VEC)
                for c in range(4, NC):
                    P.op(VEC, lambda e, c=c: e.reduce_sum(
                        nsum[:, c:c + 1], zT_c(c), axis=AX.X))
            return last, nsum_sq

        ffn_state = {}

        def emit_ffn_back(nsum_sq, stage_guard=None, rt_guards=(),
                          with_n2=False):
            vp_ones_sv = P.op(POOL, lambda e: e.memset(p_buf[64:72, :], 1.0),
                              "sq")
            sel_sv = None

            P.wait(VEC, "sa", nsum_sq)
            P.wait(VEC, "sq", gs_sq)
            P.drain(VEC)
            P.op(VEC, lambda e: e.tensor_tensor(tmp8[:], nsum[:], gsS[:],
                                                ALU.mult))
            P.drain(VEC)
            zbn_sv = P.op(VEC, lambda e: e.tensor_tensor(
                tmp8[:], tmp8[:], cC[:], ALU.add), "sv")

            with contextlib.ExitStack() as ps:
                p_r = ps.enter_context(
                    nc.psum_tensor(f"p_r_{P.cnt['sp']}", [8, 1], F32))
                p_rt = ps.enter_context(
                    nc.psum_tensor(f"p_rt_{P.cnt['sp']}", [1, 8], F32))
                p_b1 = ps.enter_context(
                    nc.psum_tensor(f"p_b1_{P.cnt['sp']}", [128, 1], F32))
                p_b2 = ps.enter_context(
                    nc.psum_tensor(f"p_b2_{P.cnt['sp']}", [64, 1], F32))
                p_b3 = ps.enter_context(
                    nc.psum_tensor(f"p_b3_{P.cnt['sp']}", [128, 1], F32))
                p_hb = ps.enter_context(
                    nc.psum_tensor(f"p_hb_{P.cnt['sp']}", [64, 1], F32))

                for g_sem, g_val in rt_guards:
                    P.wait(TEN, g_sem, g_val)
                P.wait(TEN, "sv", zbn_sv)
                for c in range(NC):
                    P.op(TEN, lambda e, c=c: e.matmul(
                        p_r[:], wr2[:, 8 * c:8 * (c + 1)], tmp8[:, c:c + 1],
                        start=(c == 0), stop=False))
                r_ms = P.op(TEN, lambda e: e.matmul(
                    p_r[:], br_row, ones11, start=False, stop=True), "sp")
                P.wait(VEC, "sp", r_ms)
                r8_sv = P.op(VEC, lambda e: e.tensor_copy(r8[:], p_r[:]), "sv")
                P.wait(TEN, "sv", r8_sv)
                rt_ms = P.op(TEN, lambda e: e.transpose(
                    p_rt[:], r8[:], identf8[:]), "sp")
                P.wait(VEC, "sp", rt_ms)
                P.op(VEC, lambda e: e.tensor_copy(rrow[:], p_rt[:]))
                P.drain(VEC)
                P.op(VEC, lambda e: e.reduce_max(rmax[:], rrow[:], axis=AX.X))
                P.drain(VEC)
                nm_sv = P.op(VEC, lambda e: e.tensor_scalar(
                    rnmax[:], rmax[:], -1.0, None, ALU.mult), "sv")
                P.wait(SCA, "sv", nm_sv)
                ex_sa = P.op(SCA, lambda e: e.activation(
                    rex[:], rrow[:], AF.Exp, bias=rnmax[:], scale=1.0), "sa")
                P.wait(VEC, "sa", ex_sa)
                P.op(VEC, lambda e: e.reduce_sum(rsum[:], rex[:], axis=AX.X))
                P.op(VEC, lambda e: e.tensor_scalar(
                    rtmp[:], rex[:], 1.0, None, ALU.subtract))
                P.drain(VEC)
                P.op(VEC, lambda e: e.reciprocal(rrec[:], rsum[:]))
                P.op(VEC, lambda e: e.tensor_tensor(rtmp2[:], rtmp[:], rtmp[:],
                                                    ALU.mult))
                P.drain(VEC)
                P.op(VEC, lambda e: e.tensor_scalar(
                    rtmp2[:], rtmp2[:], EPSA, None, ALU.add))
                P.drain(VEC)
                P.op(VEC, lambda e: e.reciprocal(rtmp[:], rtmp2[:]))
                P.drain(VEC)
                mask_sv = P.op(VEC, lambda e: e.tensor_scalar(
                    rtmp[:], rtmp[:], EPSA, None, ALU.mult), "sv")
                P.drain(VEC)
                P.op(VEC, lambda e: e.tensor_tensor(rtmp2[:], rtmp[:], iota8f,
                                                    ALU.mult))
                P.drain(VEC)
                idx_sv = P.op(VEC, lambda e: e.reduce_sum(
                    idxf[:], rtmp2[:], axis=AX.X), "sv")
                P.wait(TEN, "sv", mask_sv)
                mt_ms = P.op(TEN, lambda e: e.transpose(
                    p_r[:], rtmp[:], identf8[0:1, 0:1]), "sp")
                P.wait(VEC, "sp", mt_ms)
                P.op(VEC, lambda e: e.tensor_copy(tmp8b[0:8, 0:1], p_r[:]))
                P.drain(VEC)
                P.wait(VEC, "sq", vp_ones_sv)
                sel_sv = P.op(VEC, lambda e: e.tensor_scalar(
                    p_buf[64:72, :], p_buf[64:72, :], tmp8b[0:8, 0:1], None,
                    ALU.mult), "sv")
                P.wait(TEN, "sv", idx_sv)
                b1_ms = P.op(TEN, lambda e: e.matmul(
                    p_b1[:], c128row, idxf[:], start=True, stop=True), "sp")
                b2_ms = P.op(TEN, lambda e: e.matmul(
                    p_b2[:], c64row, idxf[:], start=True, stop=True), "sp")
                b3_ms = P.op(TEN, lambda e: e.matmul(
                    p_b3[:], c05row, rrec[:], start=True, stop=True), "sp")
                P.wait(VEC, "sp", b3_ms)
                P.op(VEC, lambda e: e.tensor_copy(idxc[:], p_b1[:]))
                P.op(VEC, lambda e: e.tensor_copy(uidxc[:], p_b2[:]))
                P.op(VEC, lambda e: e.tensor_copy(sgcol[:], p_b3[:]))
                P.op(VEC, lambda e: e.tensor_copy(idxi1[:], idxf[:]))
                P.drain(VEC)
                P.op(VEC, lambda e: e.tensor_copy(idxi[:], idxc[:]))
                P.op(VEC, lambda e: e.tensor_copy(uidxi[:], uidxc[:]))
                P.drain(VEC)
                P.op(VEC, lambda e: e.tensor_tensor(widx[:], idxi[:], iotai[:],
                                                    ALU.add))
                wix_sv = P.op(VEC, lambda e: e.tensor_tensor(
                    uidx[:], uidxi[:], iotai[0:64, :], ALU.add), "sv")

                P.wait(POOL, "sv", wix_sv)
                if stage_guard is not None:
                    P.wait(POOL, stage_guard[0], stage_guard[1])
                P.dma(POOL, lambda e: e.indirect_dma_start(
                    wd_f32, None, d_wd2,
                    bass.IndirectOffsetOnAxis(ap=widx[:], axis=0)), "sdg")
                P.dma(POOL, lambda e: e.indirect_dma_start(
                    wu_f32, None, d_wu2,
                    bass.IndirectOffsetOnAxis(ap=uidx[:], axis=0)), "sdg")
                g_ms = P.dma(POOL, lambda e: e.indirect_dma_start(
                    bd_f[:], None, d_bd2,
                    bass.IndirectOffsetOnAxis(ap=uidx[:], axis=0)), "sdg")

                P.wait(VEC, "sdg", g_ms)
                for c in range(NC):
                    P.op(VEC, lambda e, c=c: e.tensor_scalar(
                        wd_bf[:, 64 * c:64 * (c + 1)],
                        wd_f32[:, 64 * c:64 * (c + 1)],
                        ffng2[:, c:c + 1], None, ALU.mult))
                wprep_sv = P.op(VEC, lambda e: e.tensor_copy(
                    wu_ext[0:64, :], wu_f32), "sv")
                P.wait(TEN, "sdg", g_ms)
                for c in range(NC):
                    P.op(TEN, lambda e, c=c: e.matmul(
                        p_hb[:], wd_f32[:, 64 * c:64 * (c + 1)],
                        ffnb2[:, c:c + 1], start=(c == 0), stop=(c == NC - 1)),
                        "sp" if c == NC - 1 else None)
                hb_ms = P.cnt["sp"]
                P.wait(VEC, "sp", hb_ms)
                hb_sv = P.op(VEC, lambda e: e.tensor_tensor(
                    hbias[:], p_hb[:], bd_f[:], ALU.add), "sv")
                ffn_state["stage_last_sv"] = wprep_sv
                ffn_state["stage_last_sp"] = hb_ms

            with contextlib.ExitStack() as psh:
                p_h = [psh.enter_context(nc.psum_tensor(
                    f"p_h{i}_{P.cnt['sp']}", [64, 512], F32)) for i in range(2)]
                P.wait(TEN, "sv", wprep_sv)
                P.wait(TEN, "sv", hb_sv)
                relu_ms = [None, None]
                for tc in range(4):
                    if relu_ms[tc % 2] is not None:
                        P.wait(TEN, "sa", relu_ms[tc % 2])
                    ms = None
                    for c in range(NC):
                        ms = P.op(TEN, lambda e, c=c, tc=tc: e.matmul(
                            p_h[tc % 2][:],
                            wd_bf[:, 64 * c:64 * (c + 1)],
                            zT_c(c)[:, 512 * tc:512 * (tc + 1)],
                            start=(c == 0), stop=(c == NC - 1)),
                            "sp" if c == NC - 1 else None)
                    P.wait(SCA, "sp", ms)
                    P.wait(SCA, "sv", hb_sv)
                    relu_ms[tc % 2] = P.op(SCA, lambda e, tc=tc: e.activation(
                        p_buf[0:64, 512 * tc:512 * (tc + 1)], p_h[tc % 2][:],
                        AF.Relu, bias=hbias[:], scale=1.0), "sa")
            with contextlib.ExitStack() as ps:
                p_y = [ps.enter_context(nc.psum_tensor(
                    f"p_yf{i}_{P.cnt['sp']}", [128, 1024], F32)) for i in range(2)]
                if with_n2:
                    p_t2n = [ps.enter_context(nc.psum_tensor(
                        f"p_t2n{k}_{P.cnt['sp']}", [128, 1024], BF16))
                        for k in range(2)]
                    p_qk2 = [ps.enter_context(nc.psum_tensor(
                        f"p_qk2{k}_{P.cnt['sp']}", [128, 512], F32))
                        for k in range(2)]
                    zview_n = big_b.rearrange("p (c tt) -> p c tt", c=8)
                    vpb = vprime
                    P.wait(SCA, "sq", eps_sq)
                    P.wait(SCA, "sq", vp_ones_sq)
                    P.wait(VEC, "sq", sv_const)
                    wplan = []
                    for dram_w, n_ch, ccols, dstw in (
                            (d_wqk, 8, 1024, wqk_bf), (d_wv, 8, 512, wv_bf),
                            (d_wout, 4, 1024, wout_bf)):
                        dr = dram_w.rearrange("(c p) f -> c p f", p=128)
                        for cch in range(n_ch):
                            wplan.append((dr, cch, ccols, dstw))
                    wdma_ms = [None] * len(wplan)
                    wcast_ms = [None] * len(wplan)
                    n_dma = [0]
                    n_cast = [0]

                    def emit_wdma():
                        c = n_dma[0]
                        if c >= len(wplan):
                            return
                        dr, cch, ccols, dstw = wplan[c]
                        sl_ = stage[:, 1024 * (c % 2):1024 * (c % 2) + ccols]
                        if c < 2:
                            P.wait(SYN, "sv", wprep_sv)
                            P.wait(SYN, "sv", hb_sv)
                            P.wait(SYN, "sp", ffn_state["stage_last_sp"])
                        else:
                            P.wait(SYN, "sq", wcast_ms[c - 2])
                        wdma_ms[c] = P.dma(SYN, lambda e, sl_=sl_, dr=dr,
                                           cch=cch: e.dma_start(sl_, dr[cch]),
                                           "sdl" if c % 2 == 0 else "sdlb")
                        n_dma[0] += 1

                    def emit_wcast():
                        c = n_cast[0]
                        if c >= len(wplan) or wdma_ms[c] is None:
                            return
                        dr, cch, ccols, dstw = wplan[c]
                        sl_ = stage[:, 1024 * (c % 2):1024 * (c % 2) + ccols]
                        P.wait(POOL, "sdl" if c % 2 == 0 else "sdlb",
                               wdma_ms[c])
                        wcast_ms[c] = P.op(POOL, lambda e, sl_=sl_, dstw=dstw,
                                           cch=cch, ccols=ccols: e.tensor_copy(
                            dstw[:, ccols * cch:ccols * (cch + 1)], sl_), "sq")
                        n_cast[0] += 1

                    emit_wdma()
                    emit_wdma()
                    qk_batches = [(tc, fc) for tc in range(4)
                                  for fc in range(8)]
                    qb_next = [0]
                    qk_ev2 = [None, None]
                    qk_started = [False]

                    def emit_qk_batch():
                        b = qb_next[0]
                        if b >= len(qk_batches):
                            return False
                        tc, fc = qk_batches[b]
                        if n2ev_t[4 * tc + 3] is None:
                            return False
                        if n_cast[0] < 8:
                            return False
                        if not qk_started[0]:
                            P.wait(TEN, "sq", wcast_ms[7])
                            qk_started[0] = True
                        sl_ = b % 2
                        P.wait(TEN, "sv", qk_ev2[sl_])
                        if fc == 0:
                            P.wait(TEN, "sv", n2ev_t[4 * tc + 3])
                        ms = None
                        for c in range(NC):
                            ms = P.op(TEN, lambda e, c=c, fc=fc, tc=tc,
                                      sl_=sl_: e.matmul(
                                p_qk2[sl_][:],
                                wqk_bf[:, D * c + 128 * fc:
                                       D * c + 128 * (fc + 1)],
                                zT_c(c)[:, 512 * tc:512 * (tc + 1)],
                                start=(c == 0), stop=(c == NC - 1),
                                skip_group_check=True),
                                "sp" if c == NC - 1 else None)
                        P.wait(VEC, "sp", ms)
                        if fc % 2 == 0:
                            qk_ev2[sl_] = P.op(VEC, lambda e, fc=fc, tc=tc,
                                               sl_=sl_: e.tensor_scalar(
                                qk_of(fc)[:, 512 * tc:512 * (tc + 1)],
                                p_qk2[sl_][:], 0.125, bqk2s[:, fc:fc + 1],
                                ALU.mult, ALU.add), "sv")
                        else:
                            qk_ev2[sl_] = P.op(VEC, lambda e, fc=fc, tc=tc,
                                               sl_=sl_: e.tensor_scalar(
                                qk_of(fc)[:, 512 * tc:512 * (tc + 1)],
                                p_qk2[sl_][:], 1.0, bqk2[:, fc:fc + 1],
                                ALU.mult, ALU.add), "sv")
                        qb_next[0] += 1
                        return True
                P.wait(TEN, "sa", relu_ms[0])
                P.wait(TEN, "sa", relu_ms[1])
                P.wait(TEN, "sq", buc_sq)
                P.wait(TEN, "sv", sel_sv)
                up_ev = [None, None]
                x_done = []
                n2ap = [None] * NT
                n2r1 = [None] * NT
                n2ag = [None] * NT
                n2nb = [None] * NT
                n2tr = [None] * NT
                n2ev_t = [None] * NT
                n2_last = None
                n_it = NT + 6 if with_n2 else NT
                for i in range(n_it):
                    if i < NT:
                        t = i
                        if up_ev[t % 2] is not None:
                            P.wait(TEN, "sv", up_ev[t % 2])
                        ms = None
                        for h in range(2):
                            ms = P.op(TEN, lambda e, t=t, h=h: e.matmul(
                                p_y[t % 2][:, 512 * h:512 * (h + 1)],
                                p_buf[0:72, 128 * t:128 * (t + 1)],
                                wu_ext[0:72, 512 * h:512 * (h + 1)],
                                start=True, stop=True, skip_group_check=True),
                                "sp" if h == 1 else None)
                    if with_n2 and 0 <= i - 5 < NT:
                        td = i - 5
                        P.wait(TEN, "sa", n2ap[td])
                        P.wait(TEN, "sv", n2ev_t[td - 2] if td >= 2 else None)
                        for c in range(NC):
                            n2tr[td] = P.op(TEN, lambda e, td=td, c=c:
                                            e.transpose(
                                p_t2n[td % 2][:, 128 * c:128 * (c + 1)],
                                vpb[:, 1024 * (td % 2) + 128 * c:
                                    1024 * (td % 2) + 128 * (c + 1)],
                                identb[:]), "sp" if c == NC - 1 else None)
                    if i < NT:
                        P.wait(VEC, "sp", ms)
                        up_ev[t % 2] = P.op(
                            VEC, lambda e, t=t: e.scalar_tensor_tensor(
                                xt(t), p_y[t % 2][:], sgcol[:], xt(t),
                                ALU.mult, ALU.add), "sv")
                        x_done.append(("sv", up_ev[t % 2]))
                    if with_n2:
                        P.drain(VEC)
                        if 0 <= i - 1 < NT:
                            td = i - 1
                            if td >= 2:
                                P.wait(VEC, "sv", n2ag[td - 2])
                            P.op(VEC, lambda e, td=td: e.bn_stats(
                                bnbuf[:, 12 * (td % 2):12 * (td % 2) + 6],
                                xt(td)[:, 0:512]))
                            P.op(VEC, lambda e, td=td: e.bn_stats(
                                bnbuf[:, 12 * (td % 2) + 6:12 * (td % 2) + 12],
                                xt(td)[:, 512:1024]))
                        if 0 <= i - 2 < NT:
                            td = i - 2
                            n2ag[td] = P.op(VEC, lambda e, td=td: e.bn_aggr(
                                stat_mv[:, 2 * td:2 * td + 2],
                                bnbuf[:, 12 * (td % 2):12 * (td % 2) + 12]),
                                "sv")
                        if 0 <= i - 4 < NT:
                            td = i - 4
                            P.wait(VEC, "sa", n2r1[td])
                            n2nb[td] = P.op(VEC, lambda e, td=td:
                                            e.scalar_tensor_tensor(
                                sc_c[:, td:td + 1], stat_mv[:, 2 * td:
                                                            2 * td + 1],
                                -1.0, r2_all[:, td:td + 1],
                                ALU.mult, ALU.mult), "sv")
                        if 0 <= i - 5 < NT:
                            td = i - 5
                            P.wait(VEC, "sp", n2tr[td])
                            n2_last = P.op(VEC, lambda e, td=td: e.tensor_copy(
                                zview_n[:, :, 128 * td:128 * (td + 1)],
                                p_t2n[td % 2][:].rearrange(
                                    "p (c tt) -> p c tt", c=8)), "sv")
                            n2ev_t[td] = n2_last
                        if 0 <= i - 3 < NT:
                            td = i - 3
                            P.wait(SCA, "sv", n2ag[td])
                            P.op(SCA, lambda e, td=td: e.activation(
                                sc_a[:, td:td + 1],
                                stat_mv[:, 2 * td + 1:2 * td + 2], AF.Ln,
                                bias=epscol[:], scale=1.0))
                        P.drain(SCA)
                        if 0 <= i - 3 < NT:
                            td = i - 3
                            n2r1[td] = P.op(SCA, lambda e, td=td: e.activation(
                                r2_all[:, td:td + 1], sc_a[:, td:td + 1],
                                AF.Exp, scale=-0.5), "sa")
                        if 0 <= i - 4 < NT:
                            td = i - 4
                            P.wait(SCA, "sv", n2nb[td])
                            if td >= 2:
                                P.wait(SCA, "sp", n2tr[td - 2])
                            n2ap[td] = P.op(SCA, lambda e, td=td: e.activation(
                                vpb[:, 1024 * (td % 2):1024 * (td % 2 + 1)],
                                xt(td), AF.Identity,
                                bias=sc_c[:, td:td + 1],
                                scale=r2_all[:, td:td + 1]), "sa")
                        emit_wcast()
                        emit_wdma()
                        emit_wcast()
                        emit_wdma()
                        if QK_IN_LOOP:
                            for _ in range(3):
                                if not emit_qk_batch():
                                    break
                if with_n2:
                    while n_cast[0] < len(wplan):
                        emit_wdma()
                        emit_wcast()
                    while emit_qk_batch():
                        pass
                    ffn_state["sq_wqk"] = wcast_ms[7]
                    ffn_state["sq_wv"] = wcast_ms[15]
                    ffn_state["sq_wout"] = wcast_ms[19]
                    ffn_state["qkproj_sv"] = max(
                        v for v in qk_ev2 if v is not None)
                    P.wait(POOL, "sv", n2_last)
                    ffn_state["vp2_sq"] = P.op(POOL, lambda e: e.memset(
                        vprime.rearrange("p (t h c) -> p t h c", t=16, h=8)
                        [:, :, :, 64:65], 1.0), "sq")
            return x_done, n2_last

        def emit_ffn(x_ready, stage_guard=None, psum_guard=None,
                     with_n2=False):
            zt_sv, nsum_sq = ln_stats_apply("ffn", x_ready, want_nsum=True,
                                            psum_guard=psum_guard)
            return emit_ffn_back(nsum_sq, stage_guard=stage_guard,
                                 with_n2=with_n2)

        x_done, n2_ev = emit_ffn(x_tile_dma, stage_guard=("sq", buc_sq),
                                 with_n2=True)

        sq_wqk = ffn_state["sq_wqk"]
        sq_wv = ffn_state["sq_wv"]
        sq_wout = ffn_state["sq_wout"]
        qkproj_sv = ffn_state["qkproj_sv"]
        vp2_sq = ffn_state["vp2_sq"]
        P.wait(POOL, "sq", sq_wout)
        ffn1_done_sv = x_done[NT - 1][1]
        P.wait(TEN, "sv", qkproj_sv)
        g_row = stage[0:1, 0:1024]
        b_row = stage[0:1, 1024:2048]
        P.dma(POOL, lambda e: e.dma_start(g_row, d_gamma_row), "sdw")
        dw_gb = P.dma(POOL, lambda e: e.dma_start(b_row, d_beta_row), "sdw")
        with nc.psum_tensor("p_bcg", [128, 512], F32) as p_bcg:
            P.wait(TEN, "sdw", dw_gb)
            for src_row, dst in ((g_row, gamC), (b_row, betC)):
                for half in range(2):
                    ms = P.op(TEN, lambda e, h=half, sr=src_row, pb=p_bcg:
                              e.matmul(
                        pb[:], crs2row, sr[:, 512 * h:512 * (h + 1)],
                        start=True, stop=True), "sp")
                    P.wait(VEC, "sp", ms)
                    ms_v = P.op(VEC, lambda e, h=half, d=dst, pb=p_bcg:
                                e.tensor_copy(
                        d[:, 512 * h:512 * (h + 1)], pb[:]), "sv")
                    P.wait(TEN, "sv", ms_v)
        sv_gb = P.cnt["sv"]
        P.wait(POOL, "sv", sv_gb)
        P.dma(POOL, lambda e: e.dma_start(g_row[:, 0:512], d_bv_row), "sdw")
        dw_bvo = P.dma(POOL, lambda e: e.dma_start(b_row, d_bout_row), "sdw")
        P.wait(VEC, "sdw", dw_bvo)
        P.op(VEC, lambda e: e.tensor_copy(bv_b[:], g_row[:, 0:512]))
        sv_rows = P.op(VEC, lambda e: e.tensor_copy(bout_b[:], b_row), "sv")

        with contextlib.ExitStack() as ps:
            p_v = [ps.enter_context(nc.psum_tensor(
                f"p_v{i}", [128, 512], F32)) for i in range(2)]
            p_q2 = ps.enter_context(nc.psum_tensor("p_q2", [1, 2048], F32))
            p_nc_ = ps.enter_context(nc.psum_tensor("p_nc", [128, 4], F32))
            stb = stage.bitcast(BF16)
            bsl = [stb[:, 0:2048], stb[:, 2048:4096]]
            cs_list = [None] * 8
            mx_list = [None] * 8
            P.wait(TEN, "sq", sq_wv)
            P.wait(TEN, "sv", qkproj_sv)
            P.wait(TEN, "sv", sv_rows)
            P.wait(VEC, "sq", vp2_sq)
            P.wait(VEC, "sq", sq_wout)
            P.wait(VEC, "sv", sv_rows)
            v_ev = [None, None]

            def emit_mx(fc):
                P.wait(VEC, "sp", cs_list[fc])
                mx_list[fc] = P.op(VEC, lambda e, fc=fc: e.reduce_max(
                    mqk8[:, fc:fc + 1], p_q2[:], axis=AX.X), "sv")

            def emit_vtile(t):
                sl = t % 2
                if v_ev[sl] is not None:
                    P.wait(TEN, "sv", v_ev[sl])
                for c in range(NC):
                    P.op(TEN, lambda e, c=c, t=t, sl=sl: e.matmul(
                        p_v[sl][:],
                        big_b[:, S * c + 128 * t:S * c + 128 * (t + 1)],
                        wv_bf[:, 512 * c:512 * (c + 1)],
                        start=(c == 0), stop=False))
                ms = P.op(TEN, lambda e, sl=sl: e.matmul(
                    p_v[sl][:], onesb_r[:], bv_b[:], start=False, stop=True),
                    "sp")
                P.wait(VEC, "sp", ms)
                v_ev[sl] = P.op(VEC, lambda e, t=t, sl=sl: e.tensor_copy(
                    vprime[:, 520 * t:520 * (t + 1)].rearrange(
                        "p (h c) -> p h c", h=8)[:, :, 0:64],
                    p_v[sl][:].rearrange("p (h c) -> p h c", h=8)), "sv")

            for fc in range(8):
                if fc >= 1:
                    emit_mx(fc - 1)
                if fc == 0:
                    P.wait(VEC, "sv", qkproj_sv)
                if fc >= 2:
                    P.wait(VEC, "sp", cs_list[fc - 2])
                s2_sv = P.op(VEC, lambda e, fc=fc: e.tensor_tensor(
                    bsl[fc % 2], qk_of(fc), qk_of(fc), ALU.mult), "sv")
                emit_vtile(2 * fc)
                emit_vtile(2 * fc + 1)
                P.wait(TEN, "sv", s2_sv)
                if fc >= 1:
                    P.wait(TEN, "sv", mx_list[fc - 1])
                ms = None
                for sp_ in range(4):
                    ms = P.op(TEN, lambda e, sp_=sp_, fc=fc: e.matmul(
                        p_q2[:, 512 * sp_:512 * (sp_ + 1)], onesb_c[:],
                        bsl[fc % 2][:, 512 * sp_:512 * (sp_ + 1)],
                        start=True, stop=True, skip_group_check=True),
                        "sp" if sp_ == 3 else None)
                cs_list[fc] = ms
            emit_mx(7)
            vproj_sv = max(v for v in v_ev if v is not None)
            P.drain(VEC)
            mq_sv = P.op(VEC, lambda e: e.tensor_copy(rtmp[:], mqk8[:]), "sv")
            P.wait(SCA, "sv", mq_sv)
            ln8_sa = P.op(SCA, lambda e: e.activation(rtmp2[:], rtmp[:], AF.Ln),
                          "sa")
            P.wait(VEC, "sa", ln8_sa)
            pr_view = rtmp2.rearrange("p (k two) -> p k two", two=2)
            sum_sv = P.op(VEC, lambda e: e.tensor_tensor(
                c4[:], pr_view[:, :, 0:1].rearrange("p k two -> p (k two)"),
                pr_view[:, :, 1:2].rearrange("p k two -> p (k two)"),
                ALU.add), "sv")
            P.wait(SCA, "sv", sum_sv)
            c4_sa = P.op(SCA, lambda e: e.activation(
                rtmp[0:1, 0:4], c4[:], AF.Exp, scale=0.5), "sa")
            P.wait(TEN, "sa", c4_sa)
            ms = P.op(TEN, lambda e: e.matmul(
                p_nc_[:], cn1row, rtmp[0:1, 0:4], start=True,
                stop=True), "sp")
            P.wait(VEC, "sp", ms)
            negc_sv = P.op(VEC, lambda e: e.tensor_copy(
                negc_bc[:], p_nc_[:]), "sv")

        with contextlib.ExitStack() as ps:
            reg = [ps.enter_context(nc.psum_tensor(
                f"p_reg{i}", [128, 1024], F32)) for i in range(2)]
            p_oA = ps.enter_context(nc.psum_tensor("p_oA", [65, 512], F32))
            p_oB = ps.enter_context(nc.psum_tensor("p_oB", [65, 512], F32))
            p_bc0 = ps.enter_context(nc.psum_tensor("p_bcst0", [64, 512], F32))
            p_bc1 = ps.enter_context(nc.psum_tensor("p_bcst1", [64, 512], F32))

            P.wait(TEN, "sv", max(vproj_sv, negc_sv))
            P.wait(TEN, "sq", vp2_sq)
            P.wait(SCA, "sv", negc_sv)
            P.wait(SCA, "sq", sq_wout)
            P.wait(VEC, "sq", sq_wout)
            reg_free = [None, None]
            pslot_free = [None, None, None, None]

            def psl_ap(k):
                if k < 3:
                    return p_buf[:, 1024 * k:1024 * (k + 1)]
                return wqk_bf[:, 1024:2048]
            rowtmp = stage[0:1, 1536:2048]
            rowtmpB = wqk_bf.bitcast(F32)[0:1, 0:512]
            bcA_sb = stage[0:64, 1024:1536]
            bcB_sb = stage[64:128, 1024:1536]

            def vsl(j, h):
                return vprime[:, 520 * j + 65 * h:520 * j + 65 * h + 65]

            state = {"pend": None, "o_free": None, "bc_free": None}

            def onorm_act(pend):
                _, _, pv15 = pend
                P.wait(SCA, "sp", pv15)
                P.op(SCA, lambda e: e.activation(rowtmp, p_oA[64:65, :], AF.Ln))
                P.op(SCA, lambda e: e.activation(rowtmpB, p_oB[64:65, :],
                                                 AF.Ln))
                P.drain(SCA)
                P.op(SCA, lambda e: e.activation(recA, rowtmp, AF.Exp,
                                                 scale=-1.0))
                return P.op(SCA, lambda e: e.activation(
                    recB[:], rowtmpB, AF.Exp, scale=-1.0), "sa")

            def onorm_pe(pend, rec_ms):
                P.wait(TEN, "sa", rec_ms)
                P.wait(TEN, "sv", state["bc_free"])
                P.op(TEN, lambda e: e.matmul(
                    p_bc0[:], crows[0:1, 0:64], recA, start=True, stop=True))
                return P.op(TEN, lambda e: e.matmul(
                    p_bc1[:], crows[0:1, 0:64], recB[:], start=True,
                    stop=True), "sp")

            def onorm_vec(pend, bc_ms):
                pr, i, _ = pend
                P.wait(VEC, "sp", bc_ms)
                P.op(VEC, lambda e: e.tensor_copy(bcA_sb, p_bc0[:]))
                P.op(VEC, lambda e: e.tensor_copy(bcB_sb, p_bc1[:]))
                P.drain(VEC)
                P.op(VEC, lambda e, pr=pr, i=i: e.tensor_tensor(
                    ost_of(pr)[0:64, 512 * i:512 * (i + 1)], p_oA[0:64, :],
                    bcA_sb, ALU.mult))
                ev = P.op(VEC, lambda e, pr=pr, i=i: e.tensor_tensor(
                    ost_of(pr)[64:128, 512 * i:512 * (i + 1)],
                    p_oB[0:64, :], bcB_sb, ALU.mult), "sv")
                state["o_free"] = ev
                state["bc_free"] = ev

            blocks = [(i, pr) for i in range(4) for pr in range(4)]
            TOT = 16 * 16
            exp_hist = {}
            qk_hist = {}
            rec_pend = None
            for g in range(TOT + 3):
                if g >= TOT:
                    P.wait(TEN, "sa", exp_hist.get(g - 2))
                if g < TOT:
                    b, j = blocks[g // 16], g % 16
                    i, pr = b
                    sl = g % 2
                    P.wait(TEN, "sa", exp_hist.get(g - 2))
                    P.op(TEN, lambda e, pr=pr, i=i, j=j, sl=sl: e.matmul(
                        reg[sl][:, 0:512],
                        qk_of(2 * pr + 1)[0:64, 128 * j:128 * (j + 1)],
                        qk_of(2 * pr)[0:64, 512 * i:512 * (i + 1)],
                        start=True, stop=True, tile_position=(0, 0),
                        skip_group_check=True))
                    qk_hist[g] = P.op(TEN, lambda e, pr=pr, i=i, j=j, sl=sl:
                                      e.matmul(
                        reg[sl][:, 512:1024],
                        qk_of(2 * pr + 1)[64:128, 128 * j:128 * (j + 1)],
                        qk_of(2 * pr)[64:128, 512 * i:512 * (i + 1)],
                        start=True, stop=True, tile_position=(64, 0),
                        skip_group_check=True), "sp")
                if rec_pend is not None and g % 16 == 3:
                    pend, rec_ms = rec_pend
                    bc_ms = onorm_pe(pend, rec_ms)
                    onorm_vec(pend, bc_ms)
                    rec_pend = None
                pg = g - 3
                if pg >= 0:
                    pb, pj = blocks[pg // 16], pg % 16
                    pi, ppr = pb
                    hA, hB = 2 * ppr, 2 * ppr + 1
                    psl = pg % 4
                    if pj == 0:
                        P.wait(TEN, "sv", state["o_free"])
                    P.op(TEN, lambda e, pj=pj, hA=hA, psl=psl: e.matmul(
                        p_oA[:], vsl(pj, hA),
                        psl_ap(psl)[:, 0:512],
                        start=(pj == 0), stop=(pj == 15),
                        skip_group_check=True))
                    pv_ms = P.op(TEN, lambda e, pj=pj, hB=hB, psl=psl: e.matmul(
                        p_oB[:], vsl(pj, hB),
                        psl_ap(psl)[:, 512:1024],
                        start=(pj == 0), stop=(pj == 15),
                        skip_group_check=True), "sp")
                    pslot_free[psl] = pv_ms
                    if pj == 15:
                        pend = (ppr, pi, pv_ms)
                        rec_pend = (pend, onorm_act(pend))
                if g < TOT:
                    if pslot_free[g % 4] is not None and \
                            pslot_free[g % 4] > qk_hist[g]:
                        P.wait(SCA, "sp", pslot_free[g % 4])
                    else:
                        P.wait(SCA, "sp", qk_hist[g])
                        P.wait(SCA, "sp", pslot_free[g % 4])
                    exp_hist[g] = P.op(SCA, lambda e, pr=pr, sl=sl, es=(g % 4):
                                       e.activation(
                        psl_ap(es), reg[sl][:],
                        AF.Exp, bias=negc_bc[:, pr:pr + 1], scale=1.0), "sa")
            pend, rec_ms = rec_pend
            bc_ms = onorm_pe(pend, rec_ms)
            onorm_vec(pend, bc_ms)
            ostack_sv = P.cnt["sv"]
            p4_sp_last = P.cnt["sp"]

        P.op(VEC, lambda e: e.tensor_scalar(
            alpha_a[:], r2_all[:], RS2, 1.0, ALU.mult, ALU.add))
        al_sv = P.op(VEC, lambda e: e.scalar_tensor_tensor(
            m2_a[:], mv_mean2, -RS2, r2_all[:], ALU.mult, ALU.mult), "sv")
        P.drain(VEC)

        xbt = vprime.bitcast(F32)

        def xa_of(k):
            return xbt[:, 1024 * k:1024 * (k + 1)]

        zview = big_b.rearrange("p (c tt) -> p c tt", c=8)
        with contextlib.ExitStack() as ps:
            p_y = [ps.enter_context(nc.psum_tensor(
                f"p_yo{i}", [128, 1024], F32)) for i in range(3)]
            p_t2f = [ps.enter_context(nc.psum_tensor(
                f"p_t2f{k}", [128, 1024], BF16)) for k in range(2)]
            P.wait(TEN, "sv", ostack_sv)
            P.wait(TEN, "sq", sq_wout)
            P.wait(SCA, "sv", ostack_sv)
            P.wait(SCA, "sp", p4_sp_last)
            P.wait(SCA, "sq", eps_sq)
            w1a_hist = [None] * NT
            fin_hist = [None] * NT
            yr_hist = [None] * NT
            ap_hist = [None] * NT
            ag_hist = [None] * NT
            x2ag_hist = [None] * NT
            lnb_hist = [None] * NT
            sum2_hist = [None] * NT
            sexp_hist = [None] * NT
            nb_hist = [None] * NT
            tr_hist = [None] * NT
            xb_pend = [None] * NT
            w1b_pend = [None] * NT
            ev_hist = [None, None]
            last_ev = None
            nacc_q = []
            nsum_sq2 = None

            wq_f = wqk_bf.bitcast(F32)

            def u4(k):
                return wq_f[:, 1024 * k:1024 * (k + 1)]

            def ymv(k):
                return stat_mv[:, 4 + 2 * k:6 + 2 * k]

            def x2m(td):
                return x2mv[:, 2 * td:2 * td + 1]

            def x2v(td):
                return x2mv[:, 2 * td + 1:2 * td + 2]

            def scol(buf, td):
                return buf[:, td:td + 1]

            for i in range(NT + 10):
                if i < NT:
                    t = i
                    sl = t % 3
                    if t >= 3:
                        P.wait(TEN, "sa", w1a_hist[t - 3])
                        P.wait(TEN, "sv", ag_hist[t - 3])
                    for h in range(2):
                        for pr in range(4):
                            P.op(TEN, lambda e, t=t, h=h, pr=pr, sl=sl:
                                 e.matmul(
                                p_y[sl][:, 512 * h:512 * (h + 1)],
                                ost_of(pr)[:, 128 * t:128 * (t + 1)],
                                wout_bf[:, D * pr + 512 * h:
                                        D * pr + 512 * (h + 1)],
                                start=(pr == 0), stop=False,
                                skip_group_check=True))
                        y_ms = P.op(TEN, lambda e, h=h, sl=sl: e.matmul(
                            p_y[sl][:, 512 * h:512 * (h + 1)], onesb_r[:],
                            bout_b[:, 512 * h:512 * (h + 1)], start=False,
                            stop=True, skip_group_check=True), "sp")
                if 0 <= i - 10 < NT:
                    td = i - 10
                    P.wait(TEN, "sa", ap_hist[td])
                    P.wait(TEN, "sv", ev_hist[td % 2])
                    for c in range(NC):
                        tr_hist[td] = P.op(TEN, lambda e, td=td, c=c:
                                           e.transpose(
                            p_t2f[td % 2][:, 128 * c:128 * (c + 1)],
                            big_a[:, D * td + 128 * c:D * td + 128 * (c + 1)],
                            identb[:]), "sp" if c == NC - 1 else None)
                if i < NT:
                    P.wait(VEC, "sp", y_ms)
                    if i >= 1:
                        P.wait(VEC, "sv", ag_hist[i - 1])
                    P.op(VEC, lambda e, sl=sl: e.bn_stats(
                        bnbuf[:, 0:6], p_y[sl][:, 0:512]))
                    P.op(VEC, lambda e, sl=sl: e.bn_stats(
                        bnbuf[:, 6:12], p_y[sl][:, 512:1024]))
                P.drain(VEC)
                if i < NT:
                    ag_hist[i] = P.op(VEC, lambda e, i=i: e.bn_aggr(
                        ymv(i % 2), bnbuf[:, 0:12]), "sv")
                if 0 <= i - 2 < NT:
                    td = i - 2
                    P.wait(VEC, "sq", xb_pend[td])
                    P.op(VEC, lambda e, td=td: e.scalar_tensor_tensor(
                        xa_of(td % 4), gamC[:], ynb[:, td % 2:td % 2 + 1],
                        xa_of(td % 4), ALU.mult, ALU.add))
                if 0 <= i - 3 < NT:
                    td = i - 3
                    P.wait(VEC, "sq", w1b_pend[td])
                    fin_hist[td] = P.op(VEC, lambda e, td=td: e.tensor_tensor(
                        xt(td), xa_of(td % 4), u4(td % 4), ALU.add), "sv")
                if 0 <= i - 4 < NT:
                    td = i - 4
                    if td >= 2:
                        P.wait(VEC, "sv", x2ag_hist[td - 2])
                    P.op(VEC, lambda e, td=td: e.bn_stats(
                        bnbuf[:, 12 + 12 * (td % 2):18 + 12 * (td % 2)],
                        xt(td)[:, 0:512]))
                    P.op(VEC, lambda e, td=td: e.bn_stats(
                        bnbuf[:, 18 + 12 * (td % 2):24 + 12 * (td % 2)],
                        xt(td)[:, 512:1024]))
                if 0 <= i - 5 < NT:
                    td = i - 5
                    x2ag_hist[td] = P.op(VEC, lambda e, td=td: e.bn_aggr(
                        x2mv[:, 2 * td:2 * td + 2],
                        bnbuf[:, 12 + 12 * (td % 2):24 + 12 * (td % 2)]), "sv")
                if 0 <= i - 8 < NT:
                    td = i - 8
                    P.wait(VEC, "sa", lnb_hist[td])
                    sum2_hist[td] = P.op(VEC, lambda e, td=td: e.tensor_tensor(
                        scol(sc_a, td), scol(sc_a, td), scol(sc_c, td),
                        ALU.add), "sv")
                if 0 <= i - 9 < NT:
                    td = i - 9
                    P.wait(VEC, "sa", sexp_hist[td])
                    nb_hist[td] = P.op(VEC, lambda e, td=td:
                                       e.scalar_tensor_tensor(
                        scol(sc_c, td), x2m(td), -1.0, scol(sc_b, td),
                        ALU.mult, ALU.mult), "sv")
                if 0 <= i - 1 < NT:
                    td = i - 1
                    P.wait(VEC, "sa", yr_hist[td])
                    P.op(VEC, lambda e, td=td: e.scalar_tensor_tensor(
                        ynb[:, td % 2:td % 2 + 1], ymv(td % 2)[:, 0:1], -1.0,
                        yrstd[:, td % 2:td % 2 + 1], ALU.mult, ALU.mult))
                if 0 <= i - 10 < NT:
                    td = i - 10
                    P.wait(VEC, "sp", tr_hist[td])
                    last_ev = P.op(VEC, lambda e, td=td: e.tensor_copy(
                        zview[:, :, 128 * td:128 * (td + 1)],
                        p_t2f[td % 2][:].rearrange("p (c tt) -> p c tt", c=8)),
                        "sv")
                    ev_hist[td % 2] = last_ev
                if i < NT:
                    t = i
                    P.wait(SCA, "sv", ag_hist[t])
                    P.op(SCA, lambda e, t=t: e.activation(
                        ylnv[:, t % 2:t % 2 + 1], ymv(t % 2)[:, 1:2], AF.Ln,
                        bias=epscol[:], scale=1.0))
                if 0 <= i - 6 < NT:
                    td = i - 6
                    P.wait(SCA, "sv", x2ag_hist[td])
                    P.op(SCA, lambda e, td=td: e.activation(
                        scol(sc_a, td), x2v(td), AF.Ln,
                        bias=epscol[:], scale=1.0))
                P.drain(SCA)
                if i < NT:
                    t = i
                    yr_hist[t] = P.op(SCA, lambda e, t=t: e.activation(
                        yrstd[:, t % 2:t % 2 + 1], ylnv[:, t % 2:t % 2 + 1],
                        AF.Exp, scale=-0.5), "sa")
                if 0 <= i - 6 < NT:
                    td = i - 6
                    P.op(SCA, lambda e, td=td: e.activation(
                        scol(sc_b, td), scol(sc_a, td), AF.Exp, scale=-1.0))
                if 0 <= i - 7 < NT:
                    td = i - 7
                    lnb_hist[td] = P.op(SCA, lambda e, td=td: e.activation(
                        scol(sc_c, td), x2v(td), AF.Ln,
                        bias=epscol[:], scale=scol(sc_b, td)), "sa")
                P.drain(SCA)
                if i < NT:
                    t = i
                    if t == 0:
                        P.wait(SCA, "sv", al_sv)
                    if t >= 4:
                        P.wait(SCA, "sv", fin_hist[t - 4])
                    xa_sa = P.op(SCA, lambda e, t=t: e.activation(
                        xa_of(t % 4), xt(t), AF.Identity,
                        bias=m2_a[:, t:t + 1], scale=alpha_a[:, t:t + 1]),
                        "sa")
                    w1a_hist[t] = P.op(SCA, lambda e, t=t: e.activation(
                        u4(t % 4), p_y[t % 3][:], AF.Identity,
                        scale=yrstd[:, t % 2:t % 2 + 1]), "sa")
                if 0 <= i - 8 < NT:
                    td = i - 8
                    P.wait(SCA, "sv", sum2_hist[td])
                    sexp_hist[td] = P.op(SCA, lambda e, td=td: e.activation(
                        scol(sc_b, td), scol(sc_a, td), AF.Exp, scale=-0.5),
                        "sa")
                if 0 <= i - 9 < NT:
                    td = i - 9
                    P.wait(SCA, "sv", nb_hist[td])
                    ap_hist[td] = P.op(SCA, lambda e, td=td: e.activation(
                        nz_t(td), xt(td), AF.Identity,
                        bias=scol(sc_c, td), scale=scol(sc_b, td)), "sa")
                if i < NT:
                    t = i
                    P.wait(POOL, "sa", xa_sa)
                    xb_pend[t] = P.op(POOL, lambda e, t=t: e.tensor_tensor(
                        xa_of(t % 4), xa_of(t % 4), betC[:], ALU.add), "sq")
                    P.wait(POOL, "sa", w1a_hist[t])
                    w1b_pend[t] = P.op(POOL, lambda e, t=t: e.tensor_tensor(
                        u4(t % 4), u4(t % 4), gamC[:], ALU.mult), "sq")
            P.wait(SCA, "sv", last_ev)
            for c in range(4):
                nsum_sq2 = P.op(SCA, lambda e, c=c: e.activation(
                    big_a[:, 2048 * c:2048 * (c + 1)], zT_c(c), AF.Identity,
                    scale=1.0, accum_out=nsum[:, c:c + 1]), "sa")
            P.drain(VEC)
            for c in range(4, NC):
                P.op(VEC, lambda e, c=c: e.reduce_sum(
                    nsum[:, c:c + 1], zT_c(c), axis=AX.X))

        w1a_last = w1a_hist[NT - 1]
        x_done3, _ = emit_ffn_back(nsum_sq2, stage_guard=("sa", w1a_last),
                                   rt_guards=[("sa", w1a_last)])

        yr_ = d_y.rearrange("(t p) d -> t p d", p=128)
        for t in range(NT):
            P.wait(SYN, x_done3[t][0], x_done3[t][1])
            P.dma(SYN, lambda e, t=t: e.dma_start(yr_[t], xt(t)), "sout")
        P.wait(SYN, "sout", P.cnt["sout"])

        block = E(nc.Block())

        @block.sync
        def _(eng):
            for fn in P.ops[SYN]:
                fn(eng)

        @block.tensor
        def _(eng):
            for fn in P.ops[TEN]:
                fn(eng)

        @block.vector
        def _(eng):
            for fn in P.ops[VEC]:
                fn(eng)

        @block.scalar
        def _(eng):
            for fn in P.ops[SCA]:
                fn(eng)

        @block.gpsimd
        def _(eng):
            for fn in P.ops[POOL]:
                fn(eng)

    return nc


def _host_inputs(inp):
    W_qk = np.asarray(inp["L0_attn_W_qk"])
    b_qk = np.asarray(inp["L0_attn_b_qk"])
    perm = np.empty(1024, dtype=np.int64)
    for p in range(4):
        for hi in range(2):
            h = 2 * p + hi
            d = np.arange(64)
            perm[(2 * p) * 128 + hi * 64 + d] = h * 128 + 2 * d
            perm[(2 * p + 1) * 128 + hi * 64 + d] = h * 128 + 2 * d + 1
    wqk = np.ascontiguousarray(W_qk[:, perm])
    bqkp = np.ascontiguousarray(b_qk[perm])

    def chunk2(v):
        return np.ascontiguousarray(np.asarray(v).reshape(8, 128).T)

    com = {
        "wqk": wqk,
        "wv": np.ascontiguousarray(np.asarray(inp["L0_attn_W_v"])),
        "wout": np.ascontiguousarray(np.asarray(inp["L0_attn_W_out"])),
        "bv_row": np.asarray(inp["L0_attn_b_v"]).reshape(1, 512).copy(),
        "bout_row": np.asarray(inp["L0_attn_b_out"]).reshape(1, 1024).copy(),
        "gamma_row": np.asarray(inp["L0_attn_ln_gamma"]).reshape(1, 1024).copy(),
        "beta_row": np.asarray(inp["L0_attn_ln_beta"]).reshape(1, 1024).copy(),
        "spk": np.concatenate([
            chunk2(bqkp), chunk2(inp["L0_ffn_ln_g"]),
            chunk2(inp["L0_ffn_ln_b"]), chunk2(inp["L0_ffn_bn_mean"]),
            chunk2(inp["L0_ffn_bn_var"]), chunk2(inp["L0_ffn_bn_b"]),
            chunk2(inp["L0_ffn_bn_g"]),
            np.ascontiguousarray(np.asarray(inp["L0_ffn_Wr"]).reshape(8, 128, 8)
                                 .transpose(1, 0, 2).reshape(128, 64)),
        ], axis=1).astype(np.float32),
        "wd2": np.ascontiguousarray(
            np.asarray(inp["L0_ffn_W_down"]).reshape(8, 8, 128, 64)
            .transpose(0, 2, 1, 3).reshape(1024, 512)),
        "wu2": np.ascontiguousarray(
            np.asarray(inp["L0_ffn_W_up"]).reshape(512, 1024)),
        "bd2": np.asarray(inp["L0_ffn_b_down"]).reshape(512, 1).copy(),
        "bu2": np.ascontiguousarray(np.asarray(inp["L0_ffn_b_up"])),
        "identb": np.eye(128, dtype=np.float32).astype(ml_dtypes.bfloat16),
        "identf8": np.eye(8, dtype=np.float32),
        "crows": np.concatenate([
            np.full(128, 1.0, np.float32), np.full(128, -1.0, np.float32),
            np.full(128, 128.0, np.float32), np.full(64, 64.0, np.float32),
            np.full(128, 0.5, np.float32), np.full(128, RS2, np.float32),
            np.arange(8, dtype=np.float32), np.full(8, 1.0, np.float32),
            np.asarray(inp["L0_ffn_br"], dtype=np.float32).reshape(8),
            np.full(40, 1.0, np.float32),
        ]).reshape(1, 768),
        "onesb_r": np.ones((1, 128), np.float32).astype(ml_dtypes.bfloat16),
        "onesb_c": np.ones((128, 1), np.float32).astype(ml_dtypes.bfloat16),
        "iotai": np.arange(128, dtype=np.int32).reshape(128, 1),
    }
    for k, v in com.items():
        if v.dtype == np.float64:
            com[k] = v.astype(np.float32)
    return com


def kernel(**inputs):
    x = np.ascontiguousarray(np.asarray(inputs["L0_attn_x"], dtype=np.float32))
    com = _host_inputs(inputs)
    nc = build_program()
    in_maps = []
    for b in range(NCORE):
        m = dict(com)
        m["x"] = np.ascontiguousarray(x[b])
        in_maps.append(m)
    res = run_bass_kernel_spmd(nc, in_maps, core_ids=list(range(NCORE)))
    out = np.stack([np.asarray(res.results[b]["y"]) for b in range(NCORE)],
                   axis=0)
    return out.astype(np.float32)


# version 31
# speedup vs baseline: 1.0620x; 1.0106x over previous
import contextlib
from collections import defaultdict

import numpy as np
import ml_dtypes
import concourse.bass as bass
import concourse.mybir as mybir
from concourse.bass_utils import run_bass_kernel_spmd

F32 = mybir.dt.float32
BF16 = mybir.dt.bfloat16
I32 = mybir.dt.int32
AF = mybir.ActivationFunctionType
ALU = mybir.AluOpType
AX = mybir.AxisListType

NCORE = 8
S = 2048
D = 1024
NT = 16
QK_IN_LOOP = True
NC = 8
EPS = 1e-5
RS2 = 2.0 ** (-0.5)
EPSA = 1e-30


class Prog:

    def __init__(self, sems):
        self.ops = defaultdict(list)
        self.cnt = defaultdict(int)
        self.sems = sems

    def emit(self, eng, fn):
        self.ops[eng].append(fn)

    def op(self, eng, fn, sem=None):
        if sem is None:
            self.emit(eng, fn)
            return None
        self.cnt[sem] += 1
        s = self.sems[sem]

        def run(e, fn=fn, s=s):
            fn(e).then_inc(s, 1)
        self.emit(eng, run)
        return self.cnt[sem]

    def dma(self, eng, fn, sem):
        self.cnt[sem] += 16
        s = self.sems[sem]

        def run(e, fn=fn, s=s):
            fn(e).then_inc(s, 16)
        self.emit(eng, run)
        return self.cnt[sem]

    def wait(self, eng, sem, value):
        if value is None:
            return
        s = self.sems[sem]
        self.emit(eng, lambda e, s=s, v=value: e.wait_ge(s, v))

    def drain(self, eng):
        self.emit(eng, lambda e: e.drain())


def build_program():
    nc = bass.Bass("TRN2", target_bir_lowering=False)

    d_x = nc.dram_tensor("x", [S, D], F32, kind="ExternalInput").ap()
    d_wqk = nc.dram_tensor("wqk", [D, D], F32, kind="ExternalInput").ap()
    d_wv = nc.dram_tensor("wv", [D, 512], F32, kind="ExternalInput").ap()
    d_wout = nc.dram_tensor("wout", [512, D], F32, kind="ExternalInput").ap()
    d_spk = nc.dram_tensor("spk", [128, 120], F32, kind="ExternalInput").ap()
    d_bv_row = nc.dram_tensor("bv_row", [1, 512], F32, kind="ExternalInput").ap()
    d_bout_row = nc.dram_tensor("bout_row", [1, D], F32, kind="ExternalInput").ap()
    d_gamma_row = nc.dram_tensor("gamma_row", [1, D], F32, kind="ExternalInput").ap()
    d_beta_row = nc.dram_tensor("beta_row", [1, D], F32, kind="ExternalInput").ap()
    d_wd2 = nc.dram_tensor("wd2", [1024, 512], F32, kind="ExternalInput").ap()
    d_wu2 = nc.dram_tensor("wu2", [512, 1024], F32, kind="ExternalInput").ap()
    d_bd2 = nc.dram_tensor("bd2", [512, 1], F32, kind="ExternalInput").ap()
    d_bu2 = nc.dram_tensor("bu2", [8, 1024], F32, kind="ExternalInput").ap()
    d_identb = nc.dram_tensor("identb", [128, 128], BF16, kind="ExternalInput").ap()
    d_identf8 = nc.dram_tensor("identf8", [8, 8], F32, kind="ExternalInput").ap()
    d_crows = nc.dram_tensor("crows", [1, 768], F32, kind="ExternalInput").ap()
    d_onesb_r = nc.dram_tensor("onesb_r", [1, 128], BF16, kind="ExternalInput").ap()
    d_onesb_c = nc.dram_tensor("onesb_c", [128, 1], BF16, kind="ExternalInput").ap()
    d_iotai = nc.dram_tensor("iotai", [128, 1], I32, kind="ExternalInput").ap()
    d_y = nc.dram_tensor("y", [S, D], F32, kind="ExternalOutput").ap()

    st = contextlib.ExitStack()
    with st:
        E = st.enter_context
        x_sb = E(nc.sbuf_tensor("sb_x_sb", [128, NT * D], F32))
        big_a = E(nc.sbuf_tensor("sb_big_a", [128, NT * D], BF16))
        big_b = E(nc.sbuf_tensor("sb_big_b", [128, NT * D], BF16))
        vprime = E(nc.sbuf_tensor("sb_vprime", [128, 16 * 520], BF16))
        p_buf = E(nc.sbuf_tensor("sb_p_buf", [128, 3072], BF16))
        hT = p_buf[0:65, :]
        wqk_bf = E(nc.sbuf_tensor("sb_wqk_bf", [128, 8 * D], BF16))
        wv_bf = E(nc.sbuf_tensor("sb_wv_bf", [128, 8 * 512], BF16))
        wout_bf = E(nc.sbuf_tensor("sb_wout_bf", [128, 4 * D], BF16))
        stage = E(nc.sbuf_tensor("sb_stage", [128, 2048], F32))
        gamC = E(nc.sbuf_tensor("sb_gamC", [128, D], BF16))
        betC = E(nc.sbuf_tensor("sb_betC", [128, D], BF16))
        wd_bf = E(nc.sbuf_tensor("sb_wd_bf", [128, 512], BF16))
        wu_ext = E(nc.sbuf_tensor("sb_wu_ext", [72, 1024], BF16))
        bv_b = E(nc.sbuf_tensor("sb_bv_b", [1, 512], BF16))
        bout_b = E(nc.sbuf_tensor("sb_bout_b", [1, D], BF16))
        spk = E(nc.sbuf_tensor("sb_spk", [128, 120], F32))
        bqk2 = spk[:, 0:8]
        ffng2 = spk[:, 8:16]
        ffnb2 = spk[:, 16:24]
        bnm2 = spk[:, 24:32]
        bnv2 = spk[:, 32:40]
        bnb2 = spk[:, 40:48]
        bnsc2 = spk[:, 48:56]
        wr2 = spk[:, 56:120]
        bqk2s = E(nc.sbuf_tensor("sb_bqk2s", [128, 8], F32))
        tmp8 = E(nc.sbuf_tensor("sb_tmp8", [128, 8], F32))
        tmp8b = E(nc.sbuf_tensor("sb_tmp8b", [128, 8], F32))
        bnbuf = E(nc.sbuf_tensor("sb_bnbuf", [128, 12 * NT], F32))
        stat_mv = E(nc.sbuf_tensor("sb_stat_mv", [128, 2 * NT], F32))
        sc_a = E(nc.sbuf_tensor("sb_sc_a", [128, NT], F32))
        sc_b = E(nc.sbuf_tensor("sb_sc_b", [128, NT], F32))
        sc_c = E(nc.sbuf_tensor("sb_sc_c", [128, NT], F32))
        mu2_all = E(nc.sbuf_tensor("sb_mu2_all", [128, NT], F32))
        r2_all = E(nc.sbuf_tensor("sb_r2_all", [128, NT], F32))
        alpha_a = E(nc.sbuf_tensor("sb_alpha_a", [128, NT], F32))
        m2_a = E(nc.sbuf_tensor("sb_m2_a", [128, NT], F32))
        nsum = E(nc.sbuf_tensor("sb_nsum", [128, 8], F32))
        r8 = E(nc.sbuf_tensor("sb_r8", [8, 1], F32))
        rrow = E(nc.sbuf_tensor("sb_rrow", [1, 8], F32))
        rex = E(nc.sbuf_tensor("sb_rex", [1, 8], F32))
        rtmp = E(nc.sbuf_tensor("sb_rtmp", [1, 8], F32))
        rtmp2 = E(nc.sbuf_tensor("sb_rtmp2", [1, 8], F32))
        rmax = E(nc.sbuf_tensor("sb_rmax", [1, 1], F32))
        rnmax = E(nc.sbuf_tensor("sb_rnmax", [1, 1], F32))
        rsum = E(nc.sbuf_tensor("sb_rsum", [1, 1], F32))
        rrec = E(nc.sbuf_tensor("sb_rrec", [1, 1], F32))
        gate1 = E(nc.sbuf_tensor("sb_gate1", [1, 1], F32))
        idxf = E(nc.sbuf_tensor("sb_idxf", [1, 1], F32))
        idxi1 = E(nc.sbuf_tensor("sb_idxi1", [1, 1], I32))
        idxc = E(nc.sbuf_tensor("sb_idxc", [128, 1], F32))
        idxi = E(nc.sbuf_tensor("sb_idxi", [128, 1], I32))
        widx = E(nc.sbuf_tensor("sb_widx", [128, 1], I32))
        uidxc = E(nc.sbuf_tensor("sb_uidxc", [64, 1], F32))
        uidxi = E(nc.sbuf_tensor("sb_uidxi", [64, 1], I32))
        uidx = E(nc.sbuf_tensor("sb_uidx", [64, 1], I32))
        sgcol = E(nc.sbuf_tensor("sb_sgcol", [128, 1], F32))
        bd_f = E(nc.sbuf_tensor("sb_bd_f", [64, 1], F32))
        hbias = E(nc.sbuf_tensor("sb_hbias", [64, 1], F32))
        mqk8 = E(nc.sbuf_tensor("sb_mqk8", [1, 8], F32))
        c4 = E(nc.sbuf_tensor("sb_c4", [1, 4], F32))
        negc_bc = E(nc.sbuf_tensor("sb_negc_bc", [128, 4], F32))
        yve = E(nc.sbuf_tensor("sb_yve", [128, 2], F32))
        x2mv = E(nc.sbuf_tensor("sb_x2mv", [128, 2 * NT], F32))
        epscol = E(nc.sbuf_tensor("sb_epscol", [128, 1], F32))
        gsS = E(nc.sbuf_tensor("sb_gsS", [128, 8], F32))
        cC = E(nc.sbuf_tensor("sb_cC", [128, 8], F32))
        npart = E(nc.sbuf_tensor("sb_npart", [128, 32], F32))
        ylnv = E(nc.sbuf_tensor("sb_ylnv", [128, 2], F32))
        yrstd = E(nc.sbuf_tensor("sb_yrstd", [128, 2], F32))
        ynb = E(nc.sbuf_tensor("sb_ynb", [128, 2], F32))
        identb = E(nc.sbuf_tensor("sb_identb", [128, 128], BF16))
        identf8 = E(nc.sbuf_tensor("sb_identf8", [8, 8], F32))
        crows = E(nc.sbuf_tensor("sb_crows", [1, 768], F32))
        c1row = crows[0:1, 0:128]
        cn1row = crows[0:1, 128:256]
        c128row = crows[0:1, 256:384]
        c64row = crows[0:1, 384:448]
        c05row = crows[0:1, 448:576]
        crs2row = crows[0:1, 576:704]
        onesb_r = E(nc.sbuf_tensor("sb_onesb_r", [1, 128], BF16))
        onesb_c = E(nc.sbuf_tensor("sb_onesb_c", [128, 1], BF16))
        iotai = E(nc.sbuf_tensor("sb_iotai", [128, 1], I32))
        iota8f = crows[0:1, 704:712]
        br_row = crows[0:1, 720:728]
        ones11 = crows[0:1, 712:713]

        sems = {}
        for name in ["sdx", "sdxb", "sdx2", "sdx2b", "sdw", "sdl", "sdlb",
                     "sdg", "sout", "sp", "sv", "sa", "sq"]:
            sems[name] = E(nc.semaphore(name))

        P = Prog(sems)
        TEN, VEC, SCA, POOL, SYN = "tensor", "vector", "scalar", "gpsimd", "sync"

        def xt(t):
            return x_sb[:, D * t:D * (t + 1)]

        def nz_t(t):
            return big_a[:, D * t:D * (t + 1)]

        def zT_c(c):
            return big_b[:, S * c:S * (c + 1)]

        def qk_of(fc):
            return big_a[:, 2048 * fc:2048 * (fc + 1)]

        def ost_of(pr):
            return big_b[:, 2048 * pr:2048 * (pr + 1)]

        wd_f32 = stage[:, 0:512]
        wu_f32 = stage[0:64, 512:1536]
        bu_row = stage[64:65, 512:1536]

        def u_of(k):
            return stage[:, 1024 * k:1024 * (k + 1)]

        recA = stage[0:1, 0:512]
        recB = stage[0:1, 512:1024]
        bcAB = stage[0:64, 1024:1536]

        mv_mean = stat_mv.rearrange("p (t k) -> p t k", k=2)[:, :, 0:1]
        mv_var = stat_mv.rearrange("p (t k) -> p t k", k=2)[:, :, 1:2]
        mv_mean2 = mv_mean.rearrange("p t k -> p (t k)")
        mv_var2 = mv_var.rearrange("p t k -> p (t k)")

        xr = d_x.rearrange("(t p) d -> t p d", p=128)
        for sbuf, dram in [
            (spk[:], d_spk), (identb[:], d_identb), (identf8[:], d_identf8),
            (crows[:], d_crows), (onesb_r[:], d_onesb_r), (onesb_c[:], d_onesb_c),
            (iotai[:], d_iotai),
        ]:
            P.dma(POOL, lambda e, sb=sbuf, dr=dram: e.dma_start(sb, dr), "sdw")
        dw_small = P.cnt["sdw"]
        x_tile_dma = [None] * NT
        for grp, (eng, sems2) in enumerate((
                (SYN, ("sdx", "sdxb")), (POOL, ("sdx2", "sdx2b")))):
            for half in range(2):
                sname = sems2[half]
                tiles = [t for t in range(NT)
                         if t % 2 == grp and (t >= 8) == bool(half)]
                full = None
                for t in tiles:
                    full = P.dma(eng, lambda e, t=t: e.dma_start(
                        xt(t), xr[t]), sname)
                for t in tiles:
                    x_tile_dma[t] = (sname, full)
        P.wait(TEN, "sdw", dw_small)
        P.wait(VEC, "sdw", dw_small)
        P.wait(SCA, "sdw", dw_small)



        vp_ones_sq = P.op(POOL, lambda e: e.memset(
            vprime.rearrange("p (t h c) -> p t h c", t=16, h=8)[:, :, :, 64:65],
            1.0), "sq")
        eps_sq = P.op(POOL, lambda e: e.memset(epscol[:], EPS), "sq")

        P.wait(POOL, "sdw", dw_small)
        P.op(POOL, lambda e: e.tensor_scalar(bqk2s[:], bqk2, 0.125, None,
                                             ALU.mult))
        P.drain(POOL)
        sv_ve = P.op(POOL, lambda e: e.tensor_scalar(tmp8[:], bnv2, EPS, None,
                                                     ALU.add), "sq")
        P.wait(SCA, "sq", sv_ve)
        P.op(SCA, lambda e: e.activation(tmp8b[:], tmp8[:], AF.Ln))
        P.drain(SCA)
        sa_bnr = P.op(SCA, lambda e: e.activation(tmp8[:], tmp8b[:], AF.Exp,
                                                  scale=-0.5), "sa")
        P.wait(POOL, "sa", sa_bnr)
        sv_const = P.op(POOL, lambda e: e.tensor_tensor(bnsc2, bnsc2,
                                                        tmp8[:], ALU.mult),
                        "sq")
        P.drain(POOL)
        P.op(POOL, lambda e: e.tensor_tensor(gsS[:], ffng2, bnsc2, ALU.mult))
        P.op(POOL, lambda e: e.tensor_tensor(cC[:], ffnb2, bnm2, ALU.subtract))
        P.drain(POOL)
        P.op(POOL, lambda e: e.tensor_scalar(gsS[:], gsS[:], 1.0 / S, None,
                                             ALU.mult))
        P.op(POOL, lambda e: e.tensor_tensor(cC[:], cC[:], bnsc2, ALU.mult))
        P.drain(POOL)
        gs_sq = P.op(POOL, lambda e: e.tensor_tensor(cC[:], cC[:], bnb2,
                                                     ALU.add), "sq")

        bu_stage = stage[0:8, 0:1024]
        dw_bu = P.dma(POOL, lambda e: e.dma_start(bu_stage, d_bu2), "sdw")
        P.wait(POOL, "sdw", dw_bu)
        buc_sq = P.op(POOL, lambda e: e.tensor_copy(wu_ext[64:72, :], bu_stage),
                      "sq")


        def ln_stats_apply(kind, x_ready, want_nsum=False, psum_guard=None):
            for t in range(NT):
                if x_ready is not None and x_ready[t] is not None:
                    P.wait(VEC, x_ready[t][0], x_ready[t][1])
                P.op(VEC, lambda e, t=t: e.bn_stats(
                    bnbuf[:, 12 * t:12 * t + 6], xt(t)[:, 0:512]))
                P.op(VEC, lambda e, t=t: e.bn_stats(
                    bnbuf[:, 12 * t + 6:12 * t + 12], xt(t)[:, 512:1024]))
            P.drain(VEC)
            for t in range(NT):
                P.op(VEC, lambda e, t=t: e.bn_aggr(
                    stat_mv[:, 2 * t:2 * t + 2], bnbuf[:, 12 * t:12 * t + 12]))
            P.drain(VEC)
            sv1 = P.op(VEC, lambda e: e.tensor_scalar(
                sc_a[:], mv_var2, EPS, None, ALU.add), "sv")
            P.wait(SCA, "sv", sv1)
            P.op(SCA, lambda e: e.activation(sc_c[:], sc_a[:], AF.Ln))
            P.drain(SCA)
            sa1 = P.op(SCA, lambda e: e.activation(sc_b[:], sc_c[:], AF.Exp,
                                                   scale=-0.5), "sa")
            P.wait(VEC, "sa", sa1)
            if kind == "ffn":
                P.op(VEC, lambda e: e.tensor_tensor(sc_c[:], mv_var2, sc_b[:],
                                                    ALU.mult))
                P.drain(VEC)
                P.op(VEC, lambda e: e.tensor_tensor(sc_c[:], sc_c[:], sc_b[:],
                                                    ALU.mult))
                P.drain(VEC)
                sv2 = P.op(VEC, lambda e: e.tensor_scalar(
                    sc_c[:], sc_c[:], EPS, None, ALU.add), "sv")
                P.wait(SCA, "sv", sv2)
                P.op(SCA, lambda e: e.activation(sc_a[:], sc_c[:], AF.Ln))
                P.drain(SCA)
                sa2 = P.op(SCA, lambda e: e.activation(
                    sc_c[:], sc_a[:], AF.Exp, scale=-0.5), "sa")
                P.wait(VEC, "sa", sa2)
                P.op(VEC, lambda e: e.tensor_tensor(sc_b[:], sc_b[:], sc_c[:],
                                                    ALU.mult))
                P.drain(VEC)
            else:
                P.op(VEC, lambda e: e.tensor_copy(mu2_all[:], mv_mean2))
                P.op(VEC, lambda e: e.tensor_copy(r2_all[:], sc_b[:]))
                P.drain(VEC)
            nb_sv = P.op(VEC, lambda e: e.scalar_tensor_tensor(
                sc_c[:], mv_mean2, -1.0, sc_b[:], ALU.mult, ALU.mult), "sv")
            P.drain(VEC)
            zview = big_b.rearrange("p (c tt) -> p c tt", c=8)
            last = None
            nsum_sq = None
            ev_t = [None] * NT
            with contextlib.ExitStack() as ps:
                p_t2 = [ps.enter_context(nc.psum_tensor(
                    f"p_t2{k}_{P.cnt['sp']}", [128, 1024], BF16))
                    for k in range(2)]
                ev_hist = [None, None]
                if psum_guard is not None:
                    P.wait(TEN, psum_guard[0], psum_guard[1])
                P.wait(SCA, "sv", nb_sv)
                for t in range(NT):
                    ap_ms = P.op(SCA, lambda e, t=t: e.activation(
                        nz_t(t), xt(t), AF.Identity,
                        bias=sc_c[:, t:t + 1], scale=sc_b[:, t:t + 1]), "sa")
                    P.wait(TEN, "sa", ap_ms)
                    P.wait(TEN, "sv", ev_hist[t % 2])
                    ms = None
                    for c in range(NC):
                        ms = P.op(TEN, lambda e, t=t, c=c, pt=p_t2: e.transpose(
                            pt[t % 2][:, 128 * c:128 * (c + 1)],
                            big_a[:, D * t + 128 * c:D * t + 128 * (c + 1)],
                            identb[:]), "sp" if c == NC - 1 else None)
                    P.wait(VEC, "sp", ms)
                    last = P.op(VEC, lambda e, t=t, pt=p_t2: e.tensor_copy(
                        zview[:, :, 128 * t:128 * (t + 1)],
                        pt[t % 2][:].rearrange("p (c tt) -> p c tt", c=8)),
                        "sv")
                    ev_hist[t % 2] = last
                    ev_t[t] = last
            if want_nsum:
                P.wait(SCA, "sv", last)
                for c in range(4):
                    nsum_sq = P.op(SCA, lambda e, c=c: e.activation(
                        big_a[:, 2048 * c:2048 * (c + 1)], zT_c(c), AF.Identity,
                        scale=1.0, accum_out=nsum[:, c:c + 1]), "sa")
                P.drain(VEC)
                for c in range(4, NC):
                    P.op(VEC, lambda e, c=c: e.reduce_sum(
                        nsum[:, c:c + 1], zT_c(c), axis=AX.X))
            return last, nsum_sq

        ffn_state = {}

        def emit_ffn_back(nsum_sq, stage_guard=None, rt_guards=(),
                          with_n2=False):
            vp_ones_sv = P.op(POOL, lambda e: e.memset(p_buf[64:72, :], 1.0),
                              "sq")
            sel_sv = None

            P.wait(VEC, "sa", nsum_sq)
            P.wait(VEC, "sq", gs_sq)
            P.drain(VEC)
            P.op(VEC, lambda e: e.tensor_tensor(tmp8[:], nsum[:], gsS[:],
                                                ALU.mult))
            P.drain(VEC)
            zbn_sv = P.op(VEC, lambda e: e.tensor_tensor(
                tmp8[:], tmp8[:], cC[:], ALU.add), "sv")

            with contextlib.ExitStack() as ps:
                p_r = ps.enter_context(
                    nc.psum_tensor(f"p_r_{P.cnt['sp']}", [8, 1], F32))
                p_rt = ps.enter_context(
                    nc.psum_tensor(f"p_rt_{P.cnt['sp']}", [1, 8], F32))
                p_b1 = ps.enter_context(
                    nc.psum_tensor(f"p_b1_{P.cnt['sp']}", [128, 1], F32))
                p_b2 = ps.enter_context(
                    nc.psum_tensor(f"p_b2_{P.cnt['sp']}", [64, 1], F32))
                p_b3 = ps.enter_context(
                    nc.psum_tensor(f"p_b3_{P.cnt['sp']}", [128, 1], F32))
                p_hb = ps.enter_context(
                    nc.psum_tensor(f"p_hb_{P.cnt['sp']}", [64, 1], F32))

                for g_sem, g_val in rt_guards:
                    P.wait(TEN, g_sem, g_val)
                P.wait(TEN, "sv", zbn_sv)
                for c in range(NC):
                    P.op(TEN, lambda e, c=c: e.matmul(
                        p_r[:], wr2[:, 8 * c:8 * (c + 1)], tmp8[:, c:c + 1],
                        start=(c == 0), stop=False))
                r_ms = P.op(TEN, lambda e: e.matmul(
                    p_r[:], br_row, ones11, start=False, stop=True), "sp")
                P.wait(VEC, "sp", r_ms)
                r8_sv = P.op(VEC, lambda e: e.tensor_copy(r8[:], p_r[:]), "sv")
                P.wait(TEN, "sv", r8_sv)
                rt_ms = P.op(TEN, lambda e: e.transpose(
                    p_rt[:], r8[:], identf8[:]), "sp")
                P.wait(VEC, "sp", rt_ms)
                P.op(VEC, lambda e: e.tensor_copy(rrow[:], p_rt[:]))
                P.drain(VEC)
                P.op(VEC, lambda e: e.reduce_max(rmax[:], rrow[:], axis=AX.X))
                P.drain(VEC)
                nm_sv = P.op(VEC, lambda e: e.tensor_scalar(
                    rnmax[:], rmax[:], -1.0, None, ALU.mult), "sv")
                P.wait(SCA, "sv", nm_sv)
                ex_sa = P.op(SCA, lambda e: e.activation(
                    rex[:], rrow[:], AF.Exp, bias=rnmax[:], scale=1.0), "sa")
                P.wait(VEC, "sa", ex_sa)
                P.op(VEC, lambda e: e.reduce_sum(rsum[:], rex[:], axis=AX.X))
                P.op(VEC, lambda e: e.tensor_scalar(
                    rtmp[:], rex[:], 1.0, None, ALU.subtract))
                P.drain(VEC)
                P.op(VEC, lambda e: e.reciprocal(rrec[:], rsum[:]))
                P.op(VEC, lambda e: e.tensor_tensor(rtmp2[:], rtmp[:], rtmp[:],
                                                    ALU.mult))
                P.drain(VEC)
                P.op(VEC, lambda e: e.tensor_scalar(
                    rtmp2[:], rtmp2[:], EPSA, None, ALU.add))
                P.drain(VEC)
                P.op(VEC, lambda e: e.reciprocal(rtmp[:], rtmp2[:]))
                P.drain(VEC)
                mask_sv = P.op(VEC, lambda e: e.tensor_scalar(
                    rtmp[:], rtmp[:], EPSA, None, ALU.mult), "sv")
                P.drain(VEC)
                P.op(VEC, lambda e: e.tensor_tensor(rtmp2[:], rtmp[:], iota8f,
                                                    ALU.mult))
                P.drain(VEC)
                idx_sv = P.op(VEC, lambda e: e.reduce_sum(
                    idxf[:], rtmp2[:], axis=AX.X), "sv")
                P.wait(TEN, "sv", mask_sv)
                mt_ms = P.op(TEN, lambda e: e.transpose(
                    p_r[:], rtmp[:], identf8[0:1, 0:1]), "sp")
                P.wait(VEC, "sp", mt_ms)
                P.op(VEC, lambda e: e.tensor_copy(tmp8b[0:8, 0:1], p_r[:]))
                P.drain(VEC)
                P.wait(VEC, "sq", vp_ones_sv)
                sel_sv = P.op(VEC, lambda e: e.tensor_scalar(
                    p_buf[64:72, :], p_buf[64:72, :], tmp8b[0:8, 0:1], None,
                    ALU.mult), "sv")
                P.wait(TEN, "sv", idx_sv)
                b1_ms = P.op(TEN, lambda e: e.matmul(
                    p_b1[:], c128row, idxf[:], start=True, stop=True), "sp")
                b2_ms = P.op(TEN, lambda e: e.matmul(
                    p_b2[:], c64row, idxf[:], start=True, stop=True), "sp")
                b3_ms = P.op(TEN, lambda e: e.matmul(
                    p_b3[:], c05row, rrec[:], start=True, stop=True), "sp")
                P.wait(VEC, "sp", b3_ms)
                P.op(VEC, lambda e: e.tensor_copy(idxc[:], p_b1[:]))
                P.op(VEC, lambda e: e.tensor_copy(uidxc[:], p_b2[:]))
                P.op(VEC, lambda e: e.tensor_copy(sgcol[:], p_b3[:]))
                P.op(VEC, lambda e: e.tensor_copy(idxi1[:], idxf[:]))
                P.drain(VEC)
                P.op(VEC, lambda e: e.tensor_copy(idxi[:], idxc[:]))
                P.op(VEC, lambda e: e.tensor_copy(uidxi[:], uidxc[:]))
                P.drain(VEC)
                P.op(VEC, lambda e: e.tensor_tensor(widx[:], idxi[:], iotai[:],
                                                    ALU.add))
                wix_sv = P.op(VEC, lambda e: e.tensor_tensor(
                    uidx[:], uidxi[:], iotai[0:64, :], ALU.add), "sv")

                P.wait(POOL, "sv", wix_sv)
                if stage_guard is not None:
                    P.wait(POOL, stage_guard[0], stage_guard[1])
                P.dma(POOL, lambda e: e.indirect_dma_start(
                    wd_f32, None, d_wd2,
                    bass.IndirectOffsetOnAxis(ap=widx[:], axis=0)), "sdg")
                P.dma(POOL, lambda e: e.indirect_dma_start(
                    wu_f32, None, d_wu2,
                    bass.IndirectOffsetOnAxis(ap=uidx[:], axis=0)), "sdg")
                g_ms = P.dma(POOL, lambda e: e.indirect_dma_start(
                    bd_f[:], None, d_bd2,
                    bass.IndirectOffsetOnAxis(ap=uidx[:], axis=0)), "sdg")

                P.wait(VEC, "sdg", g_ms)
                for c in range(NC):
                    P.op(VEC, lambda e, c=c: e.tensor_scalar(
                        wd_bf[:, 64 * c:64 * (c + 1)],
                        wd_f32[:, 64 * c:64 * (c + 1)],
                        ffng2[:, c:c + 1], None, ALU.mult))
                wprep_sv = P.op(VEC, lambda e: e.tensor_copy(
                    wu_ext[0:64, :], wu_f32), "sv")
                P.wait(TEN, "sdg", g_ms)
                for c in range(NC):
                    P.op(TEN, lambda e, c=c: e.matmul(
                        p_hb[:], wd_f32[:, 64 * c:64 * (c + 1)],
                        ffnb2[:, c:c + 1], start=(c == 0), stop=(c == NC - 1)),
                        "sp" if c == NC - 1 else None)
                hb_ms = P.cnt["sp"]
                P.wait(VEC, "sp", hb_ms)
                hb_sv = P.op(VEC, lambda e: e.tensor_tensor(
                    hbias[:], p_hb[:], bd_f[:], ALU.add), "sv")
                ffn_state["stage_last_sv"] = wprep_sv
                ffn_state["stage_last_sp"] = hb_ms

            with contextlib.ExitStack() as psh:
                p_h = [psh.enter_context(nc.psum_tensor(
                    f"p_h{i}_{P.cnt['sp']}", [64, 512], F32)) for i in range(2)]
                P.wait(TEN, "sv", wprep_sv)
                P.wait(TEN, "sv", hb_sv)
                relu_ms = [None, None]
                for tc in range(4):
                    if relu_ms[tc % 2] is not None:
                        P.wait(TEN, "sa", relu_ms[tc % 2])
                    ms = None
                    for c in range(NC):
                        ms = P.op(TEN, lambda e, c=c, tc=tc: e.matmul(
                            p_h[tc % 2][:],
                            wd_bf[:, 64 * c:64 * (c + 1)],
                            zT_c(c)[:, 512 * tc:512 * (tc + 1)],
                            start=(c == 0), stop=(c == NC - 1)),
                            "sp" if c == NC - 1 else None)
                    P.wait(SCA, "sp", ms)
                    P.wait(SCA, "sv", hb_sv)
                    relu_ms[tc % 2] = P.op(SCA, lambda e, tc=tc: e.activation(
                        p_buf[0:64, 512 * tc:512 * (tc + 1)], p_h[tc % 2][:],
                        AF.Relu, bias=hbias[:], scale=1.0), "sa")
            with contextlib.ExitStack() as ps:
                p_y = [ps.enter_context(nc.psum_tensor(
                    f"p_yf{i}_{P.cnt['sp']}", [128, 1024], F32)) for i in range(2)]
                if with_n2:
                    p_t2n = [ps.enter_context(nc.psum_tensor(
                        f"p_t2n{k}_{P.cnt['sp']}", [128, 1024], BF16))
                        for k in range(2)]
                    p_qk2 = [ps.enter_context(nc.psum_tensor(
                        f"p_qk2{k}_{P.cnt['sp']}", [128, 512], F32))
                        for k in range(2)]
                    zview_n = big_b.rearrange("p (c tt) -> p c tt", c=8)
                    vpb = vprime
                    P.wait(SCA, "sq", eps_sq)
                    P.wait(SCA, "sq", vp_ones_sq)
                    P.wait(VEC, "sq", sv_const)
                    wplan = []
                    for dram_w, n_ch, ccols, dstw in (
                            (d_wqk, 8, 1024, wqk_bf), (d_wv, 8, 512, wv_bf),
                            (d_wout, 4, 1024, wout_bf)):
                        dr = dram_w.rearrange("(c p) f -> c p f", p=128)
                        for cch in range(n_ch):
                            wplan.append((dr, cch, ccols, dstw))
                    wdma_ms = [None] * len(wplan)
                    wcast_ms = [None] * len(wplan)
                    n_dma = [0]
                    n_cast = [0]

                    def emit_wdma():
                        c = n_dma[0]
                        if c >= len(wplan):
                            return
                        dr, cch, ccols, dstw = wplan[c]
                        sl_ = stage[:, 1024 * (c % 2):1024 * (c % 2) + ccols]
                        if c < 2:
                            P.wait(SYN, "sv", wprep_sv)
                            P.wait(SYN, "sv", hb_sv)
                            P.wait(SYN, "sp", ffn_state["stage_last_sp"])
                        else:
                            P.wait(SYN, "sq", wcast_ms[c - 2])
                        wdma_ms[c] = P.dma(SYN, lambda e, sl_=sl_, dr=dr,
                                           cch=cch: e.dma_start(sl_, dr[cch]),
                                           "sdl" if c % 2 == 0 else "sdlb")
                        n_dma[0] += 1

                    def emit_wcast():
                        c = n_cast[0]
                        if c >= len(wplan) or wdma_ms[c] is None:
                            return
                        dr, cch, ccols, dstw = wplan[c]
                        sl_ = stage[:, 1024 * (c % 2):1024 * (c % 2) + ccols]
                        P.wait(POOL, "sdl" if c % 2 == 0 else "sdlb",
                               wdma_ms[c])
                        wcast_ms[c] = P.op(POOL, lambda e, sl_=sl_, dstw=dstw,
                                           cch=cch, ccols=ccols: e.tensor_copy(
                            dstw[:, ccols * cch:ccols * (cch + 1)], sl_), "sq")
                        n_cast[0] += 1

                    emit_wdma()
                    emit_wdma()
                    qk_batches = [(tc, fc) for tc in range(4)
                                  for fc in range(8)]
                    qb_next = [0]
                    qk_ev2 = [None, None]
                    qk_started = [False]

                    def emit_qk_batch():
                        b = qb_next[0]
                        if b >= len(qk_batches):
                            return False
                        tc, fc = qk_batches[b]
                        if n2ev_t[4 * tc + 3] is None:
                            return False
                        if n_cast[0] < 8:
                            return False
                        if not qk_started[0]:
                            P.wait(TEN, "sq", wcast_ms[7])
                            qk_started[0] = True
                        sl_ = b % 2
                        P.wait(TEN, "sv", qk_ev2[sl_])
                        if fc == 0:
                            P.wait(TEN, "sv", n2ev_t[4 * tc + 3])
                        ms = None
                        for c in range(NC):
                            ms = P.op(TEN, lambda e, c=c, fc=fc, tc=tc,
                                      sl_=sl_: e.matmul(
                                p_qk2[sl_][:],
                                wqk_bf[:, D * c + 128 * fc:
                                       D * c + 128 * (fc + 1)],
                                zT_c(c)[:, 512 * tc:512 * (tc + 1)],
                                start=(c == 0), stop=(c == NC - 1),
                                skip_group_check=True),
                                "sp" if c == NC - 1 else None)
                        P.wait(VEC, "sp", ms)
                        if fc % 2 == 0:
                            qk_ev2[sl_] = P.op(VEC, lambda e, fc=fc, tc=tc,
                                               sl_=sl_: e.tensor_scalar(
                                qk_of(fc)[:, 512 * tc:512 * (tc + 1)],
                                p_qk2[sl_][:], 0.125, bqk2s[:, fc:fc + 1],
                                ALU.mult, ALU.add), "sv")
                        else:
                            qk_ev2[sl_] = P.op(VEC, lambda e, fc=fc, tc=tc,
                                               sl_=sl_: e.tensor_scalar(
                                qk_of(fc)[:, 512 * tc:512 * (tc + 1)],
                                p_qk2[sl_][:], 1.0, bqk2[:, fc:fc + 1],
                                ALU.mult, ALU.add), "sv")
                        qb_next[0] += 1
                        return True
                P.wait(TEN, "sa", relu_ms[0])
                P.wait(TEN, "sa", relu_ms[1])
                P.wait(TEN, "sq", buc_sq)
                P.wait(TEN, "sv", sel_sv)
                up_ev = [None, None]
                x_done = []
                n2ap = [None] * NT
                n2r1 = [None] * NT
                n2ag = [None] * NT
                n2nb = [None] * NT
                n2tr = [None] * NT
                n2ev_t = [None] * NT
                n2_last = None
                n_it = NT + 6 if with_n2 else NT
                for i in range(n_it):
                    if i < NT:
                        t = i
                        if up_ev[t % 2] is not None:
                            P.wait(TEN, "sv", up_ev[t % 2])
                        ms = None
                        for h in range(2):
                            ms = P.op(TEN, lambda e, t=t, h=h: e.matmul(
                                p_y[t % 2][:, 512 * h:512 * (h + 1)],
                                p_buf[0:72, 128 * t:128 * (t + 1)],
                                wu_ext[0:72, 512 * h:512 * (h + 1)],
                                start=True, stop=True, skip_group_check=True),
                                "sp" if h == 1 else None)
                    if with_n2 and 0 <= i - 5 < NT:
                        td = i - 5
                        P.wait(TEN, "sa", n2ap[td])
                        P.wait(TEN, "sv", n2ev_t[td - 2] if td >= 2 else None)
                        for c in range(NC):
                            n2tr[td] = P.op(TEN, lambda e, td=td, c=c:
                                            e.transpose(
                                p_t2n[td % 2][:, 128 * c:128 * (c + 1)],
                                vpb[:, 1024 * (td % 2) + 128 * c:
                                    1024 * (td % 2) + 128 * (c + 1)],
                                identb[:]), "sp" if c == NC - 1 else None)
                    if i < NT:
                        P.wait(VEC, "sp", ms)
                        up_ev[t % 2] = P.op(
                            VEC, lambda e, t=t: e.scalar_tensor_tensor(
                                xt(t), p_y[t % 2][:], sgcol[:], xt(t),
                                ALU.mult, ALU.add), "sv")
                        x_done.append(("sv", up_ev[t % 2]))
                    if with_n2:
                        P.drain(VEC)
                        if 0 <= i - 1 < NT:
                            td = i - 1
                            if td >= 2:
                                P.wait(VEC, "sv", n2ag[td - 2])
                            P.op(VEC, lambda e, td=td: e.bn_stats(
                                bnbuf[:, 12 * (td % 2):12 * (td % 2) + 6],
                                xt(td)[:, 0:512]))
                            P.op(VEC, lambda e, td=td: e.bn_stats(
                                bnbuf[:, 12 * (td % 2) + 6:12 * (td % 2) + 12],
                                xt(td)[:, 512:1024]))
                        if 0 <= i - 2 < NT:
                            td = i - 2
                            n2ag[td] = P.op(VEC, lambda e, td=td: e.bn_aggr(
                                stat_mv[:, 2 * td:2 * td + 2],
                                bnbuf[:, 12 * (td % 2):12 * (td % 2) + 12]),
                                "sv")
                        if 0 <= i - 4 < NT:
                            td = i - 4
                            P.wait(VEC, "sa", n2r1[td])
                            n2nb[td] = P.op(VEC, lambda e, td=td:
                                            e.scalar_tensor_tensor(
                                sc_c[:, td:td + 1], stat_mv[:, 2 * td:
                                                            2 * td + 1],
                                -1.0, r2_all[:, td:td + 1],
                                ALU.mult, ALU.mult), "sv")
                        if 0 <= i - 5 < NT:
                            td = i - 5
                            P.wait(VEC, "sp", n2tr[td])
                            n2_last = P.op(VEC, lambda e, td=td: e.tensor_copy(
                                zview_n[:, :, 128 * td:128 * (td + 1)],
                                p_t2n[td % 2][:].rearrange(
                                    "p (c tt) -> p c tt", c=8)), "sv")
                            n2ev_t[td] = n2_last
                        if 0 <= i - 3 < NT:
                            td = i - 3
                            P.wait(SCA, "sv", n2ag[td])
                            P.op(SCA, lambda e, td=td: e.activation(
                                sc_a[:, td:td + 1],
                                stat_mv[:, 2 * td + 1:2 * td + 2], AF.Ln,
                                bias=epscol[:], scale=1.0))
                        P.drain(SCA)
                        if 0 <= i - 3 < NT:
                            td = i - 3
                            n2r1[td] = P.op(SCA, lambda e, td=td: e.activation(
                                r2_all[:, td:td + 1], sc_a[:, td:td + 1],
                                AF.Exp, scale=-0.5), "sa")
                        if 0 <= i - 4 < NT:
                            td = i - 4
                            P.wait(SCA, "sv", n2nb[td])
                            if td >= 2:
                                P.wait(SCA, "sp", n2tr[td - 2])
                            n2ap[td] = P.op(SCA, lambda e, td=td: e.activation(
                                vpb[:, 1024 * (td % 2):1024 * (td % 2 + 1)],
                                xt(td), AF.Identity,
                                bias=sc_c[:, td:td + 1],
                                scale=r2_all[:, td:td + 1]), "sa")
                        emit_wcast()
                        emit_wdma()
                        emit_wcast()
                        emit_wdma()
                        if QK_IN_LOOP:
                            for _ in range(3):
                                if not emit_qk_batch():
                                    break
                if with_n2:
                    while n_cast[0] < len(wplan):
                        emit_wdma()
                        emit_wcast()
                    while emit_qk_batch():
                        pass
                    ffn_state["sq_wqk"] = wcast_ms[7]
                    ffn_state["sq_wv"] = wcast_ms[15]
                    ffn_state["sq_wout"] = wcast_ms[19]
                    ffn_state["qkproj_sv"] = max(
                        v for v in qk_ev2 if v is not None)
                    P.wait(POOL, "sv", n2_last)
                    ffn_state["vp2_sq"] = P.op(POOL, lambda e: e.memset(
                        vprime.rearrange("p (t h c) -> p t h c", t=16, h=8)
                        [:, :, :, 64:65], 1.0), "sq")
            return x_done, n2_last

        def emit_ffn(x_ready, stage_guard=None, psum_guard=None,
                     with_n2=False):
            zt_sv, nsum_sq = ln_stats_apply("ffn", x_ready, want_nsum=True,
                                            psum_guard=psum_guard)
            return emit_ffn_back(nsum_sq, stage_guard=stage_guard,
                                 with_n2=with_n2)

        x_done, n2_ev = emit_ffn(x_tile_dma, stage_guard=("sq", buc_sq),
                                 with_n2=True)

        sq_wqk = ffn_state["sq_wqk"]
        sq_wv = ffn_state["sq_wv"]
        sq_wout = ffn_state["sq_wout"]
        qkproj_sv = ffn_state["qkproj_sv"]
        vp2_sq = ffn_state["vp2_sq"]
        P.wait(POOL, "sq", sq_wout)
        ffn1_done_sv = x_done[NT - 1][1]
        P.wait(TEN, "sv", qkproj_sv)
        g_row = stage[0:1, 0:1024]
        b_row = stage[0:1, 1024:2048]
        P.dma(POOL, lambda e: e.dma_start(g_row, d_gamma_row), "sdw")
        dw_gb = P.dma(POOL, lambda e: e.dma_start(b_row, d_beta_row), "sdw")
        with nc.psum_tensor("p_bcg", [128, 512], F32) as p_bcg:
            P.wait(TEN, "sdw", dw_gb)
            for src_row, dst in ((g_row, gamC), (b_row, betC)):
                for half in range(2):
                    ms = P.op(TEN, lambda e, h=half, sr=src_row, pb=p_bcg:
                              e.matmul(
                        pb[:], crs2row, sr[:, 512 * h:512 * (h + 1)],
                        start=True, stop=True), "sp")
                    P.wait(VEC, "sp", ms)
                    ms_v = P.op(VEC, lambda e, h=half, d=dst, pb=p_bcg:
                                e.tensor_copy(
                        d[:, 512 * h:512 * (h + 1)], pb[:]), "sv")
                    P.wait(TEN, "sv", ms_v)
        sv_gb = P.cnt["sv"]
        P.wait(POOL, "sv", sv_gb)
        P.dma(POOL, lambda e: e.dma_start(g_row[:, 0:512], d_bv_row), "sdw")
        dw_bvo = P.dma(POOL, lambda e: e.dma_start(b_row, d_bout_row), "sdw")
        P.wait(VEC, "sdw", dw_bvo)
        P.op(VEC, lambda e: e.tensor_copy(bv_b[:], g_row[:, 0:512]))
        sv_rows = P.op(VEC, lambda e: e.tensor_copy(bout_b[:], b_row), "sv")

        with contextlib.ExitStack() as ps:
            p_v = [ps.enter_context(nc.psum_tensor(
                f"p_v{i}", [128, 512], F32)) for i in range(2)]
            p_q2 = ps.enter_context(nc.psum_tensor("p_q2", [1, 2048], F32))
            p_nc_ = ps.enter_context(nc.psum_tensor("p_nc", [128, 4], F32))
            stb = stage.bitcast(BF16)
            bsl = [stb[:, 0:2048], stb[:, 2048:4096]]
            cs_list = [None] * 8
            mx_list = [None] * 8
            P.wait(TEN, "sq", sq_wv)
            P.wait(TEN, "sv", qkproj_sv)
            P.wait(TEN, "sv", sv_rows)
            P.wait(VEC, "sq", vp2_sq)
            P.wait(VEC, "sq", sq_wout)
            P.wait(VEC, "sv", sv_rows)
            v_ev = [None, None]

            def emit_mx(fc):
                P.wait(VEC, "sp", cs_list[fc])
                mx_list[fc] = P.op(VEC, lambda e, fc=fc: e.reduce_max(
                    mqk8[:, fc:fc + 1], p_q2[:], axis=AX.X), "sv")

            def emit_vtile(t):
                sl = t % 2
                if v_ev[sl] is not None:
                    P.wait(TEN, "sv", v_ev[sl])
                for c in range(NC):
                    P.op(TEN, lambda e, c=c, t=t, sl=sl: e.matmul(
                        p_v[sl][:],
                        big_b[:, S * c + 128 * t:S * c + 128 * (t + 1)],
                        wv_bf[:, 512 * c:512 * (c + 1)],
                        start=(c == 0), stop=False))
                ms = P.op(TEN, lambda e, sl=sl: e.matmul(
                    p_v[sl][:], onesb_r[:], bv_b[:], start=False, stop=True),
                    "sp")
                P.wait(VEC, "sp", ms)
                v_ev[sl] = P.op(VEC, lambda e, t=t, sl=sl: e.tensor_copy(
                    vprime[:, 520 * t:520 * (t + 1)].rearrange(
                        "p (h c) -> p h c", h=8)[:, :, 0:64],
                    p_v[sl][:].rearrange("p (h c) -> p h c", h=8)), "sv")

            for fc in range(8):
                if fc >= 1:
                    emit_mx(fc - 1)
                if fc == 0:
                    P.wait(VEC, "sv", qkproj_sv)
                if fc >= 2:
                    P.wait(VEC, "sp", cs_list[fc - 2])
                s2_sv = P.op(VEC, lambda e, fc=fc: e.tensor_tensor(
                    bsl[fc % 2], qk_of(fc), qk_of(fc), ALU.mult), "sv")
                emit_vtile(2 * fc)
                emit_vtile(2 * fc + 1)
                P.wait(TEN, "sv", s2_sv)
                if fc >= 1:
                    P.wait(TEN, "sv", mx_list[fc - 1])
                ms = None
                for sp_ in range(4):
                    ms = P.op(TEN, lambda e, sp_=sp_, fc=fc: e.matmul(
                        p_q2[:, 512 * sp_:512 * (sp_ + 1)], onesb_c[:],
                        bsl[fc % 2][:, 512 * sp_:512 * (sp_ + 1)],
                        start=True, stop=True, skip_group_check=True),
                        "sp" if sp_ == 3 else None)
                cs_list[fc] = ms
            emit_mx(7)
            vproj_sv = max(v for v in v_ev if v is not None)
            P.drain(VEC)
            mq_sv = P.op(VEC, lambda e: e.tensor_copy(rtmp[:], mqk8[:]), "sv")
            P.wait(SCA, "sv", mq_sv)
            ln8_sa = P.op(SCA, lambda e: e.activation(rtmp2[:], rtmp[:], AF.Ln),
                          "sa")
            P.wait(VEC, "sa", ln8_sa)
            pr_view = rtmp2.rearrange("p (k two) -> p k two", two=2)
            sum_sv = P.op(VEC, lambda e: e.tensor_tensor(
                c4[:], pr_view[:, :, 0:1].rearrange("p k two -> p (k two)"),
                pr_view[:, :, 1:2].rearrange("p k two -> p (k two)"),
                ALU.add), "sv")
            P.wait(SCA, "sv", sum_sv)
            c4_sa = P.op(SCA, lambda e: e.activation(
                rtmp[0:1, 0:4], c4[:], AF.Exp, scale=0.5), "sa")
            P.wait(TEN, "sa", c4_sa)
            ms = P.op(TEN, lambda e: e.matmul(
                p_nc_[:], cn1row, rtmp[0:1, 0:4], start=True,
                stop=True), "sp")
            P.wait(VEC, "sp", ms)
            negc_sv = P.op(VEC, lambda e: e.tensor_copy(
                negc_bc[:], p_nc_[:]), "sv")

        with contextlib.ExitStack() as ps:
            reg = [ps.enter_context(nc.psum_tensor(
                f"p_reg{i}", [128, 1024], F32)) for i in range(2)]
            p_oA = ps.enter_context(nc.psum_tensor("p_oA", [65, 512], F32))
            p_oB = ps.enter_context(nc.psum_tensor("p_oB", [65, 512], F32))
            p_bc0 = ps.enter_context(nc.psum_tensor("p_bcst0", [64, 512], F32))
            p_bc1 = ps.enter_context(nc.psum_tensor("p_bcst1", [64, 512], F32))

            P.wait(TEN, "sv", max(vproj_sv, negc_sv))
            P.wait(TEN, "sq", vp2_sq)
            P.wait(SCA, "sv", negc_sv)
            P.wait(SCA, "sq", sq_wout)
            P.wait(VEC, "sq", sq_wout)
            reg_free = [None, None]
            pslot_free = [None] * 5

            def psl_ap(k):
                if k < 3:
                    return p_buf[:, 1024 * k:1024 * (k + 1)]
                return wqk_bf[:, 1024 * (k - 2):1024 * (k - 1)]
            rowtmp = stage[0:1, 1536:2048]
            rowtmpB = wqk_bf.bitcast(F32)[0:1, 0:512]
            bcA_sb = stage[0:64, 1024:1536]
            bcB_sb = stage[64:128, 1024:1536]

            def vsl(j, h):
                return vprime[:, 520 * j + 65 * h:520 * j + 65 * h + 65]

            state = {"pend": None, "o_free": None, "bc_free": None}

            def onorm_act(pend):
                _, _, pv15 = pend
                P.wait(SCA, "sp", pv15)
                P.op(SCA, lambda e: e.activation(rowtmp, p_oA[64:65, :], AF.Ln))
                P.op(SCA, lambda e: e.activation(rowtmpB, p_oB[64:65, :],
                                                 AF.Ln))
                P.drain(SCA)
                P.op(SCA, lambda e: e.activation(recA, rowtmp, AF.Exp,
                                                 scale=-1.0))
                return P.op(SCA, lambda e: e.activation(
                    recB[:], rowtmpB, AF.Exp, scale=-1.0), "sa")

            def onorm_pe(pend, rec_ms):
                P.wait(TEN, "sa", rec_ms)
                P.wait(TEN, "sv", state["bc_free"])
                P.op(TEN, lambda e: e.matmul(
                    p_bc0[:], crows[0:1, 0:64], recA, start=True, stop=True))
                return P.op(TEN, lambda e: e.matmul(
                    p_bc1[:], crows[0:1, 0:64], recB[:], start=True,
                    stop=True), "sp")

            def onorm_vec(pend, bc_ms):
                pr, i, _ = pend
                P.wait(VEC, "sp", bc_ms)
                P.op(VEC, lambda e: e.tensor_copy(bcA_sb, p_bc0[:]))
                P.op(VEC, lambda e: e.tensor_copy(bcB_sb, p_bc1[:]))
                P.drain(VEC)
                P.op(VEC, lambda e, pr=pr, i=i: e.tensor_tensor(
                    ost_of(pr)[0:64, 512 * i:512 * (i + 1)], p_oA[0:64, :],
                    bcA_sb, ALU.mult))
                ev = P.op(VEC, lambda e, pr=pr, i=i: e.tensor_tensor(
                    ost_of(pr)[64:128, 512 * i:512 * (i + 1)],
                    p_oB[0:64, :], bcB_sb, ALU.mult), "sv")
                state["o_free"] = ev
                state["bc_free"] = ev

            blocks = [(i, pr) for i in range(4) for pr in range(4)]
            TOT = 16 * 16
            exp_hist = {}
            qk_hist = {}
            rec_pend = None
            for g in range(TOT + 3):
                if g >= TOT:
                    P.wait(TEN, "sa", exp_hist.get(g - 2))
                if g < TOT:
                    b, j = blocks[g // 16], g % 16
                    i, pr = b
                    sl = g % 2
                    P.wait(TEN, "sa", exp_hist.get(g - 2))
                    P.op(TEN, lambda e, pr=pr, i=i, j=j, sl=sl: e.matmul(
                        reg[sl][:, 0:512],
                        qk_of(2 * pr + 1)[0:64, 128 * j:128 * (j + 1)],
                        qk_of(2 * pr)[0:64, 512 * i:512 * (i + 1)],
                        start=True, stop=True, tile_position=(0, 0),
                        skip_group_check=True))
                    qk_hist[g] = P.op(TEN, lambda e, pr=pr, i=i, j=j, sl=sl:
                                      e.matmul(
                        reg[sl][:, 512:1024],
                        qk_of(2 * pr + 1)[64:128, 128 * j:128 * (j + 1)],
                        qk_of(2 * pr)[64:128, 512 * i:512 * (i + 1)],
                        start=True, stop=True, tile_position=(64, 0),
                        skip_group_check=True), "sp")
                if rec_pend is not None and g % 16 == 3:
                    pend, rec_ms = rec_pend
                    bc_ms = onorm_pe(pend, rec_ms)
                    onorm_vec(pend, bc_ms)
                    rec_pend = None
                pg = g - 3
                if pg >= 0:
                    pb, pj = blocks[pg // 16], pg % 16
                    pi, ppr = pb
                    hA, hB = 2 * ppr, 2 * ppr + 1
                    psl = pg % 5
                    if pj == 0:
                        P.wait(TEN, "sv", state["o_free"])
                    P.op(TEN, lambda e, pj=pj, hA=hA, psl=psl: e.matmul(
                        p_oA[:], vsl(pj, hA),
                        psl_ap(psl)[:, 0:512],
                        start=(pj == 0), stop=(pj == 15),
                        skip_group_check=True))
                    pv_ms = P.op(TEN, lambda e, pj=pj, hB=hB, psl=psl: e.matmul(
                        p_oB[:], vsl(pj, hB),
                        psl_ap(psl)[:, 512:1024],
                        start=(pj == 0), stop=(pj == 15),
                        skip_group_check=True), "sp")
                    pslot_free[psl] = pv_ms
                    if pj == 15:
                        pend = (ppr, pi, pv_ms)
                        rec_pend = (pend, onorm_act(pend))
                if g < TOT:
                    if pslot_free[g % 5] is not None and \
                            pslot_free[g % 5] > qk_hist[g]:
                        P.wait(SCA, "sp", pslot_free[g % 5])
                    else:
                        P.wait(SCA, "sp", qk_hist[g])
                        P.wait(SCA, "sp", pslot_free[g % 4])
                    exp_hist[g] = P.op(SCA, lambda e, pr=pr, sl=sl, es=(g % 5):
                                       e.activation(
                        psl_ap(es), reg[sl][:],
                        AF.Exp, bias=negc_bc[:, pr:pr + 1], scale=1.0), "sa")
            pend, rec_ms = rec_pend
            bc_ms = onorm_pe(pend, rec_ms)
            onorm_vec(pend, bc_ms)
            ostack_sv = P.cnt["sv"]
            p4_sp_last = P.cnt["sp"]

        P.op(VEC, lambda e: e.tensor_scalar(
            alpha_a[:], r2_all[:], RS2, 1.0, ALU.mult, ALU.add))
        al_sv = P.op(VEC, lambda e: e.scalar_tensor_tensor(
            m2_a[:], mv_mean2, -RS2, r2_all[:], ALU.mult, ALU.mult), "sv")
        P.drain(VEC)

        xbt = vprime.bitcast(F32)

        def xa_of(k):
            return xbt[:, 1024 * k:1024 * (k + 1)]

        zview = big_b.rearrange("p (c tt) -> p c tt", c=8)
        with contextlib.ExitStack() as ps:
            p_y = [ps.enter_context(nc.psum_tensor(
                f"p_yo{i}", [128, 1024], F32)) for i in range(3)]
            p_t2f = [ps.enter_context(nc.psum_tensor(
                f"p_t2f{k}", [128, 1024], BF16)) for k in range(2)]
            P.wait(TEN, "sv", ostack_sv)
            P.wait(TEN, "sq", sq_wout)
            P.wait(SCA, "sv", ostack_sv)
            P.wait(SCA, "sp", p4_sp_last)
            P.wait(SCA, "sq", eps_sq)
            w1a_hist = [None] * NT
            fin_hist = [None] * NT
            yr_hist = [None] * NT
            ap_hist = [None] * NT
            ag_hist = [None] * NT
            x2ag_hist = [None] * NT
            lnb_hist = [None] * NT
            sum2_hist = [None] * NT
            sexp_hist = [None] * NT
            nb_hist = [None] * NT
            tr_hist = [None] * NT
            xb_pend = [None] * NT
            w1b_pend = [None] * NT
            ev_hist = [None, None]
            last_ev = None
            nacc_q = []
            nsum_sq2 = None

            wq_f = wqk_bf.bitcast(F32)

            def u4(k):
                return wq_f[:, 1024 * k:1024 * (k + 1)]

            def ymv(k):
                return stat_mv[:, 4 + 2 * k:6 + 2 * k]

            def x2m(td):
                return x2mv[:, 2 * td:2 * td + 1]

            def x2v(td):
                return x2mv[:, 2 * td + 1:2 * td + 2]

            def scol(buf, td):
                return buf[:, td:td + 1]

            for i in range(NT + 10):
                if i < NT:
                    t = i
                    sl = t % 3
                    if t >= 3:
                        P.wait(TEN, "sa", w1a_hist[t - 3])
                        P.wait(TEN, "sv", ag_hist[t - 3])
                    for h in range(2):
                        for pr in range(4):
                            P.op(TEN, lambda e, t=t, h=h, pr=pr, sl=sl:
                                 e.matmul(
                                p_y[sl][:, 512 * h:512 * (h + 1)],
                                ost_of(pr)[:, 128 * t:128 * (t + 1)],
                                wout_bf[:, D * pr + 512 * h:
                                        D * pr + 512 * (h + 1)],
                                start=(pr == 0), stop=False,
                                skip_group_check=True))
                        y_ms = P.op(TEN, lambda e, h=h, sl=sl: e.matmul(
                            p_y[sl][:, 512 * h:512 * (h + 1)], onesb_r[:],
                            bout_b[:, 512 * h:512 * (h + 1)], start=False,
                            stop=True, skip_group_check=True), "sp")
                if 0 <= i - 10 < NT:
                    td = i - 10
                    P.wait(TEN, "sa", ap_hist[td])
                    P.wait(TEN, "sv", ev_hist[td % 2])
                    for c in range(NC):
                        tr_hist[td] = P.op(TEN, lambda e, td=td, c=c:
                                           e.transpose(
                            p_t2f[td % 2][:, 128 * c:128 * (c + 1)],
                            big_a[:, D * td + 128 * c:D * td + 128 * (c + 1)],
                            identb[:]), "sp" if c == NC - 1 else None)
                if i < NT:
                    P.wait(VEC, "sp", y_ms)
                    if i >= 1:
                        P.wait(VEC, "sv", ag_hist[i - 1])
                    P.op(VEC, lambda e, sl=sl: e.bn_stats(
                        bnbuf[:, 0:6], p_y[sl][:, 0:512]))
                    P.op(VEC, lambda e, sl=sl: e.bn_stats(
                        bnbuf[:, 6:12], p_y[sl][:, 512:1024]))
                P.drain(VEC)
                if i < NT:
                    ag_hist[i] = P.op(VEC, lambda e, i=i: e.bn_aggr(
                        ymv(i % 2), bnbuf[:, 0:12]), "sv")
                if 0 <= i - 2 < NT:
                    td = i - 2
                    P.wait(VEC, "sq", xb_pend[td])
                    P.op(VEC, lambda e, td=td: e.scalar_tensor_tensor(
                        xa_of(td % 4), gamC[:], ynb[:, td % 2:td % 2 + 1],
                        xa_of(td % 4), ALU.mult, ALU.add))
                if 0 <= i - 3 < NT:
                    td = i - 3
                    P.wait(VEC, "sq", w1b_pend[td])
                    fin_hist[td] = P.op(VEC, lambda e, td=td: e.tensor_tensor(
                        xt(td), xa_of(td % 4), u4(td % 4), ALU.add), "sv")
                if 0 <= i - 4 < NT:
                    td = i - 4
                    if td >= 2:
                        P.wait(VEC, "sv", x2ag_hist[td - 2])
                    P.op(VEC, lambda e, td=td: e.bn_stats(
                        bnbuf[:, 12 + 12 * (td % 2):18 + 12 * (td % 2)],
                        xt(td)[:, 0:512]))
                    P.op(VEC, lambda e, td=td: e.bn_stats(
                        bnbuf[:, 18 + 12 * (td % 2):24 + 12 * (td % 2)],
                        xt(td)[:, 512:1024]))
                if 0 <= i - 5 < NT:
                    td = i - 5
                    x2ag_hist[td] = P.op(VEC, lambda e, td=td: e.bn_aggr(
                        x2mv[:, 2 * td:2 * td + 2],
                        bnbuf[:, 12 + 12 * (td % 2):24 + 12 * (td % 2)]), "sv")
                if 0 <= i - 8 < NT:
                    td = i - 8
                    P.wait(VEC, "sa", lnb_hist[td])
                    sum2_hist[td] = P.op(VEC, lambda e, td=td: e.tensor_tensor(
                        scol(sc_a, td), scol(sc_a, td), scol(sc_c, td),
                        ALU.add), "sv")
                if 0 <= i - 9 < NT:
                    td = i - 9
                    P.wait(VEC, "sa", sexp_hist[td])
                    nb_hist[td] = P.op(VEC, lambda e, td=td:
                                       e.scalar_tensor_tensor(
                        scol(sc_c, td), x2m(td), -1.0, scol(sc_b, td),
                        ALU.mult, ALU.mult), "sv")
                if 0 <= i - 1 < NT:
                    td = i - 1
                    P.wait(VEC, "sa", yr_hist[td])
                    P.op(VEC, lambda e, td=td: e.scalar_tensor_tensor(
                        ynb[:, td % 2:td % 2 + 1], ymv(td % 2)[:, 0:1], -1.0,
                        yrstd[:, td % 2:td % 2 + 1], ALU.mult, ALU.mult))
                if 0 <= i - 10 < NT:
                    td = i - 10
                    P.wait(VEC, "sp", tr_hist[td])
                    last_ev = P.op(VEC, lambda e, td=td: e.tensor_copy(
                        zview[:, :, 128 * td:128 * (td + 1)],
                        p_t2f[td % 2][:].rearrange("p (c tt) -> p c tt", c=8)),
                        "sv")
                    ev_hist[td % 2] = last_ev
                if i < NT:
                    t = i
                    P.wait(SCA, "sv", ag_hist[t])
                    P.op(SCA, lambda e, t=t: e.activation(
                        ylnv[:, t % 2:t % 2 + 1], ymv(t % 2)[:, 1:2], AF.Ln,
                        bias=epscol[:], scale=1.0))
                if 0 <= i - 6 < NT:
                    td = i - 6
                    P.wait(SCA, "sv", x2ag_hist[td])
                    P.op(SCA, lambda e, td=td: e.activation(
                        scol(sc_a, td), x2v(td), AF.Ln,
                        bias=epscol[:], scale=1.0))
                P.drain(SCA)
                if i < NT:
                    t = i
                    yr_hist[t] = P.op(SCA, lambda e, t=t: e.activation(
                        yrstd[:, t % 2:t % 2 + 1], ylnv[:, t % 2:t % 2 + 1],
                        AF.Exp, scale=-0.5), "sa")
                if 0 <= i - 6 < NT:
                    td = i - 6
                    P.op(SCA, lambda e, td=td: e.activation(
                        scol(sc_b, td), scol(sc_a, td), AF.Exp, scale=-1.0))
                if 0 <= i - 7 < NT:
                    td = i - 7
                    lnb_hist[td] = P.op(SCA, lambda e, td=td: e.activation(
                        scol(sc_c, td), x2v(td), AF.Ln,
                        bias=epscol[:], scale=scol(sc_b, td)), "sa")
                P.drain(SCA)
                if i < NT:
                    t = i
                    if t == 0:
                        P.wait(SCA, "sv", al_sv)
                    if t >= 4:
                        P.wait(SCA, "sv", fin_hist[t - 4])
                    xa_sa = P.op(SCA, lambda e, t=t: e.activation(
                        xa_of(t % 4), xt(t), AF.Identity,
                        bias=m2_a[:, t:t + 1], scale=alpha_a[:, t:t + 1]),
                        "sa")
                    w1a_hist[t] = P.op(SCA, lambda e, t=t: e.activation(
                        u4(t % 4), p_y[t % 3][:], AF.Identity,
                        scale=yrstd[:, t % 2:t % 2 + 1]), "sa")
                if 0 <= i - 8 < NT:
                    td = i - 8
                    P.wait(SCA, "sv", sum2_hist[td])
                    sexp_hist[td] = P.op(SCA, lambda e, td=td: e.activation(
                        scol(sc_b, td), scol(sc_a, td), AF.Exp, scale=-0.5),
                        "sa")
                if 0 <= i - 9 < NT:
                    td = i - 9
                    P.wait(SCA, "sv", nb_hist[td])
                    ap_hist[td] = P.op(SCA, lambda e, td=td: e.activation(
                        nz_t(td), xt(td), AF.Identity,
                        bias=scol(sc_c, td), scale=scol(sc_b, td)), "sa")
                if i < NT:
                    t = i
                    P.wait(POOL, "sa", xa_sa)
                    xb_pend[t] = P.op(POOL, lambda e, t=t: e.tensor_tensor(
                        xa_of(t % 4), xa_of(t % 4), betC[:], ALU.add), "sq")
                    P.wait(POOL, "sa", w1a_hist[t])
                    w1b_pend[t] = P.op(POOL, lambda e, t=t: e.tensor_tensor(
                        u4(t % 4), u4(t % 4), gamC[:], ALU.mult), "sq")
            P.wait(SCA, "sv", last_ev)
            for c in range(4):
                nsum_sq2 = P.op(SCA, lambda e, c=c: e.activation(
                    big_a[:, 2048 * c:2048 * (c + 1)], zT_c(c), AF.Identity,
                    scale=1.0, accum_out=nsum[:, c:c + 1]), "sa")
            P.drain(VEC)
            for c in range(4, NC):
                P.op(VEC, lambda e, c=c: e.reduce_sum(
                    nsum[:, c:c + 1], zT_c(c), axis=AX.X))

        w1a_last = w1a_hist[NT - 1]
        x_done3, _ = emit_ffn_back(nsum_sq2, stage_guard=("sa", w1a_last),
                                   rt_guards=[("sa", w1a_last)])

        yr_ = d_y.rearrange("(t p) d -> t p d", p=128)
        for t in range(NT):
            P.wait(SYN, x_done3[t][0], x_done3[t][1])
            P.dma(SYN, lambda e, t=t: e.dma_start(yr_[t], xt(t)), "sout")
        P.wait(SYN, "sout", P.cnt["sout"])

        block = E(nc.Block())

        @block.sync
        def _(eng):
            for fn in P.ops[SYN]:
                fn(eng)

        @block.tensor
        def _(eng):
            for fn in P.ops[TEN]:
                fn(eng)

        @block.vector
        def _(eng):
            for fn in P.ops[VEC]:
                fn(eng)

        @block.scalar
        def _(eng):
            for fn in P.ops[SCA]:
                fn(eng)

        @block.gpsimd
        def _(eng):
            for fn in P.ops[POOL]:
                fn(eng)

    return nc


def _host_inputs(inp):
    W_qk = np.asarray(inp["L0_attn_W_qk"])
    b_qk = np.asarray(inp["L0_attn_b_qk"])
    perm = np.empty(1024, dtype=np.int64)
    for p in range(4):
        for hi in range(2):
            h = 2 * p + hi
            d = np.arange(64)
            perm[(2 * p) * 128 + hi * 64 + d] = h * 128 + 2 * d
            perm[(2 * p + 1) * 128 + hi * 64 + d] = h * 128 + 2 * d + 1
    wqk = np.ascontiguousarray(W_qk[:, perm])
    bqkp = np.ascontiguousarray(b_qk[perm])

    def chunk2(v):
        return np.ascontiguousarray(np.asarray(v).reshape(8, 128).T)

    com = {
        "wqk": wqk,
        "wv": np.ascontiguousarray(np.asarray(inp["L0_attn_W_v"])),
        "wout": np.ascontiguousarray(np.asarray(inp["L0_attn_W_out"])),
        "bv_row": np.asarray(inp["L0_attn_b_v"]).reshape(1, 512).copy(),
        "bout_row": np.asarray(inp["L0_attn_b_out"]).reshape(1, 1024).copy(),
        "gamma_row": np.asarray(inp["L0_attn_ln_gamma"]).reshape(1, 1024).copy(),
        "beta_row": np.asarray(inp["L0_attn_ln_beta"]).reshape(1, 1024).copy(),
        "spk": np.concatenate([
            chunk2(bqkp), chunk2(inp["L0_ffn_ln_g"]),
            chunk2(inp["L0_ffn_ln_b"]), chunk2(inp["L0_ffn_bn_mean"]),
            chunk2(inp["L0_ffn_bn_var"]), chunk2(inp["L0_ffn_bn_b"]),
            chunk2(inp["L0_ffn_bn_g"]),
            np.ascontiguousarray(np.asarray(inp["L0_ffn_Wr"]).reshape(8, 128, 8)
                                 .transpose(1, 0, 2).reshape(128, 64)),
        ], axis=1).astype(np.float32),
        "wd2": np.ascontiguousarray(
            np.asarray(inp["L0_ffn_W_down"]).reshape(8, 8, 128, 64)
            .transpose(0, 2, 1, 3).reshape(1024, 512)),
        "wu2": np.ascontiguousarray(
            np.asarray(inp["L0_ffn_W_up"]).reshape(512, 1024)),
        "bd2": np.asarray(inp["L0_ffn_b_down"]).reshape(512, 1).copy(),
        "bu2": np.ascontiguousarray(np.asarray(inp["L0_ffn_b_up"])),
        "identb": np.eye(128, dtype=np.float32).astype(ml_dtypes.bfloat16),
        "identf8": np.eye(8, dtype=np.float32),
        "crows": np.concatenate([
            np.full(128, 1.0, np.float32), np.full(128, -1.0, np.float32),
            np.full(128, 128.0, np.float32), np.full(64, 64.0, np.float32),
            np.full(128, 0.5, np.float32), np.full(128, RS2, np.float32),
            np.arange(8, dtype=np.float32), np.full(8, 1.0, np.float32),
            np.asarray(inp["L0_ffn_br"], dtype=np.float32).reshape(8),
            np.full(40, 1.0, np.float32),
        ]).reshape(1, 768),
        "onesb_r": np.ones((1, 128), np.float32).astype(ml_dtypes.bfloat16),
        "onesb_c": np.ones((128, 1), np.float32).astype(ml_dtypes.bfloat16),
        "iotai": np.arange(128, dtype=np.int32).reshape(128, 1),
    }
    for k, v in com.items():
        if v.dtype == np.float64:
            com[k] = v.astype(np.float32)
    return com


def kernel(**inputs):
    x = np.ascontiguousarray(np.asarray(inputs["L0_attn_x"], dtype=np.float32))
    com = _host_inputs(inputs)
    nc = build_program()
    in_maps = []
    for b in range(NCORE):
        m = dict(com)
        m["x"] = np.ascontiguousarray(x[b])
        in_maps.append(m)
    res = run_bass_kernel_spmd(nc, in_maps, core_ids=list(range(NCORE)))
    out = np.stack([np.asarray(res.results[b]["y"]) for b in range(NCORE)],
                   axis=0)
    return out.astype(np.float32)
